# version 04
# baseline (speedup 1.0000x reference)
import numpy as np
from contextlib import ExitStack

import concourse.bass as bass
import concourse.mybir as mybir
from concourse.bass_utils import run_bass_kernel_spmd

F32 = mybir.dt.float32
F16 = mybir.dt.float16
AF = mybir.ActivationFunctionType
OP = mybir.AluOpType

DM = 2048
T = 8192
TQ = 2048
DSL = 512
TC = 512
DC = 16
DCS = 4
NTCQ = 4
NTC = 16
N16 = 16
NJ = 64
NDO = 16
PR = 164

DEBUG = False


class Sem:
    def __init__(self, nc, stack, name):
        self.h = stack.enter_context(nc.semaphore(name))
        self.n = 0

    def inc(self, instr, k):
        instr.then_inc(self.h, k)
        self.n += k
        return self.n


def build_nc():
    nc = bass.Bass("TRN2", target_bir_lowering=False)

    x_rows = nc.dram_tensor("x_rows", [DM, TQ], F32, kind="ExternalInput").ap()
    x_scan = nc.dram_tensor("x_scan", [DSL, T], F32, kind="ExternalInput").ap()
    wx_t = nc.dram_tensor("wx_t", [DM, 160], F32, kind="ExternalInput").ap()
    dtpw_t = nc.dram_tensor("dtpw_t", [128, DSL], F32, kind="ExternalInput").ap()
    dt_b = nc.dram_tensor("dt_b", [128, 4], F32, kind="ExternalInput").ap()
    a_log = nc.dram_tensor("a_log", [128, 64], F32, kind="ExternalInput").ap()
    d_vec = nc.dram_tensor("d_vec", [128, 16], F32, kind="ExternalInput").ap()
    wout_lay = nc.dram_tensor("wout_lay", [NDO * DC * 128, 128], F32, kind="ExternalInput").ap()
    woutb = nc.dram_tensor("woutb", [128, 16], F32, kind="ExternalInput").ap()
    w0_lay = nc.dram_tensor("w0_lay", [NJ * DC * 128, 128], F32, kind="ExternalInput").ap()
    b0_in = nc.dram_tensor("b0_in", [128, 64], F32, kind="ExternalInput").ap()
    w1_lay = nc.dram_tensor("w1_lay", [NDO * NJ * 128, 128], F32, kind="ExternalInput").ap()
    b1_in = nc.dram_tensor("b1_in", [128, 16], F32, kind="ExternalInput").ap()

    out_d = nc.dram_tensor("out", [DM, TQ], F32, kind="ExternalOutput").ap()
    if DEBUG:
        dbg_yin = nc.dram_tensor("dbg_yin", [DSL, T], F16, kind="ExternalOutput").ap()
        dbg_yout = nc.dram_tensor("dbg_yout", [DM, T], F16, kind="ExternalOutput").ap()
        dbg_f = nc.dram_tensor("dbg_f", [DM, TQ], F16, kind="ExternalOutput").ap()
        dbg_pout = nc.dram_tensor("dbg_pout", [4 * PR, TQ], F16, kind="ExternalOutput").ap()
        dbg_xn = nc.dram_tensor("dbg_xn", [DM, TQ], F16, kind="ExternalOutput").ap()

    w1_sc = nc.dram_tensor("w1_sc", [NDO * NJ * 128, 128], F16)
    xn_sc = nc.dram_tensor("xn_sc", [DM, TQ], F16)
    g_sc = nc.dram_tensor("g_sc", [NJ * 128, TQ], F16)
    f_sc = nc.dram_tensor("f_sc", [DM, TQ], F16)
    p_in = nc.dram_tensor("p_in", [PR, TQ], F16)
    p_out = nc.dram_tensor("p_out", [4 * PR, TQ], F16)
    y_in = nc.dram_tensor("y_in", [DSL, T], F16)
    y_out = nc.dram_tensor("y_out", [DM, T], F16)

    with ExitStack() as stack:
        ec = stack.enter_context

        wx16 = ec(nc.sbuf_tensor("wx16", [128, DC * 160], F16))
        dtpw16 = ec(nc.sbuf_tensor("dtpw16", [128, DSL], F16))
        aneg = ec(nc.sbuf_tensor("aneg", [128, 64], F32))
        dtb = ec(nc.sbuf_tensor("dtb", [128, 4], F32))
        dvec = ec(nc.sbuf_tensor("dvec", [128, 16], F32))
        wob = ec(nc.sbuf_tensor("wob", [128, 16], F32))
        b0 = ec(nc.sbuf_tensor("b0", [128, 64], F32))
        b1 = ec(nc.sbuf_tensor("b1", [128, 16], F32))
        onesc = ec(nc.sbuf_tensor("onesc", [128, 1], F16))
        onesr = ec(nc.sbuf_tensor("onesr", [1, 128], F16))
        big = ec(nc.sbuf_tensor("big", [128, 32768], F16))
        scr = ec(nc.sbuf_tensor("scr", [128, 4096], F16))
        w016 = ec(nc.sbuf_tensor("w016", [128, 2 * 2048], F16))
        w1buf = ec(nc.sbuf_tensor("w1buf", [128, 2 * 4096], F16))
        fstage = ec(nc.sbuf_tensor("fstage", [128, 2 * TC], F16))
        gst = ec(nc.sbuf_tensor("gst", [128, 2 * TC], F16))
        mis1 = ec(nc.sbuf_tensor("mis1", [128, 1024], F32))
        mis2 = ec(nc.sbuf_tensor("mis2", [128, 1024], F16))
        dtl2 = ec(nc.sbuf_tensor("dtl2", [128, 2 * TC], F16))
        delta16 = ec(nc.sbuf_tensor("delta16", [128, DCS * TC], F16))
        dx16 = ec(nc.sbuf_tensor("dx16", [128, DCS * TC], F16))
        xns16 = ec(nc.sbuf_tensor("xns16", [128, TC], F16))
        xsc32 = ec(nc.sbuf_tensor("xsc32", [128, 2 * TC], F32))
        murs = ec(nc.sbuf_tensor("murs", [128, 2 * TC], F16))
        rsrs = ec(nc.sbuf_tensor("rsrs", [128, 2 * TC], F16))
        brep = ec(nc.sbuf_tensor("brep", [128, 2 * TC], F16))
        crep = ec(nc.sbuf_tensor("crep", [128, 2 * TC], F16))
        a32 = ec(nc.sbuf_tensor("a32", [128, 8 * TC], F32))
        u16 = ec(nc.sbuf_tensor("u16", [128, TC], F16))
        h16 = ec(nc.sbuf_tensor("h16", [128, DCS * TC], F16))
        tmp16 = ec(nc.sbuf_tensor("tmp16", [128, DCS * TC], F16))
        yacc = ec(nc.sbuf_tensor("yacc", [128, DCS * TC], F32))
        y16s = ec(nc.sbuf_tensor("y16s", [128, DCS * TC], F16))
        hstate = ec(nc.sbuf_tensor("hstate", [128, 64], F32))
        sexp32 = ec(nc.sbuf_tensor("sexp32", [128, TC], F32))
        st_a = ec(nc.sbuf_tensor("st_a", [1, TC], F32))
        st_b = ec(nc.sbuf_tensor("st_b", [1, TC], F32))
        st_c = ec(nc.sbuf_tensor("st_c", [1, TC], F32))
        st_d = ec(nc.sbuf_tensor("st_d", [1, TC], F32))
        mu16 = ec(nc.sbuf_tensor("mu16", [1, TC], F16))
        rstd16 = ec(nc.sbuf_tensor("rstd16", [1, TC], F16))
        dtlst = ec(nc.sbuf_tensor("dtlst", [128, TC], F16))
        bcst = ec(nc.sbuf_tensor("bcst", [32, TC], F16))
        xn0 = ec(nc.sbuf_tensor("xn0", [128, TC], F32))
        att32 = ec(nc.sbuf_tensor("att32", [128, 2 * TC], F32))
        outT = ec(nc.sbuf_tensor("outT", [128, 2 * TC], F32))
        xC32 = ec(nc.sbuf_tensor("xC32", [128, 2 * TC], F32))
        xnC16 = ec(nc.sbuf_tensor("xnC16", [128, 2 * TC], F16))
        fC16 = ec(nc.sbuf_tensor("fC16", [128, 2 * TC], F16))

        pb = [ec(nc.psum_tensor(f"pb{i}", [128, 512], F32)) for i in range(8)]

        s_t = Sem(nc, stack, "s_t")
        s_s = Sem(nc, stack, "s_s")
        s_v = Sem(nc, stack, "s_v")
        s_g = Sem(nc, stack, "s_g")
        s_cc = Sem(nc, stack, "s_cc")
        m_cst = Sem(nc, stack, "m_cst")
        m_xsc = [Sem(nc, stack, f"m_xsc{i}") for i in range(2)]
        m_mur = [Sem(nc, stack, f"m_mur{i}") for i in range(2)]
        m_bc = [Sem(nc, stack, f"m_bc{i}") for i in range(2)]
        m_dtl = [Sem(nc, stack, f"m_dtl{i}") for i in range(2)]
        m_w0 = Sem(nc, stack, "m_w0")
        m_w1b = [Sem(nc, stack, f"m_w1b{i}") for i in range(2)]
        m_egl = Sem(nc, stack, "m_egl")
        m_cyl = Sem(nc, stack, "m_cyl")
        m_cld = [Sem(nc, stack, f"m_cld{i}") for i in range(2)]
        st_pin = Sem(nc, stack, "st_pin")
        st_xn = Sem(nc, stack, "st_xn")
        st_gst = [Sem(nc, stack, f"st_gst{i}") for i in range(2)]
        st_y16 = Sem(nc, stack, "st_y16")
        st_f = [Sem(nc, stack, f"st_f{i}") for i in range(2)]
        st_out = [Sem(nc, stack, f"st_out{i}") for i in range(2)]
        q_g = Sem(nc, stack, "q_g")
        st_dbg = Sem(nc, stack, "st_dbg")

        blk = ec(nc.Block())
        ev = {}

        def wge(eng, sem, val):
            assert val <= sem.n, f"wait {val} > emitted {sem.n}"
            if val > 0:
                eng.wait_ge(sem.h, val)

        def evget(*keys):
            for k in keys:
                if k in ev:
                    return ev[k]
            return 0

        @blk.sync
        def _(e):
            for dst, src in [(dtb, dt_b), (dvec, d_vec), (wob, woutb), (b0, b0_in), (b1, b1_in)]:
                m_cst.inc(e.dma_start(out=dst[:, :], in_=src), 16)
            m_cst.inc(e.dma_start(out=mis1[:, 0:64], in_=a_log), 16)
            m_cst.inc(e.dma_start(out=mis1[:, 64:576], in_=dtpw_t), 16)
            ev["c_l1"] = m_cst.n

        @blk.scalar
        def _(e):
            wge(e, m_cst, ev["c_l1"])
            s_s.inc(e.activation(aneg[:, :], mis1[:, 0:64], AF.Exp), 1)
            ev["c_exp"] = s_s.n

        @blk.vector
        def _(e):
            wge(e, s_s, ev["c_exp"])
            s_v.inc(e.tensor_scalar(aneg[:, :], aneg[:, :], -1.0, None, OP.mult), 1)
            ev["c_aneg"] = s_v.n

        @blk.gpsimd
        def _(e):
            s_g.inc(e.memset(onesc[:, :], 1.0), 1)
            s_g.inc(e.memset(onesr[:, :], 1.0), 1)
            wge(e, m_cst, ev["c_l1"])
            s_g.inc(e.tensor_copy(dtpw16[:, :], mis1[:, 64:576]), 1)
            ev["c_dtpw"] = s_g.n

        for h in range(4):
            @blk.sync
            def _(e, h=h):
                if h == 0:
                    wge(e, s_s, ev["c_exp"])
                    wge(e, s_g, ev["c_dtpw"])
                else:
                    wge(e, s_g, ev[f"c_wx{h - 1}"])
                src = wx_t.rearrange("(c p) n -> p c n", p=128)
                m_cst.inc(e.dma_start(out=mis1[:, 0:640], in_=src[:, h * 4:(h + 1) * 4, :]), 16)
                ev[f"c_wxl{h}"] = m_cst.n

            @blk.gpsimd
            def _(e, h=h):
                wge(e, m_cst, ev[f"c_wxl{h}"])
                s_g.inc(e.tensor_copy(wx16[:, h * 640:(h + 1) * 640], mis1[:, 0:640]), 1)
                ev[f"c_wx{h}"] = s_g.n
        ev["consts_g"] = s_g.n

        for tc in range(NTCQ):
            for dc in range(DC):
                par = dc % 2

                @blk.sync
                def _(e, tc=tc, dc=dc, par=par):
                    if dc >= 2:
                        wge(e, s_v, ev[("A1v", tc, dc - 2)])
                        wge(e, s_s, ev[("A1s", tc, dc - 2)])
                    elif tc > 0:
                        wge(e, s_v, ev[("A2v_last", tc - 1)])
                    m_xsc[par].inc(e.dma_start(
                        out=xsc32[:, par * TC:(par + 1) * TC],
                        in_=x_rows[dc * 128:(dc + 1) * 128, tc * TC:(tc + 1) * TC]), 16)
                    ev[("A1l", tc, dc)] = m_xsc[par].n

                @blk.vector
                def _(e, tc=tc, dc=dc, par=par):
                    wge(e, m_xsc[par], ev[("A1l", tc, dc)])
                    if dc < 2 and tc > 0:
                        wge(e, s_t, ev[("Astatm", tc - 1)])
                    s_v.inc(e.tensor_copy(tmp16[:, par * TC:(par + 1) * TC],
                                          xsc32[:, par * TC:(par + 1) * TC]), 1)
                    ev[("A1v", tc, dc)] = s_v.n

                @blk.scalar
                def _(e, tc=tc, dc=dc, par=par):
                    wge(e, m_xsc[par], ev[("A1l", tc, dc)])
                    s_s.inc(e.activation(tmp16[:, (2 + par) * TC:(3 + par) * TC],
                                         xsc32[:, par * TC:(par + 1) * TC], AF.Square), 1)
                    ev[("A1s", tc, dc)] = s_s.n

                @blk.tensor
                def _(e, tc=tc, dc=dc, par=par):
                    wge(e, s_v, ev[("A1v", tc, dc)])
                    wge(e, s_s, ev[("A1s", tc, dc)])
                    if dc == 0:
                        wge(e, s_g, ev["consts_g"])
                        if tc > 0:
                            wge(e, s_v, ev[("Astat", tc - 1)])
                    e.matmul(pb[0][0:1, :], onesc[:, :], tmp16[:, par * TC:(par + 1) * TC],
                             start=(dc == 0), stop=(dc == DC - 1), skip_group_check=True)
                    i2 = e.matmul(pb[0][32:33, :], onesc[:, :], tmp16[:, (2 + par) * TC:(3 + par) * TC],
                                  start=(dc == 0), stop=(dc == DC - 1), skip_group_check=True)
                    if dc == DC - 1:
                        s_t.inc(i2, 1)
                        ev[("Astatm", tc)] = s_t.n

            ev[("A1v_last", tc)] = ev[("A1v", tc, DC - 1)]
            ev[("A1s_last", tc)] = ev[("A1s", tc, DC - 1)]

            @blk.vector
            def _(e, tc=tc):
                wge(e, s_t, ev[("Astatm", tc)])
                if tc > 0:
                    wge(e, st_pin, ev[("Apst", tc - 1)])
                e.tensor_scalar(st_a[:, :], pb[0][0:1, :], 1.0 / DM, None, OP.mult)
                e.tensor_scalar(st_b[:, :], pb[0][32:33, :], 1.0 / DM, None, OP.mult)
                e.tensor_tensor(st_c[:, :], st_a[:, :], st_a[:, :], OP.mult)
                e.tensor_tensor(st_d[:, :], st_b[:, :], st_c[:, :], OP.subtract)
                e.tensor_scalar(st_d[:, :], st_d[:, :], 1e-5, None, OP.add)
                e.reciprocal(st_c[:, :], st_d[:, :])
                s_v.inc(e.tensor_copy(mu16[:, :], st_a[:, :]), 1)
                ev[("Astat", tc)] = s_v.n

            @blk.scalar
            def _(e, tc=tc):
                wge(e, s_v, ev[("Astat", tc)])
                if tc > 0:
                    wge(e, st_pin, ev[("Apst", tc - 1)])
                s_s.inc(e.activation(rstd16[:, :], st_c[:, :], AF.Sqrt), 1)
                ev[("Arstd", tc)] = s_s.n

            @blk.tensor
            def _(e, tc=tc):
                wge(e, s_s, ev[("Arstd", tc)])
                if tc > 0:
                    wge(e, s_v, ev[("A2v_last", tc - 1)])
                e.matmul(pb[1][:, :], onesr[:, :], mu16[:, :], start=True, stop=True)
                i2 = e.matmul(pb[2][:, :], onesr[:, :], rstd16[:, :], start=True, stop=True)
                s_t.inc(i2, 1)
                ev[("Arep", tc)] = s_t.n

            for dc in range(DC):
                par = dc % 2

                @blk.sync
                def _(e, tc=tc, dc=dc, par=par):
                    if dc >= 2:
                        wge(e, s_v, ev[("A2v", tc, dc - 2)])
                    else:
                        wge(e, s_v, ev[("A1v_last", tc)])
                        wge(e, s_s, ev[("A1s_last", tc)])
                    m_xsc[par].inc(e.dma_start(
                        out=xsc32[:, par * TC:(par + 1) * TC],
                        in_=x_rows[dc * 128:(dc + 1) * 128, tc * TC:(tc + 1) * TC]), 16)
                    ev[("A2l", tc, dc)] = m_xsc[par].n

                @blk.vector
                def _(e, tc=tc, dc=dc, par=par):
                    wge(e, m_xsc[par], ev[("A2l", tc, dc)])
                    wge(e, s_t, ev[("Arep", tc)])
                    e.tensor_tensor(xn0[:, :], xsc32[:, par * TC:(par + 1) * TC], pb[1][:, :], OP.subtract)
                    s_v.inc(e.tensor_tensor(
                        big[:, dc * 2048 + tc * TC: dc * 2048 + (tc + 1) * TC],
                        xn0[:, :], pb[2][:, :], OP.mult), 1)
                    ev[("A2v", tc, dc)] = s_v.n

                @blk.scalar
                def _(e, tc=tc, dc=dc):
                    wge(e, s_v, ev[("A2v", tc, dc)])
                    st_xn.inc(e.dma_start(
                        out=xn_sc.ap()[dc * 128:(dc + 1) * 128, tc * TC:(tc + 1) * TC],
                        in_=big[:, dc * 2048 + tc * TC: dc * 2048 + (tc + 1) * TC]), 16)
                    ev[("Axn", tc, dc)] = st_xn.n

                @blk.tensor
                def _(e, tc=tc, dc=dc):
                    wge(e, s_v, ev[("A2v", tc, dc)])
                    if dc == 0 and tc > 0:
                        wge(e, s_s, ev[("Apev", tc - 1)])
                    xn_t = big[:, dc * 2048 + tc * TC: dc * 2048 + (tc + 1) * TC]
                    e.matmul(pb[3][:, :], wx16[:, dc * 160: dc * 160 + 128], xn_t,
                             start=(dc == 0), stop=(dc == DC - 1), skip_group_check=True)
                    i2 = e.matmul(pb[4][0:32, :], wx16[:, dc * 160 + 128: (dc + 1) * 160], xn_t,
                                  start=(dc == 0), stop=(dc == DC - 1), skip_group_check=True)
                    if dc == DC - 1:
                        s_t.inc(i2, 1)
                        ev[("Apm", tc)] = s_t.n
            ev[("A2v_last", tc)] = s_v.n

            @blk.scalar
            def _(e, tc=tc):
                wge(e, s_t, ev[("Apm", tc)])
                if tc > 0:
                    wge(e, st_pin, ev[("Apst", tc - 1)])
                e.activation(dtlst[:, :], pb[3][:, :], AF.Identity)
                s_s.inc(e.activation(bcst[:, :], pb[4][0:32, :], AF.Identity), 1)
                ev[("Apev", tc)] = s_s.n
                wge(e, s_s, ev[("Apev", tc)])
                st_pin.inc(e.dma_start(out=p_in.ap()[0:128, tc * TC:(tc + 1) * TC], in_=dtlst[:, :]), 16)
                st_pin.inc(e.dma_start(out=p_in.ap()[128:160, tc * TC:(tc + 1) * TC], in_=bcst[:, :]), 16)
                st_pin.inc(e.dma_start(out=p_in.ap()[160:161, tc * TC:(tc + 1) * TC], in_=mu16[:, :]), 16)
                st_pin.inc(e.dma_start(out=p_in.ap()[161:162, tc * TC:(tc + 1) * TC], in_=rstd16[:, :]), 16)
                ev[("Apst", tc)] = st_pin.n

            for k in range(32):
                ck = tc * 32 + k

                @blk.sync
                def _(e, ck=ck):
                    if ck == 0:
                        wge(e, s_g, ev["c_wx3"])
                    else:
                        wge(e, s_s, ev[("W1c", ck - 1)])
                    m_cst.inc(e.dma_start(
                        out=mis1[:, :],
                        in_=w1_lay[ck * 1024:(ck + 1) * 1024, :].rearrange("(a p) b -> p a b", p=128)), 16)
                    ev[("W1l", ck)] = m_cst.n

                @blk.scalar
                def _(e, ck=ck):
                    wge(e, m_cst, ev[("W1l", ck)])
                    if ck > 0:
                        wge(e, q_g, ev[("W1s", ck - 1)])
                    s_s.inc(e.copy(mis2[:, :], mis1[:, :]), 1)
                    ev[("W1c", ck)] = s_s.n

                @blk.gpsimd
                def _(e, ck=ck):
                    wge(e, s_s, ev[("W1c", ck)])
                    q_g.inc(e.dma_start(
                        out=w1_sc.ap()[ck * 1024:(ck + 1) * 1024, :].rearrange("(a p) b -> p a b", p=128),
                        in_=mis2[:, :]), 16)
                    ev[("W1s", ck)] = q_g.n

        @blk.gpsimd
        def _(e):
            wge(e, st_pin, ev[("Apst", NTCQ - 1)])
            s_cc.inc(e.collective_compute(
                "AllGather", OP.bypass,
                replica_groups=[[0, 1, 2, 3], [4, 5, 6, 7]],
                ins=[p_in.ap().opt()],
                outs=[p_out.ap().opt()],
            ), 1)
            ev["ag1"] = s_cc.n

        d1banks = [0, 1, 2, 3, 5, 6]

        for tc in range(NTC):
            rnk = tc // 4
            tl = (tc % 4) * TC
            tpar = tc % 2

            @blk.sync
            def _(e, tc=tc, rnk=rnk, tl=tl, tpar=tpar):
                wge(e, s_cc, ev["ag1"])
                if tc >= 2:
                    wge(e, s_v, ev[("sdxv", tc - 2, DCS - 1)])
                m_mur[tpar].inc(e.dma_start(
                    out=murs[:, tpar * TC:(tpar + 1) * TC],
                    in_=p_out.ap()[rnk * PR + 160: rnk * PR + 161, tl:tl + TC].broadcast_to([128, TC])), 16)
                m_mur[tpar].inc(e.dma_start(
                    out=rsrs[:, tpar * TC:(tpar + 1) * TC],
                    in_=p_out.ap()[rnk * PR + 161: rnk * PR + 162, tl:tl + TC].broadcast_to([128, TC])), 16)
                ev[("mul", tc)] = m_mur[tpar].n
                if tc >= 2:
                    wge(e, s_t, ev[("sdm", tc - 2, DCS - 1)])
                m_dtl[tpar].inc(e.dma_start(
                    out=dtl2[:, tpar * TC:(tpar + 1) * TC],
                    in_=p_out.ap()[rnk * PR: rnk * PR + 128, tl:tl + TC]), 16)
                ev[("dtll", tc)] = m_dtl[tpar].n
                for dcs in range(2):
                    if tc == 0:
                        wge(e, s_v, ev[("A2v_last", NTCQ - 1)])
                    else:
                        wge(e, s_v, ev[("sdxv", tc - 1, dcs + 2)])
                    m_xsc[dcs].inc(e.dma_start(
                        out=xsc32[:, dcs * TC:(dcs + 1) * TC],
                        in_=x_scan[dcs * 128:(dcs + 1) * 128, tc * TC:(tc + 1) * TC]), 16)
                    ev[("sxl", tc, dcs)] = m_xsc[dcs].n
                for n in (0, 1):
                    if tc > 0:
                        wge(e, s_v, ev[("snv", tc - 1, N16 - 2 + n)])
                    m_bc[n].inc(e.dma_start(
                        out=brep[:, n * TC:(n + 1) * TC],
                        in_=p_out.ap()[rnk * PR + 128 + n: rnk * PR + 129 + n, tl:tl + TC].broadcast_to([128, TC])), 16)
                    m_bc[n].inc(e.dma_start(
                        out=crep[:, n * TC:(n + 1) * TC],
                        in_=p_out.ap()[rnk * PR + 144 + n: rnk * PR + 145 + n, tl:tl + TC].broadcast_to([128, TC])), 16)
                    ev[("sbl", tc, n)] = m_bc[n].n

            for dcs in range(DCS):
                @blk.tensor
                def _(e, tc=tc, dcs=dcs, tpar=tpar):
                    bank = 4 if dcs % 2 == 0 else 7
                    wge(e, m_dtl[tpar], ev[("dtll", tc)])
                    wge(e, s_s, evget(("sdev", tc, dcs - 2), ("sdev", tc - 1, dcs + 2)))
                    if (tc, dcs) == (0, 0):
                        wge(e, s_g, ev["c_dtpw"])
                        wge(e, s_s, ev[("Apev", NTCQ - 1)])
                    i = e.matmul(pb[bank][:, :], dtpw16[:, dcs * 128:(dcs + 1) * 128],
                                 dtl2[:, tpar * TC:(tpar + 1) * TC], start=True, stop=True)
                    s_t.inc(i, 1)
                    ev[("sdm", tc, dcs)] = s_t.n

                @blk.scalar
                def _(e, tc=tc, dcs=dcs):
                    bank = 4 if dcs % 2 == 0 else 7
                    wge(e, s_t, ev[("sdm", tc, dcs)])
                    if tc > 0 and dcs == 0:
                        wge(e, s_v, ev[("sdxv", tc - 1, DCS - 1)])
                    s_s.inc(e.activation(sexp32[:, :], pb[bank][:, :], AF.Exp, bias=dtb[:, dcs:dcs + 1]), 1)
                    wge(e, s_s, s_s.n)
                    s_s.inc(e.activation(delta16[:, dcs * TC:(dcs + 1) * TC], sexp32[:, :],
                                         AF.Ln, bias=1.0), 1)
                    ev[("sdev", tc, dcs)] = s_s.n

            for dcs in range(DCS):
                @blk.vector
                def _(e, tc=tc, dcs=dcs, tpar=tpar):
                    pxr = dcs % 2
                    wge(e, m_xsc[pxr], ev[("sxl", tc, dcs)])
                    wge(e, m_mur[tpar], ev[("mul", tc)])
                    wge(e, s_s, ev[("sdev", tc, dcs)])
                    e.tensor_tensor(xn0[:, :], xsc32[:, pxr * TC:(pxr + 1) * TC],
                                    murs[:, tpar * TC:(tpar + 1) * TC], OP.subtract)
                    e.tensor_tensor(xns16[:, :], xn0[:, :],
                                    rsrs[:, tpar * TC:(tpar + 1) * TC], OP.mult)
                    s_v.inc(e.tensor_tensor(dx16[:, dcs * TC:(dcs + 1) * TC],
                                            delta16[:, dcs * TC:(dcs + 1) * TC],
                                            xns16[:, :], OP.mult), 1)
                    ev[("sdxv", tc, dcs)] = s_v.n

                if dcs < 2:
                    @blk.sync
                    def _(e, tc=tc, dcs=dcs):
                        m = dcs + 2
                        wge(e, s_v, ev[("sdxv", tc, dcs)])
                        m_xsc[m % 2].inc(e.dma_start(
                            out=xsc32[:, (m % 2) * TC:(m % 2 + 1) * TC],
                            in_=x_scan[m * 128:(m + 1) * 128, tc * TC:(tc + 1) * TC]), 16)
                        ev[("sxl", tc, m)] = m_xsc[m % 2].n

            for n in (0, 1):
                for dcs in range(DCS):
                    @blk.scalar
                    def _(e, tc=tc, n=n, dcs=dcs):
                        slot = (n % 2) * 4 + dcs
                        if tc > 0:
                            wge(e, s_v, ev[("sscan", tc - 1, N16 - 2 + n, dcs)])
                        s_s.inc(e.activation(a32[:, slot * TC:(slot + 1) * TC],
                                             delta16[:, dcs * TC:(dcs + 1) * TC],
                                             AF.Exp, scale=aneg[:, dcs * 16 + n:dcs * 16 + n + 1]), 1)
                        ev[("sexp", tc, n, dcs)] = s_s.n

            for n in range(N16):
                np_ = n % 2

                @blk.vector
                def _(e, tc=tc, n=n, np_=np_):
                    wge(e, m_bc[np_], ev[("sbl", tc, n)])
                    wge(e, s_g, evget(("shc", tc, n - 1), ("shc", tc - 1, N16 - 1)))
                    if tc > 0:
                        wge(e, s_g, ev[("shc", tc - 1, n)])
                    for dcs in range(DCS):
                        slot = np_ * 4 + dcs
                        wge(e, s_s, ev[("sexp", tc, n, dcs)])
                        if n == 0:
                            wge(e, s_g, evget(("sya", tc - 1, N16 - 1, dcs)))
                        else:
                            wge(e, s_g, evget(("sya", tc, n - 1, dcs),
                                              ("sya", tc - 1, N16 - 1, dcs)))
                        e.tensor_tensor(u16[:, :], dx16[:, dcs * TC:(dcs + 1) * TC],
                                        brep[:, np_ * TC:(np_ + 1) * TC], OP.mult)
                        init = 0.0 if tc == 0 else hstate[:, dcs * 16 + n:dcs * 16 + n + 1]
                        i = e.tensor_tensor_scan(h16[:, dcs * TC:(dcs + 1) * TC],
                                                 a32[:, slot * TC:(slot + 1) * TC],
                                                 u16[:, :], init, OP.mult, OP.add)
                        s_v.inc(i, 1)
                        ev[("sscan", tc, n, dcs)] = s_v.n
                        wge(e, s_v, ev[("sscan", tc, n, dcs)])
                        if n == 0:
                            i = e.tensor_tensor(yacc[:, dcs * TC:(dcs + 1) * TC],
                                                h16[:, dcs * TC:(dcs + 1) * TC],
                                                crep[:, np_ * TC:(np_ + 1) * TC], OP.mult)
                        else:
                            i = e.tensor_tensor(tmp16[:, dcs * TC:(dcs + 1) * TC],
                                                h16[:, dcs * TC:(dcs + 1) * TC],
                                                crep[:, np_ * TC:(np_ + 1) * TC], OP.mult)
                        s_v.inc(i, 1)
                        ev[("sym", tc, n, dcs)] = s_v.n
                    ev[("snv", tc, n)] = s_v.n

                @blk.gpsimd
                def _(e, tc=tc, n=n):
                    wge(e, s_v, ev[("sscan", tc, n, DCS - 1)])
                    s_g.inc(e.tensor_copy(hstate[:, n:64:16], h16[:, TC - 1: DCS * TC: TC]), 1)
                    ev[("shc", tc, n)] = s_g.n
                    if n > 0:
                        for dcs in range(DCS):
                            wge(e, s_v, ev[("sym", tc, n, dcs)])
                            s_g.inc(e.tensor_tensor(yacc[:, dcs * TC:(dcs + 1) * TC],
                                                    yacc[:, dcs * TC:(dcs + 1) * TC],
                                                    tmp16[:, dcs * TC:(dcs + 1) * TC], OP.add), 1)
                            ev[("sya", tc, n, dcs)] = s_g.n

                if n + 2 < N16:
                    @blk.sync
                    def _(e, tc=tc, n=n, rnk=rnk, tl=tl):
                        m = n + 2
                        wge(e, s_v, ev[("snv", tc, n)])
                        m_bc[m % 2].inc(e.dma_start(
                            out=brep[:, (m % 2) * TC:(m % 2 + 1) * TC],
                            in_=p_out.ap()[rnk * PR + 128 + m: rnk * PR + 129 + m, tl:tl + TC].broadcast_to([128, TC])), 16)
                        m_bc[m % 2].inc(e.dma_start(
                            out=crep[:, (m % 2) * TC:(m % 2 + 1) * TC],
                            in_=p_out.ap()[rnk * PR + 144 + m: rnk * PR + 145 + m, tl:tl + TC].broadcast_to([128, TC])), 16)
                        ev[("sbl", tc, m)] = m_bc[m % 2].n

                    for dcs in range(DCS):
                        @blk.scalar
                        def _(e, tc=tc, n=n, dcs=dcs):
                            m = n + 2
                            wge(e, s_v, ev[("sscan", tc, n, dcs)])
                            slot = (m % 2) * 4 + dcs
                            s_s.inc(e.activation(a32[:, slot * TC:(slot + 1) * TC],
                                                 delta16[:, dcs * TC:(dcs + 1) * TC],
                                                 AF.Exp, scale=aneg[:, dcs * 16 + m:dcs * 16 + m + 1]), 1)
                            ev[("sexp", tc, m, dcs)] = s_s.n

            @blk.vector
            def _(e, tc=tc):
                if tc >= 1:
                    wge(e, st_y16, ev[("syst", tc - 1)])
                for dcs in range(DCS):
                    wge(e, s_g, ev[("sya", tc, N16 - 1, dcs)])
                    s_v.inc(e.tensor_copy(y16s[:, dcs * TC:(dcs + 1) * TC],
                                          yacc[:, dcs * TC:(dcs + 1) * TC]), 1)
                ev[("syc", tc)] = s_v.n

            @blk.scalar
            def _(e, tc=tc):
                wge(e, s_v, ev[("syc", tc)])
                for dcs in range(DCS):
                    st_y16.inc(e.dma_start(
                        out=y_in.ap()[dcs * 128:(dcs + 1) * 128, tc * TC:(tc + 1) * TC],
                        in_=y16s[:, dcs * TC:(dcs + 1) * TC]), 16)
                ev[("syst", tc)] = st_y16.n

            for k in range(16):
                u = tc * 16 + k
                j, tcq = u // 4, u % 4
                bank = d1banks[u % 6]

                if tcq == 0:
                    @blk.sync
                    def _(e, j=j):
                        if j >= 1:
                            wge(e, s_g, ev[("w0c", j - 1)])
                        src = w0_lay[j * DC * 128:(j + 1) * DC * 128, :].rearrange("(a p) b -> p a b", p=128)
                        m_w0.inc(e.dma_start(out=scr[:, :].bitcast(F32), in_=src), 16)
                        ev[("w0l", j)] = m_w0.n

                    @blk.gpsimd
                    def _(e, j=j):
                        wge(e, m_w0, ev[("w0l", j)])
                        if j >= 2:
                            wge(e, s_t, ev[("d1m", (j - 2) * 4 + 3)])
                        s_g.inc(e.tensor_copy(w016[:, (j % 2) * 2048:(j % 2 + 1) * 2048],
                                              scr[:, :].bitcast(F32)), 1)
                        ev[("w0c", j)] = s_g.n

                @blk.tensor
                def _(e, u=u, j=j, tcq=tcq, bank=bank):
                    wge(e, s_g, ev[("w0c", j)])
                    wge(e, s_s, evget(("gel", u - 6)))
                    if u == 0:
                        wge(e, s_v, ev[("A2v_last", NTCQ - 1)])
                        wge(e, s_s, ev[("Apev", NTCQ - 1)])
                    for dc in range(DC):
                        i = e.matmul(pb[bank][:, :],
                                     w016[:, (j % 2) * 2048 + dc * 128:(j % 2) * 2048 + (dc + 1) * 128],
                                     big[:, dc * 2048 + tcq * TC: dc * 2048 + (tcq + 1) * TC],
                                     start=(dc == 0), stop=(dc == DC - 1))
                    s_t.inc(i, 1)
                    ev[("d1m", u)] = s_t.n

            for k in range(16):
                u = tc * 16 + k
                j, tcq = u // 4, u % 4
                bank = d1banks[u % 6]

                @blk.scalar
                def _(e, u=u, j=j, tcq=tcq, bank=bank):
                    par = u % 2
                    wge(e, s_t, ev[("d1m", u)])
                    if u >= 2:
                        wge(e, st_gst[par], ev[("gst", u - 2)])
                    s_s.inc(e.activation(gst[:, par * TC:(par + 1) * TC], pb[bank][:, :],
                                         AF.Gelu, bias=b0[:, j:j + 1]), 1)
                    ev[("gel", u)] = s_s.n
                    wge(e, s_s, ev[("gel", u)])
                    st_gst[par].inc(e.dma_start(
                        out=g_sc.ap()[j * 128:(j + 1) * 128, tcq * TC:(tcq + 1) * TC],
                        in_=gst[:, par * TC:(par + 1) * TC]), 16)
                    ev[("gst", u)] = st_gst[par].n

        @blk.gpsimd
        def _(e):
            wge(e, st_y16, ev[("syst", NTC - 1)])
            s_cc.inc(e.collective_compute(
                "AllGather", OP.bypass,
                replica_groups=[[0, 1, 2, 3], [4, 5, 6, 7]],
                ins=[y_in.ap().opt()],
                outs=[y_out.ap().opt()],
            ), 1)
            ev["ag2"] = s_cc.n

        for tc in range(NTCQ):
            @blk.sync
            def _(e, tc=tc):
                wge(e, s_t, ev[("d1m", NTC * 16 - 1)])
                if tc > 0:
                    wge(e, s_t, ev[("e_m", tc - 1, NDO - 1)])
                wge(e, st_gst[0], ev[("gst", NTC * 16 - 2)])
                wge(e, st_gst[1], ev[("gst", NTC * 16 - 1)])
                wge(e, st_xn, ev[("Axn", NTCQ - 1, DC - 1)])
                for b8 in range(8):
                    src = g_sc.ap()[b8 * 8 * 128:(b8 + 1) * 8 * 128, tc * TC:(tc + 1) * TC]
                    src = src.rearrange("(a p) b -> p a b", p=128)
                    m_egl.inc(e.dma_start(out=big[:, b8 * 8 * TC:(b8 + 1) * 8 * TC], in_=src), 16)
                ev[("e_gl", tc)] = m_egl.n

            for do in range(NDO):
                for hh in range(2):
                    @blk.sync
                    def _(e, tc=tc, do=do, hh=hh):
                        key = ("e_mid" if hh == 0 else "e_m")
                        wge(e, s_t, evget((key, tc, do - 1), (key, tc - 1, NDO - 1)))
                        wge(e, q_g, ev[("W1s", 127)])
                        base = do * NJ * 128
                        src = w1_sc.ap()[base + hh * 32 * 128: base + (hh + 1) * 32 * 128, :]
                        src = src.rearrange("(a p) b -> p a b", p=128)
                        m_w1b[hh].inc(e.dma_start(out=w1buf[:, hh * 4096:(hh + 1) * 4096], in_=src), 16)
                        ev[("e_wl", tc, do, hh)] = m_w1b[hh].n

                @blk.tensor
                def _(e, tc=tc, do=do):
                    bank = 2 if do % 2 == 0 else 3
                    wge(e, m_egl, ev[("e_gl", tc)])
                    wge(e, s_s, evget(("e_ev", tc, do - 2), ("e_ev", tc - 1, NDO - 2 + (do % 2))))
                    wge(e, m_w1b[0], ev[("e_wl", tc, do, 0)])
                    for j in range(NJ):
                        hh = j // 32
                        if j == 32:
                            wge(e, m_w1b[1], ev[("e_wl", tc, do, 1)])
                        i = e.matmul(pb[bank][:, :],
                                     w1buf[:, hh * 4096 + (j - hh * 32) * 128: hh * 4096 + (j - hh * 32 + 1) * 128],
                                     big[:, j * TC:(j + 1) * TC],
                                     start=(j == 0), stop=(j == NJ - 1))
                        if j == 31:
                            s_t.inc(i, 1)
                            ev[("e_mid", tc, do)] = s_t.n
                    s_t.inc(i, 1)
                    ev[("e_m", tc, do)] = s_t.n

                @blk.scalar
                def _(e, tc=tc, do=do):
                    bank = 2 if do % 2 == 0 else 3
                    par = do % 2
                    wge(e, s_t, ev[("e_m", tc, do)])
                    wge(e, st_f[par], evget(("e_fs", tc, do - 2), ("e_fs", tc - 1, NDO - 2 + (do % 2))))
                    s_s.inc(e.activation(fstage[:, par * TC:(par + 1) * TC], pb[bank][:, :],
                                         AF.Identity, bias=b1[:, do:do + 1]), 1)
                    ev[("e_ev", tc, do)] = s_s.n
                    wge(e, s_s, ev[("e_ev", tc, do)])
                    st_f[par].inc(e.dma_start(
                        out=f_sc.ap()[do * 128:(do + 1) * 128, tc * TC:(tc + 1) * TC],
                        in_=fstage[:, par * TC:(par + 1) * TC]), 16)
                    ev[("e_fs", tc, do)] = st_f[par].n

        @blk.sync
        def _(e):
            wge(e, s_cc, ev["ag2"])
            wge(e, s_t, ev[("e_m", NTCQ - 1, NDO - 1)])
            pid = e.partition_id()
            qoff = (pid % 4) * TQ
            for dc in range(DC):
                m_cyl.inc(e.dma_start(
                    out=big[:, dc * 2048:(dc + 1) * 2048],
                    in_=y_out.ap()[dc * 128:(dc + 1) * 128, bass.ds(qoff, TQ)]), 16)
            ev["c_yl"] = m_cyl.n

        def prevC(name, do, tc):
            if tc >= 2:
                return ev.get((name, do, tc - 2), 0)
            if do >= 1:
                return ev.get((name, do - 1, tc + 2), 0)
            return 0

        for do in range(NDO):
            @blk.sync
            def _(e, do=do):
                if do >= 1:
                    wge(e, s_g, ev[("c_wc", do - 1)])
                else:
                    wge(e, s_g, ev[("w0c", NJ - 1)])
                src = wout_lay[do * DC * 128:(do + 1) * DC * 128, :].rearrange("(a p) b -> p a b", p=128)
                m_w0.inc(e.dma_start(out=scr[:, :].bitcast(F32), in_=src), 16)
                ev[("c_wl", do)] = m_w0.n

            @blk.gpsimd
            def _(e, do=do):
                wge(e, m_w0, ev[("c_wl", do)])
                if do < 2:
                    wge(e, s_t, ev[("d1m", NTC * 16 - 1)])
                else:
                    wge(e, s_t, ev[("c_m", do - 2, NTCQ - 1)])
                s_g.inc(e.tensor_copy(w016[:, (do % 2) * 2048:(do % 2 + 1) * 2048],
                                      scr[:, :].bitcast(F32)), 1)
                ev[("c_wc", do)] = s_g.n

            for tc in range(NTCQ):
                @blk.tensor
                def _(e, do=do, tc=tc):
                    wge(e, s_g, ev[("c_wc", do)])
                    wge(e, m_cyl, ev["c_yl"])
                    if do == 0:
                        wge(e, s_s, ev[("e_ev", NTCQ - 1, NDO - 1)])
                    else:
                        wge(e, s_s, ev[("c_ev", do - 1, tc)])
                    for dc in range(DC):
                        i = e.matmul(pb[tc][:, :],
                                     w016[:, (do % 2) * 2048 + dc * 128:(do % 2) * 2048 + (dc + 1) * 128],
                                     big[:, dc * 2048 + tc * TC: dc * 2048 + (tc + 1) * TC],
                                     start=(dc == 0), stop=(dc == DC - 1))
                    s_t.inc(i, 1)
                    ev[("c_m", do, tc)] = s_t.n

                @blk.scalar
                def _(e, do=do, tc=tc):
                    par = tc % 2
                    wge(e, s_t, ev[("c_m", do, tc)])
                    wge(e, s_v, prevC("c_v", do, tc))
                    s_s.inc(e.activation(att32[:, par * TC:(par + 1) * TC], pb[tc][:, :],
                                         AF.Relu, bias=wob[:, do:do + 1]), 1)
                    ev[("c_ev", do, tc)] = s_s.n

                @blk.sync
                def _(e, do=do, tc=tc):
                    par = tc % 2
                    wge(e, s_v, prevC("c_v", do, tc))
                    wge(e, st_f[0], ev[("e_fs", NTCQ - 1, NDO - 2)])
                    wge(e, st_f[1], ev[("e_fs", NTCQ - 1, NDO - 1)])
                    m_cld[par].inc(e.dma_start(out=xC32[:, par * TC:(par + 1) * TC],
                                               in_=x_rows[do * 128:(do + 1) * 128, tc * TC:(tc + 1) * TC]), 16)
                    m_cld[par].inc(e.dma_start(out=xnC16[:, par * TC:(par + 1) * TC],
                                               in_=xn_sc.ap()[do * 128:(do + 1) * 128, tc * TC:(tc + 1) * TC]), 16)
                    m_cld[par].inc(e.dma_start(out=fC16[:, par * TC:(par + 1) * TC],
                                               in_=f_sc.ap()[do * 128:(do + 1) * 128, tc * TC:(tc + 1) * TC]), 16)
                    ev[("c_ld", do, tc)] = m_cld[par].n

                @blk.vector
                def _(e, do=do, tc=tc):
                    par = tc % 2
                    wge(e, s_s, ev[("c_ev", do, tc)])
                    wge(e, m_cld[par], ev[("c_ld", do, tc)])
                    wge(e, st_out[par], prevC("c_st", do, tc))
                    e.scalar_tensor_tensor(outT[:, par * TC:(par + 1) * TC],
                                           xnC16[:, par * TC:(par + 1) * TC],
                                           dvec[:, do:do + 1],
                                           att32[:, par * TC:(par + 1) * TC],
                                           OP.mult, OP.add)
                    e.tensor_tensor(outT[:, par * TC:(par + 1) * TC],
                                    outT[:, par * TC:(par + 1) * TC],
                                    xC32[:, par * TC:(par + 1) * TC], OP.add)
                    s_v.inc(e.tensor_tensor(outT[:, par * TC:(par + 1) * TC],
                                            outT[:, par * TC:(par + 1) * TC],
                                            fC16[:, par * TC:(par + 1) * TC], OP.add), 1)
                    ev[("c_v", do, tc)] = s_v.n

                @blk.scalar
                def _(e, do=do, tc=tc):
                    par = tc % 2
                    wge(e, s_v, ev[("c_v", do, tc)])
                    st_out[par].inc(e.dma_start(
                        out=out_d[do * 128:(do + 1) * 128, tc * TC:(tc + 1) * TC],
                        in_=outT[:, par * TC:(par + 1) * TC]), 16)
                    ev[("c_st", do, tc)] = st_out[par].n

        if DEBUG:
            @blk.sync
            def _(e):
                wge(e, st_out[0], st_out[0].n)
                wge(e, st_out[1], st_out[1].n)
                for dst, src in [(dbg_yin, y_in.ap()), (dbg_yout, y_out.ap()),
                                 (dbg_f, f_sc.ap()), (dbg_pout, p_out.ap()),
                                 (dbg_xn, xn_sc.ap())]:
                    st_dbg.inc(e.dma_start(out=dst, in_=src), 16)
                wge(e, st_dbg, st_dbg.n)

        @blk.sync
        def _(e):
            e.wait_ge(st_out[0].h, st_out[0].n)
            e.wait_ge(st_out[1].h, st_out[1].n)
            e.wait_ge(st_y16.h, st_y16.n)
            e.wait_ge(st_pin.h, st_pin.n)
            e.wait_ge(st_xn.h, st_xn.n)
            e.wait_ge(st_f[0].h, st_f[0].n)
            e.wait_ge(st_f[1].h, st_f[1].n)
            e.wait_ge(st_gst[0].h, st_gst[0].n)
            e.wait_ge(st_gst[1].h, st_gst[1].n)
            e.wait_ge(q_g.h, q_g.n)

    return nc


_NC_CACHE = {}


def make_maps_inner(x, xpw, dtpw, dt_bias, A_log, Dv, Wout, outb, W0, bias0, W1, bias1):
    C = np.ascontiguousarray
    wx_t = C(xpw.T)
    w0_lay = C(W0.T.reshape(16, 128, 64, 128).transpose(2, 0, 1, 3).reshape(64 * 16 * 128, 128))
    w1_lay = C(W1.T.reshape(64, 128, 16, 128).transpose(2, 0, 1, 3).reshape(16 * 64 * 128, 128))
    wout_lay = C(Wout.T.reshape(16, 128, 16, 128).transpose(2, 0, 1, 3).reshape(16 * 16 * 128, 128))
    b0_l = C(bias0.reshape(64, 128).T)
    b1_l = C(bias1.reshape(16, 128).T)
    d_l = C(Dv.reshape(16, 128).T)
    woutb_l = C(outb.reshape(16, 128).T)
    in_maps = []
    for c in range(8):
        bi, qi = c // 4, c % 4
        sl = slice(qi * DSL, (qi + 1) * DSL)
        in_maps.append({
            "x_rows": C(x[bi, qi * TQ:(qi + 1) * TQ, :].T),
            "x_scan": C(x[bi, :, sl].T),
            "wx_t": wx_t,
            "dtpw_t": C(dtpw[sl, :].T),
            "dt_b": C(dt_bias[sl].reshape(4, 128).T),
            "a_log": C(A_log[sl].reshape(4, 128, 16).transpose(1, 0, 2).reshape(128, 64)),
            "d_vec": d_l,
            "wout_lay": wout_lay,
            "woutb": woutb_l,
            "w0_lay": w0_lay,
            "b0_in": b0_l,
            "w1_lay": w1_lay,
            "b1_in": b1_l,
        })
    return in_maps


def run_full(in_maps):
    if "nc" not in _NC_CACHE:
        _NC_CACHE["nc"] = build_nc()
    nc = _NC_CACHE["nc"]
    return run_bass_kernel_spmd(nc, in_maps, core_ids=list(range(8)))


def kernel(L0_attn_hidden_states, L0_attn_x_proj_weight, L0_attn_dt_proj_weight,
           L0_attn_dt_proj_bias, L0_attn_A_log, L0_attn_D, L0_attn_out_proj_weight,
           L0_attn_out_proj_bias, L0_ffn_W0, L0_ffn_bias0, L0_ffn_W1, L0_ffn_bias1):
    x = np.asarray(L0_attn_hidden_states, dtype=np.float32)
    in_maps = make_maps_inner(
        x,
        np.asarray(L0_attn_x_proj_weight, dtype=np.float32),
        np.asarray(L0_attn_dt_proj_weight, dtype=np.float32),
        np.asarray(L0_attn_dt_proj_bias, dtype=np.float32),
        np.asarray(L0_attn_A_log, dtype=np.float32),
        np.asarray(L0_attn_D, dtype=np.float32),
        np.asarray(L0_attn_out_proj_weight, dtype=np.float32),
        np.asarray(L0_attn_out_proj_bias, dtype=np.float32),
        np.asarray(L0_ffn_W0, dtype=np.float32),
        np.asarray(L0_ffn_bias0, dtype=np.float32),
        np.asarray(L0_ffn_W1, dtype=np.float32),
        np.asarray(L0_ffn_bias1, dtype=np.float32),
    )
    res = run_full(in_maps)
    out = np.empty((2, T, DM), dtype=np.float32)
    for c in range(8):
        bi, qi = c // 4, c % 4
        out[bi, qi * TQ:(qi + 1) * TQ, :] = np.asarray(res.results[c]["out"]).T
    return out


# version 07
# speedup vs baseline: 1.0298x; 1.0298x over previous
import numpy as np
from contextlib import ExitStack

import concourse.bass as bass
import concourse.mybir as mybir
from concourse.bass_utils import run_bass_kernel_spmd

F32 = mybir.dt.float32
F16 = mybir.dt.float16
AF = mybir.ActivationFunctionType
OP = mybir.AluOpType

DM = 2048
T = 8192
TQ = 2048
DSL = 512
TC = 512
DC = 16
DCS = 4
NTCQ = 4
NTC = 16
N16 = 16
NJ = 64
NDO = 16
PR = 164

DEBUG = False
CAST_BARRIER = False


class Sem:
    def __init__(self, nc, stack, name):
        self.h = stack.enter_context(nc.semaphore(name))
        self.n = 0

    def inc(self, instr, k):
        instr.then_inc(self.h, k)
        self.n += k
        return self.n


def build_nc():
    nc = bass.Bass("TRN2", target_bir_lowering=False)

    x_rows = nc.dram_tensor("x_rows", [DM, TQ], F32, kind="ExternalInput").ap()
    x_scan = nc.dram_tensor("x_scan", [DSL, T], F32, kind="ExternalInput").ap()
    wx_t = nc.dram_tensor("wx_t", [DM, 160], F32, kind="ExternalInput").ap()
    dtpw_t = nc.dram_tensor("dtpw_t", [128, DSL], F32, kind="ExternalInput").ap()
    dt_b = nc.dram_tensor("dt_b", [128, 4], F32, kind="ExternalInput").ap()
    a_log = nc.dram_tensor("a_log", [128, 64], F32, kind="ExternalInput").ap()
    d_vec = nc.dram_tensor("d_vec", [128, 16], F32, kind="ExternalInput").ap()
    wout_lay = nc.dram_tensor("wout_lay", [NDO * DC * 128, 128], F32, kind="ExternalInput").ap()
    woutb = nc.dram_tensor("woutb", [128, 16], F32, kind="ExternalInput").ap()
    w0_lay = nc.dram_tensor("w0_lay", [NJ * DC * 128, 128], F32, kind="ExternalInput").ap()
    b0_in = nc.dram_tensor("b0_in", [128, 64], F32, kind="ExternalInput").ap()
    w1_lay = nc.dram_tensor("w1_lay", [NDO * NJ * 128, 128], F32, kind="ExternalInput").ap()
    b1_in = nc.dram_tensor("b1_in", [128, 16], F32, kind="ExternalInput").ap()

    out_d = nc.dram_tensor("out", [DM, TQ], F32, kind="ExternalOutput").ap()
    if DEBUG:
        dbg_yin = nc.dram_tensor("dbg_yin", [DSL, T], F16, kind="ExternalOutput").ap()
        dbg_yout = nc.dram_tensor("dbg_yout", [DM, T], F16, kind="ExternalOutput").ap()
        dbg_f = nc.dram_tensor("dbg_f", [DM, TQ], F16, kind="ExternalOutput").ap()
        dbg_pout = nc.dram_tensor("dbg_pout", [4 * PR, TQ], F16, kind="ExternalOutput").ap()
        dbg_xn = nc.dram_tensor("dbg_xn", [DM, TQ], F16, kind="ExternalOutput").ap()

    w0_sc = nc.dram_tensor("w0_sc", [NJ * DC * 128, 128], F16)
    w1_sc = nc.dram_tensor("w1_sc", [NDO * NJ * 128, 128], F16)
    wout_sc = nc.dram_tensor("wout_sc", [NDO * DC * 128, 128], F16)
    xn_sc = nc.dram_tensor("xn_sc", [DM, TQ], F16)
    g_sc = nc.dram_tensor("g_sc", [NJ * 128, TQ], F16)
    f_sc = nc.dram_tensor("f_sc", [DM, TQ], F16)
    p_in = nc.dram_tensor("p_in", [PR, TQ], F16)
    p_out = nc.dram_tensor("p_out", [4 * PR, TQ], F16)
    y_in = nc.dram_tensor("y_in", [DSL, T], F16)
    y_out = nc.dram_tensor("y_out", [DM, T], F16)

    with ExitStack() as stack:
        ec = stack.enter_context

        wx16 = ec(nc.sbuf_tensor("wx16", [128, DC * 160], F16))
        dtpw16 = ec(nc.sbuf_tensor("dtpw16", [128, DSL], F16))
        aneg = ec(nc.sbuf_tensor("aneg", [128, 64], F32))
        dtb = ec(nc.sbuf_tensor("dtb", [128, 4], F32))
        dvec = ec(nc.sbuf_tensor("dvec", [128, 16], F32))
        wob = ec(nc.sbuf_tensor("wob", [128, 16], F32))
        b0 = ec(nc.sbuf_tensor("b0", [128, 64], F32))
        b1 = ec(nc.sbuf_tensor("b1", [128, 16], F32))
        onesc = ec(nc.sbuf_tensor("onesc", [128, 1], F16))
        onesr = ec(nc.sbuf_tensor("onesr", [1, 128], F16))
        big = ec(nc.sbuf_tensor("big", [128, 32768], F16))
        w016 = ec(nc.sbuf_tensor("w016", [128, 2 * 2048], F16))
        w1buf = ec(nc.sbuf_tensor("w1buf", [128, 2 * 4096], F16))
        fstage = ec(nc.sbuf_tensor("fstage", [128, 2 * TC], F16))
        gst = ec(nc.sbuf_tensor("gst", [128, 2 * TC], F16))
        d1st = ec(nc.sbuf_tensor("d1st", [128, 16 * TC], F16))
        mis1 = ec(nc.sbuf_tensor("mis1", [128, 64], F32))
        dtl2 = ec(nc.sbuf_tensor("dtl2", [128, 2 * TC], F16))
        delta16 = ec(nc.sbuf_tensor("delta16", [128, DCS * TC], F16))
        dx16 = ec(nc.sbuf_tensor("dx16", [128, DCS * TC], F16))
        xns16 = ec(nc.sbuf_tensor("xns16", [128, TC], F16))
        xsc32 = ec(nc.sbuf_tensor("xsc32", [128, 2 * TC], F32))
        murs = ec(nc.sbuf_tensor("murs", [128, 2 * TC], F16))
        rsrs = ec(nc.sbuf_tensor("rsrs", [128, 2 * TC], F16))
        brep = ec(nc.sbuf_tensor("brep", [128, 2 * TC], F16))
        crep = ec(nc.sbuf_tensor("crep", [128, 2 * TC], F16))
        a32 = ec(nc.sbuf_tensor("a32", [128, 8 * TC], F32))
        u4 = ec(nc.sbuf_tensor("u4", [128, DCS * TC], F16))
        h16 = ec(nc.sbuf_tensor("h16", [128, DCS * TC], F16))
        tmp16 = ec(nc.sbuf_tensor("tmp16", [128, DCS * TC], F16))
        yacc = ec(nc.sbuf_tensor("yacc", [128, DCS * TC], F32))
        y16s = ec(nc.sbuf_tensor("y16s", [128, DCS * TC], F16))
        hstate = ec(nc.sbuf_tensor("hstate", [128, 64], F32))
        sexp32 = ec(nc.sbuf_tensor("sexp32", [128, TC], F16))
        st_a = ec(nc.sbuf_tensor("st_a", [1, TC], F32))
        st_b = ec(nc.sbuf_tensor("st_b", [1, TC], F32))
        st_c = ec(nc.sbuf_tensor("st_c", [1, TC], F32))
        st_d = ec(nc.sbuf_tensor("st_d", [1, TC], F32))
        mu16 = ec(nc.sbuf_tensor("mu16", [1, TC], F16))
        rstd16 = ec(nc.sbuf_tensor("rstd16", [1, TC], F16))
        dtlst = ec(nc.sbuf_tensor("dtlst", [128, TC], F16))
        bcst = ec(nc.sbuf_tensor("bcst", [32, TC], F16))
        xn0 = ec(nc.sbuf_tensor("xn0", [128, TC], F32))

        pb = [ec(nc.psum_tensor(f"pb{i}", [128, 512], F32)) for i in range(8)]

        s_t = Sem(nc, stack, "s_t")
        s_s = Sem(nc, stack, "s_s")
        s_v = Sem(nc, stack, "s_v")
        s_g = Sem(nc, stack, "s_g")
        s_cc = Sem(nc, stack, "s_cc")
        m_cst = Sem(nc, stack, "m_cst")
        m_xsc = [Sem(nc, stack, f"m_xsc{i}") for i in range(2)]
        m_mur = [Sem(nc, stack, f"m_mur{i}") for i in range(2)]
        m_bc = [Sem(nc, stack, f"m_bc{i}") for i in range(2)]
        m_dtl = [Sem(nc, stack, f"m_dtl{i}") for i in range(2)]
        m_w0 = [Sem(nc, stack, f"m_w0{i}") for i in range(2)]
        qg_w0 = Sem(nc, stack, "qg_w0")
        qg_w1 = Sem(nc, stack, "qg_w1")
        qg_wo = Sem(nc, stack, "qg_wo")
        qg_sb = Sem(nc, stack, "qg_sb")
        m_w1b = [Sem(nc, stack, f"m_w1b{i}") for i in range(2)]
        m_egl = Sem(nc, stack, "m_egl")
        m_cyl = Sem(nc, stack, "m_cyl")
        m_cld = [Sem(nc, stack, f"m_cld{i}") for i in range(2)]
        st_pin = Sem(nc, stack, "st_pin")
        st_xn = Sem(nc, stack, "st_xn")
        st_gst = [Sem(nc, stack, f"st_gst{i}") for i in range(2)]
        st_y16 = Sem(nc, stack, "st_y16")
        st_f = [Sem(nc, stack, f"st_f{i}") for i in range(2)]
        st_out = [Sem(nc, stack, f"st_out{i}") for i in range(2)]
        q_g = Sem(nc, stack, "q_g")
        st_dbg = Sem(nc, stack, "st_dbg")

        blk = ec(nc.Block())
        ev = {}

        def wge(eng, sem, val):
            assert val <= sem.n, f"wait {val} > emitted {sem.n}"
            if val > 0:
                eng.wait_ge(sem.h, val)

        def evget(*keys):
            for k in keys:
                if k in ev:
                    return ev[k]
            return 0

        @blk.sync
        def _(e):
            for dst, src in [(dtb, dt_b), (dvec, d_vec), (wob, woutb), (b0, b0_in), (b1, b1_in)]:
                m_cst.inc(e.dma_start(out=dst[:, :], in_=src), 16)
            m_cst.inc(e.dma_start(out=mis1[:, 0:64], in_=a_log), 16)
            ev["c_l1"] = m_cst.n

        @blk.scalar
        def _(e):
            wge(e, m_cst, ev["c_l1"])
            s_s.inc(e.activation(aneg[:, :], mis1[:, 0:64], AF.Exp), 1)
            ev["c_exp"] = s_s.n

        @blk.vector
        def _(e):
            wge(e, s_s, ev["c_exp"])
            s_v.inc(e.tensor_scalar(aneg[:, :], aneg[:, :], -1.0, None, OP.mult), 1)
            ev["c_aneg"] = s_v.n

        @blk.gpsimd
        def _(e):
            s_g.inc(e.memset(onesc[:, :], 1.0), 1)
            s_g.inc(e.memset(onesr[:, :], 1.0), 1)
            qg_sb.inc(e.dma_start(out=dtpw16[:, :], in_=dtpw_t), 16)
            qg_sb.inc(e.dma_start(out=wx16[:, :], in_=wx_t.rearrange("(c p) n -> p c n", p=128)), 16)
            ev["sbcast"] = qg_sb.n
            NB0 = NJ * DC * 128 // 16
            for i in range(16):
                if i >= 2 and i % 2 == 0:
                    e.wait_ge(qg_w0.h, 16 * i)
                qg_w0.inc(e.dma_start(out=w0_sc.ap()[i * NB0:(i + 1) * NB0, :],
                                      in_=w0_lay[i * NB0:(i + 1) * NB0, :]), 16)
            ev["w0cast"] = qg_w0.n
            NBO = NDO * DC * 128 // 4
            for i in range(4):
                if i >= 2 and i % 2 == 0:
                    e.wait_ge(qg_wo.h, 16 * i)
                qg_wo.inc(e.dma_start(out=wout_sc.ap()[i * NBO:(i + 1) * NBO, :],
                                      in_=wout_lay[i * NBO:(i + 1) * NBO, :]), 16)
            ev["wocast"] = qg_wo.n
            NB1 = NDO * NJ * 128 // 16
            for i in range(16):
                if i >= 2 and i % 2 == 0:
                    e.wait_ge(qg_w1.h, 16 * i)
                qg_w1.inc(e.dma_start(out=w1_sc.ap()[i * NB1:(i + 1) * NB1, :],
                                      in_=w1_lay[i * NB1:(i + 1) * NB1, :]), 16)
            ev["w1cast"] = qg_w1.n
        ev["consts_g"] = s_g.n

        if CAST_BARRIER:
            @blk.sync
            def _(e):
                wge(e, qg_w0, ev["w0cast"])
                wge(e, qg_w1, ev["w1cast"])
                wge(e, qg_wo, ev["wocast"])
                wge(e, qg_sb, ev["sbcast"])

        for tc in range(NTCQ):
            for dc in range(DC):
                par = dc % 2

                @blk.sync
                def _(e, tc=tc, dc=dc, par=par):
                    if dc >= 2:
                        wge(e, s_v, ev[("A1v", tc, dc - 2)])
                        wge(e, s_s, ev[("A1s", tc, dc - 2)])
                    elif tc > 0:
                        wge(e, s_v, ev[("A2v_last", tc - 1)])
                    m_xsc[par].inc(e.dma_start(
                        out=xsc32[:, par * TC:(par + 1) * TC],
                        in_=x_rows[dc * 128:(dc + 1) * 128, tc * TC:(tc + 1) * TC]), 16)
                    ev[("A1l", tc, dc)] = m_xsc[par].n

                @blk.vector
                def _(e, tc=tc, dc=dc, par=par):
                    wge(e, m_xsc[par], ev[("A1l", tc, dc)])
                    if dc >= 2:
                        wge(e, s_t, ev[("Astm", tc, dc - 2)])
                    elif tc > 0:
                        wge(e, s_t, ev[("Astatm", tc - 1)])
                    s_v.inc(e.tensor_copy(tmp16[:, par * TC:(par + 1) * TC],
                                          xsc32[:, par * TC:(par + 1) * TC]), 1)
                    ev[("A1v", tc, dc)] = s_v.n

                @blk.scalar
                def _(e, tc=tc, dc=dc, par=par):
                    wge(e, m_xsc[par], ev[("A1l", tc, dc)])
                    if dc >= 2:
                        wge(e, s_t, ev[("Astm", tc, dc - 2)])
                    elif tc > 0:
                        wge(e, s_t, ev[("Astatm", tc - 1)])
                    s_s.inc(e.activation(tmp16[:, (2 + par) * TC:(3 + par) * TC],
                                         xsc32[:, par * TC:(par + 1) * TC], AF.Square), 1)
                    ev[("A1s", tc, dc)] = s_s.n

                @blk.tensor
                def _(e, tc=tc, dc=dc, par=par):
                    wge(e, s_v, ev[("A1v", tc, dc)])
                    wge(e, s_s, ev[("A1s", tc, dc)])
                    if dc == 0:
                        wge(e, s_g, ev["consts_g"])
                        wge(e, qg_sb, ev["sbcast"])
                        if tc > 0:
                            wge(e, s_v, ev[("Astat", tc - 1)])
                    e.matmul(pb[0][0:1, :], onesc[:, :], tmp16[:, par * TC:(par + 1) * TC],
                             start=(dc == 0), stop=(dc == DC - 1), skip_group_check=True)
                    i2 = e.matmul(pb[0][32:33, :], onesc[:, :], tmp16[:, (2 + par) * TC:(3 + par) * TC],
                                  start=(dc == 0), stop=(dc == DC - 1), skip_group_check=True)
                    s_t.inc(i2, 1)
                    ev[("Astm", tc, dc)] = s_t.n
                    if dc == DC - 1:
                        ev[("Astatm", tc)] = s_t.n

            ev[("A1v_last", tc)] = ev[("A1v", tc, DC - 1)]
            ev[("A1s_last", tc)] = ev[("A1s", tc, DC - 1)]

            @blk.vector
            def _(e, tc=tc):
                wge(e, s_t, ev[("Astatm", tc)])
                if tc > 0:
                    wge(e, st_pin, ev[("Apst", tc - 1)])
                    wge(e, s_v, ev[("Astat", tc - 1)])
                s_v.inc(e.tensor_scalar(st_a[:, :], pb[0][0:1, :], 1.0 / DM, None, OP.mult), 1)
                s_v.inc(e.tensor_scalar(st_b[:, :], pb[0][32:33, :], 1.0 / DM, None, OP.mult), 1)
                wge(e, s_v, s_v.n)
                s_v.inc(e.tensor_tensor(st_c[:, :], st_a[:, :], st_a[:, :], OP.mult), 1)
                wge(e, s_v, s_v.n)
                s_v.inc(e.tensor_tensor(st_d[:, :], st_b[:, :], st_c[:, :], OP.subtract), 1)
                wge(e, s_v, s_v.n)
                s_v.inc(e.tensor_scalar(st_d[:, :], st_d[:, :], 1e-5, None, OP.add), 1)
                wge(e, s_v, s_v.n)
                s_v.inc(e.reciprocal(st_c[:, :], st_d[:, :]), 1)
                wge(e, s_v, s_v.n)
                s_v.inc(e.tensor_copy(mu16[:, :], st_a[:, :]), 1)
                ev[("Astat", tc)] = s_v.n

            @blk.scalar
            def _(e, tc=tc):
                wge(e, s_v, ev[("Astat", tc)])
                if tc > 0:
                    wge(e, st_pin, ev[("Apst", tc - 1)])
                s_s.inc(e.activation(rstd16[:, :], st_c[:, :], AF.Sqrt), 1)
                ev[("Arstd", tc)] = s_s.n

            @blk.tensor
            def _(e, tc=tc):
                wge(e, s_s, ev[("Arstd", tc)])
                if tc > 0:
                    wge(e, s_v, ev[("A2v_last", tc - 1)])
                e.matmul(pb[1][:, :], onesr[:, :], mu16[:, :], start=True, stop=True)
                i2 = e.matmul(pb[2][:, :], onesr[:, :], rstd16[:, :], start=True, stop=True)
                s_t.inc(i2, 1)
                ev[("Arep", tc)] = s_t.n

            for dc in range(DC):
                par = dc % 2

                @blk.sync
                def _(e, tc=tc, dc=dc, par=par):
                    if dc >= 2:
                        wge(e, s_v, ev[("A2v", tc, dc - 2)])
                    else:
                        wge(e, s_v, ev[("A1v_last", tc)])
                        wge(e, s_s, ev[("A1s_last", tc)])
                    m_xsc[par].inc(e.dma_start(
                        out=xsc32[:, par * TC:(par + 1) * TC],
                        in_=x_rows[dc * 128:(dc + 1) * 128, tc * TC:(tc + 1) * TC]), 16)
                    ev[("A2l", tc, dc)] = m_xsc[par].n

                @blk.vector
                def _(e, tc=tc, dc=dc, par=par):
                    wge(e, m_xsc[par], ev[("A2l", tc, dc)])
                    wge(e, s_t, ev[("Arep", tc)])
                    if dc > 0:
                        wge(e, s_v, ev[("A2v", tc, dc - 1)])
                    s_v.inc(e.tensor_tensor(xn0[:, :], xsc32[:, par * TC:(par + 1) * TC],
                                            pb[1][:, :], OP.subtract), 1)
                    wge(e, s_v, s_v.n)
                    s_v.inc(e.tensor_tensor(
                        big[:, dc * 2048 + tc * TC: dc * 2048 + (tc + 1) * TC],
                        xn0[:, :], pb[2][:, :], OP.mult), 1)
                    ev[("A2v", tc, dc)] = s_v.n

                @blk.scalar
                def _(e, tc=tc, dc=dc):
                    wge(e, s_v, ev[("A2v", tc, dc)])
                    st_xn.inc(e.dma_start(
                        out=xn_sc.ap()[dc * 128:(dc + 1) * 128, tc * TC:(tc + 1) * TC],
                        in_=big[:, dc * 2048 + tc * TC: dc * 2048 + (tc + 1) * TC]), 16)
                    ev[("Axn", tc, dc)] = st_xn.n

                @blk.tensor
                def _(e, tc=tc, dc=dc):
                    wge(e, s_v, ev[("A2v", tc, dc)])
                    if dc == 0 and tc > 0:
                        wge(e, s_s, ev[("Apev", tc - 1)])
                    xn_t = big[:, dc * 2048 + tc * TC: dc * 2048 + (tc + 1) * TC]
                    e.matmul(pb[3][:, :], wx16[:, dc * 160: dc * 160 + 128], xn_t,
                             start=(dc == 0), stop=(dc == DC - 1), skip_group_check=True)
                    i2 = e.matmul(pb[4][0:32, :], wx16[:, dc * 160 + 128: (dc + 1) * 160], xn_t,
                                  start=(dc == 0), stop=(dc == DC - 1), skip_group_check=True)
                    if dc == DC - 1:
                        s_t.inc(i2, 1)
                        ev[("Apm", tc)] = s_t.n
            ev[("A2v_last", tc)] = s_v.n

            @blk.scalar
            def _(e, tc=tc):
                wge(e, s_t, ev[("Apm", tc)])
                wge(e, s_v, ev[("Astat", tc)])
                wge(e, s_s, ev[("Arstd", tc)])
                if tc > 0:
                    wge(e, st_pin, ev[("Apst", tc - 1)])
                e.activation(dtlst[:, :], pb[3][:, :], AF.Identity)
                s_s.inc(e.activation(bcst[:, :], pb[4][0:32, :], AF.Identity), 1)
                ev[("Apev", tc)] = s_s.n
                wge(e, s_s, ev[("Apev", tc)])
                st_pin.inc(e.dma_start(out=p_in.ap()[0:128, tc * TC:(tc + 1) * TC], in_=dtlst[:, :]), 16)
                st_pin.inc(e.dma_start(out=p_in.ap()[128:160, tc * TC:(tc + 1) * TC], in_=bcst[:, :]), 16)
                st_pin.inc(e.dma_start(out=p_in.ap()[160:161, tc * TC:(tc + 1) * TC], in_=mu16[:, :]), 16)
                st_pin.inc(e.dma_start(out=p_in.ap()[161:162, tc * TC:(tc + 1) * TC], in_=rstd16[:, :]), 16)
                st_pin.inc(e.dma_start(out=p_in.ap()[162:163, tc * TC:(tc + 1) * TC], in_=rstd16[:, :]), 16)
                st_pin.inc(e.dma_start(out=p_in.ap()[163:164, tc * TC:(tc + 1) * TC], in_=rstd16[:, :]), 16)
                ev[("Apst", tc)] = st_pin.n

        @blk.gpsimd
        def _(e):
            wge(e, st_pin, ev[("Apst", NTCQ - 1)])
            s_cc.inc(e.collective_compute(
                "AllGather", OP.bypass,
                replica_groups=[[0, 1, 2, 3], [4, 5, 6, 7]],
                ins=[p_in.ap().opt()],
                outs=[p_out.ap().opt()],
            ), 1)
            ev["ag1"] = s_cc.n

        d1banks = [0, 1, 2, 3, 5, 6]

        for tc in range(NTC):
            rnk = tc // 4
            tl = (tc % 4) * TC
            tpar = tc % 2

            @blk.sync
            def _(e, tc=tc, rnk=rnk, tl=tl, tpar=tpar):
                wge(e, s_cc, ev["ag1"])
                if tc >= 2:
                    wge(e, s_v, ev[("sdxv", tc - 2, DCS - 1)])
                m_mur[tpar].inc(e.dma_start(
                    out=murs[:, tpar * TC:(tpar + 1) * TC],
                    in_=p_out.ap()[rnk * PR + 160: rnk * PR + 161, tl:tl + TC].broadcast_to([128, TC])), 16)
                m_mur[tpar].inc(e.dma_start(
                    out=rsrs[:, tpar * TC:(tpar + 1) * TC],
                    in_=p_out.ap()[rnk * PR + 161: rnk * PR + 162, tl:tl + TC].broadcast_to([128, TC])), 16)
                ev[("mul", tc)] = m_mur[tpar].n
                if tc >= 2:
                    wge(e, s_t, ev[("sdm", tc - 2, DCS - 1)])
                m_dtl[tpar].inc(e.dma_start(
                    out=dtl2[:, tpar * TC:(tpar + 1) * TC],
                    in_=p_out.ap()[rnk * PR: rnk * PR + 128, tl:tl + TC]), 16)
                ev[("dtll", tc)] = m_dtl[tpar].n
                for dcs in range(2):
                    if tc == 0:
                        wge(e, s_v, ev[("A2v_last", NTCQ - 1)])
                    else:
                        wge(e, s_v, ev[("sdxv", tc - 1, dcs + 2)])
                    m_xsc[dcs].inc(e.dma_start(
                        out=xsc32[:, dcs * TC:(dcs + 1) * TC],
                        in_=x_scan[dcs * 128:(dcs + 1) * 128, tc * TC:(tc + 1) * TC]), 16)
                    ev[("sxl", tc, dcs)] = m_xsc[dcs].n
                for n in (0, 1):
                    if tc > 0:
                        wge(e, s_v, ev[("snv", tc - 1, N16 - 2 + n)])
                    m_bc[n].inc(e.dma_start(
                        out=brep[:, n * TC:(n + 1) * TC],
                        in_=p_out.ap()[rnk * PR + 128 + n: rnk * PR + 129 + n, tl:tl + TC].broadcast_to([128, TC])), 16)
                    m_bc[n].inc(e.dma_start(
                        out=crep[:, n * TC:(n + 1) * TC],
                        in_=p_out.ap()[rnk * PR + 144 + n: rnk * PR + 145 + n, tl:tl + TC].broadcast_to([128, TC])), 16)
                    ev[("sbl", tc, n)] = m_bc[n].n

            for dcs in range(DCS):
                @blk.tensor
                def _(e, tc=tc, dcs=dcs, tpar=tpar):
                    bank = 4 if dcs % 2 == 0 else 7
                    wge(e, m_dtl[tpar], ev[("dtll", tc)])
                    wge(e, s_s, evget(("sdev", tc, dcs - 2), ("sdev", tc - 1, dcs + 2)))
                    if (tc, dcs) == (0, 0):
                        wge(e, qg_sb, ev["sbcast"])
                        wge(e, s_s, ev[("Apev", NTCQ - 1)])
                    i = e.matmul(pb[bank][:, :], dtpw16[:, dcs * 128:(dcs + 1) * 128],
                                 dtl2[:, tpar * TC:(tpar + 1) * TC], start=True, stop=True)
                    s_t.inc(i, 1)
                    ev[("sdm", tc, dcs)] = s_t.n

                @blk.scalar
                def _(e, tc=tc, dcs=dcs):
                    bank = 4 if dcs % 2 == 0 else 7
                    wge(e, s_t, ev[("sdm", tc, dcs)])
                    wge(e, s_s, evget(("sdev", tc, dcs - 1), ("sdev", tc - 1, DCS - 1)))
                    if tc > 0:
                        wge(e, s_v, ev[("sdxv", tc - 1, dcs)])
                        wge(e, s_s, ev[("sexp", tc - 1, N16 - 1, dcs)])
                    s_s.inc(e.activation(sexp32[:, :], pb[bank][:, :], AF.Exp, bias=dtb[:, dcs:dcs + 1]), 1)
                    wge(e, s_s, s_s.n)
                    s_s.inc(e.activation(delta16[:, dcs * TC:(dcs + 1) * TC], sexp32[:, :],
                                         AF.Ln, bias=1.0), 1)
                    ev[("sdev", tc, dcs)] = s_s.n

            for dcs in range(DCS):
                @blk.vector
                def _(e, tc=tc, dcs=dcs, tpar=tpar):
                    pxr = dcs % 2
                    wge(e, m_xsc[pxr], ev[("sxl", tc, dcs)])
                    wge(e, m_mur[tpar], ev[("mul", tc)])
                    wge(e, s_s, ev[("sdev", tc, dcs)])
                    if dcs > 0:
                        wge(e, s_v, ev[("sdxv", tc, dcs - 1)])
                    s_v.inc(e.tensor_tensor(xn0[:, :], xsc32[:, pxr * TC:(pxr + 1) * TC],
                                            murs[:, tpar * TC:(tpar + 1) * TC], OP.subtract), 1)
                    wge(e, s_v, s_v.n)
                    s_v.inc(e.tensor_tensor(xns16[:, :], xn0[:, :],
                                            rsrs[:, tpar * TC:(tpar + 1) * TC], OP.mult), 1)
                    wge(e, s_v, s_v.n)
                    s_v.inc(e.tensor_tensor(dx16[:, dcs * TC:(dcs + 1) * TC],
                                            delta16[:, dcs * TC:(dcs + 1) * TC],
                                            xns16[:, :], OP.mult), 1)
                    ev[("sdxv", tc, dcs)] = s_v.n

                if dcs < 2:
                    @blk.sync
                    def _(e, tc=tc, dcs=dcs):
                        m = dcs + 2
                        wge(e, s_v, ev[("sdxv", tc, dcs)])
                        m_xsc[m % 2].inc(e.dma_start(
                            out=xsc32[:, (m % 2) * TC:(m % 2 + 1) * TC],
                            in_=x_scan[m * 128:(m + 1) * 128, tc * TC:(tc + 1) * TC]), 16)
                        ev[("sxl", tc, m)] = m_xsc[m % 2].n

            for n in (0, 1):
                for dcs in range(DCS):
                    @blk.scalar
                    def _(e, tc=tc, n=n, dcs=dcs):
                        slot = (n % 2) * 4 + dcs
                        wge(e, s_v, ev["c_aneg"])
                        wge(e, s_s, ev[("sdev", tc, dcs)])
                        if tc > 0:
                            wge(e, s_v, ev[("sscan", tc - 1, N16 - 2 + n, dcs)])
                        s_s.inc(e.activation(a32[:, slot * TC:(slot + 1) * TC],
                                             delta16[:, dcs * TC:(dcs + 1) * TC],
                                             AF.Exp, scale=aneg[:, dcs * 16 + n:dcs * 16 + n + 1]), 1)
                        ev[("sexp", tc, n, dcs)] = s_s.n

            for n in range(N16):
                np_ = n % 2

                @blk.vector
                def _(e, tc=tc, n=n, np_=np_):
                    wge(e, m_bc[np_], ev[("sbl", tc, n)])
                    wge(e, s_v, ev[("sdxv", tc, DCS - 1)])
                    wge(e, s_g, evget(("shc", tc, n - 1), ("shc", tc - 1, N16 - 1)))
                    if tc > 0:
                        wge(e, s_g, ev[("shc", tc - 1, n)])
                    for dcs in range(DCS):
                        wge(e, s_s, ev[("sexp", tc, n, dcs)])
                        wge(e, s_v, evget(("sscan", tc, n - 1, dcs), ("sscan", tc - 1, N16 - 1, dcs)))
                        s_v.inc(e.tensor_tensor(u4[:, dcs * TC:(dcs + 1) * TC],
                                                dx16[:, dcs * TC:(dcs + 1) * TC],
                                                brep[:, np_ * TC:(np_ + 1) * TC], OP.mult), 1)
                    ev[("su4", tc, n)] = s_v.n
                    wge(e, s_v, ev[("su4", tc, n)])
                    for dcs in range(DCS):
                        slot = np_ * 4 + dcs
                        if n == 0:
                            wge(e, s_g, evget(("sya", tc - 1, N16 - 1, dcs)))
                        else:
                            wge(e, s_g, evget(("sya", tc, n - 1, dcs),
                                              ("sya", tc - 1, N16 - 1, dcs)))
                        wge(e, s_v, evget(("sym", tc, n - 1, dcs), ("sym", tc - 1, N16 - 1, dcs)))
                        if n == 0 and tc > 0:
                            wge(e, s_v, ev[("syc", tc - 1)])
                        if tc == 0 and n == 1:
                            wge(e, s_t, ev[("Astatm", NTCQ - 1)])
                        init = 0.0 if tc == 0 else hstate[:, dcs * 16 + n:dcs * 16 + n + 1]
                        i = e.tensor_tensor_scan(h16[:, dcs * TC:(dcs + 1) * TC],
                                                 a32[:, slot * TC:(slot + 1) * TC],
                                                 u4[:, dcs * TC:(dcs + 1) * TC], init, OP.mult, OP.add)
                        s_v.inc(i, 1)
                        ev[("sscan", tc, n, dcs)] = s_v.n
                        wge(e, s_v, ev[("sscan", tc, n, dcs)])
                        if n == 0:
                            i = e.tensor_tensor(yacc[:, dcs * TC:(dcs + 1) * TC],
                                                h16[:, dcs * TC:(dcs + 1) * TC],
                                                crep[:, np_ * TC:(np_ + 1) * TC], OP.mult)
                        else:
                            i = e.tensor_tensor(tmp16[:, dcs * TC:(dcs + 1) * TC],
                                                h16[:, dcs * TC:(dcs + 1) * TC],
                                                crep[:, np_ * TC:(np_ + 1) * TC], OP.mult)
                        s_v.inc(i, 1)
                        ev[("sym", tc, n, dcs)] = s_v.n
                    ev[("snv", tc, n)] = s_v.n

                @blk.gpsimd
                def _(e, tc=tc, n=n):
                    wge(e, s_v, ev[("sscan", tc, n, DCS - 1)])
                    s_g.inc(e.tensor_copy(hstate[:, n:64:16], h16[:, TC - 1: DCS * TC: TC]), 1)
                    ev[("shc", tc, n)] = s_g.n
                    if n > 0:
                        for dcs in range(DCS):
                            wge(e, s_v, ev[("sym", tc, n, dcs)])
                            wge(e, s_g, evget(("sya", tc, n - 1, dcs)))
                            s_g.inc(e.tensor_tensor(yacc[:, dcs * TC:(dcs + 1) * TC],
                                                    yacc[:, dcs * TC:(dcs + 1) * TC],
                                                    tmp16[:, dcs * TC:(dcs + 1) * TC], OP.add), 1)
                            ev[("sya", tc, n, dcs)] = s_g.n

                if n + 2 < N16:
                    @blk.sync
                    def _(e, tc=tc, n=n, rnk=rnk, tl=tl):
                        m = n + 2
                        wge(e, s_v, ev[("snv", tc, n)])
                        m_bc[m % 2].inc(e.dma_start(
                            out=brep[:, (m % 2) * TC:(m % 2 + 1) * TC],
                            in_=p_out.ap()[rnk * PR + 128 + m: rnk * PR + 129 + m, tl:tl + TC].broadcast_to([128, TC])), 16)
                        m_bc[m % 2].inc(e.dma_start(
                            out=crep[:, (m % 2) * TC:(m % 2 + 1) * TC],
                            in_=p_out.ap()[rnk * PR + 144 + m: rnk * PR + 145 + m, tl:tl + TC].broadcast_to([128, TC])), 16)
                        ev[("sbl", tc, m)] = m_bc[m % 2].n

                    for dcs in range(DCS):
                        @blk.scalar
                        def _(e, tc=tc, n=n, dcs=dcs):
                            m = n + 2
                            wge(e, s_v, ev[("sscan", tc, n, dcs)])
                            slot = (m % 2) * 4 + dcs
                            s_s.inc(e.activation(a32[:, slot * TC:(slot + 1) * TC],
                                                 delta16[:, dcs * TC:(dcs + 1) * TC],
                                                 AF.Exp, scale=aneg[:, dcs * 16 + m:dcs * 16 + m + 1]), 1)
                            ev[("sexp", tc, m, dcs)] = s_s.n

            @blk.vector
            def _(e, tc=tc):
                if tc >= 1:
                    wge(e, st_y16, ev[("syst", tc - 1)])
                for dcs in range(DCS):
                    wge(e, s_g, ev[("sya", tc, N16 - 1, dcs)])
                    s_v.inc(e.tensor_copy(y16s[:, dcs * TC:(dcs + 1) * TC],
                                          yacc[:, dcs * TC:(dcs + 1) * TC]), 1)
                ev[("syc", tc)] = s_v.n

            @blk.scalar
            def _(e, tc=tc):
                wge(e, s_v, ev[("syc", tc)])
                for dcs in range(DCS):
                    st_y16.inc(e.dma_start(
                        out=y_in.ap()[dcs * 128:(dcs + 1) * 128, tc * TC:(tc + 1) * TC],
                        in_=y16s[:, dcs * TC:(dcs + 1) * TC]), 16)
                ev[("syst", tc)] = st_y16.n

            for k in range(16):
                u = tc * 16 + k
                j, tcq = u // 4, u % 4
                bank = d1banks[u % 6]

                if tcq == 0:
                    @blk.sync
                    def _(e, j=j):
                        if j < 2:
                            wge(e, qg_w0, ev["w0cast"])
                        else:
                            wge(e, s_t, ev[("d1m", (j - 2) * 4 + 3)])
                        src = w0_sc.ap()[j * DC * 128:(j + 1) * DC * 128, :].rearrange("(a p) b -> p a b", p=128)
                        m_w0[j % 2].inc(e.dma_start(out=w016[:, (j % 2) * 2048:(j % 2 + 1) * 2048], in_=src), 16)
                        ev[("w0l", j)] = m_w0[j % 2].n

                @blk.tensor
                def _(e, u=u, j=j, tcq=tcq, bank=bank):
                    wge(e, m_w0[j % 2], ev[("w0l", j)])
                    wge(e, s_s, evget(("d1e", u - 6)))
                    if u < 6:
                        wge(e, s_v, ev[("A2v_last", NTCQ - 1)])
                        wge(e, s_s, ev[("Apev", NTCQ - 1)])
                    for dc in range(DC):
                        i = e.matmul(pb[bank][:, :],
                                     w016[:, (j % 2) * 2048 + dc * 128:(j % 2) * 2048 + (dc + 1) * 128],
                                     big[:, dc * 2048 + tcq * TC: dc * 2048 + (tcq + 1) * TC],
                                     start=(dc == 0), stop=(dc == DC - 1))
                    s_t.inc(i, 1)
                    ev[("d1m", u)] = s_t.n

                @blk.scalar
                def _(e, u=u, j=j, tcq=tcq, bank=bank):
                    kk = u % 16
                    wge(e, s_t, ev[("d1m", u)])
                    wge(e, s_s, evget(("gel", u - 16)))
                    s_s.inc(e.activation(d1st[:, kk * TC:(kk + 1) * TC], pb[bank][:, :],
                                         AF.Identity, bias=b0[:, j:j + 1]), 1)
                    ev[("d1e", u)] = s_s.n

            for k in range(16):
                u = tc * 16 + k
                j, tcq = u // 4, u % 4
                bank = d1banks[u % 6]

                @blk.scalar
                def _(e, u=u, j=j, tcq=tcq, bank=bank):
                    kk = u % 16
                    par = u % 2
                    wge(e, s_s, ev[("d1e", u)])
                    if u >= 2:
                        wge(e, st_gst[par], ev[("gst", u - 2)])
                    s_s.inc(e.activation(gst[:, par * TC:(par + 1) * TC], d1st[:, kk * TC:(kk + 1) * TC],
                                         AF.Gelu), 1)
                    ev[("gel", u)] = s_s.n
                    wge(e, s_s, ev[("gel", u)])
                    st_gst[par].inc(e.dma_start(
                        out=g_sc.ap()[j * 128:(j + 1) * 128, tcq * TC:(tcq + 1) * TC],
                        in_=gst[:, par * TC:(par + 1) * TC]), 16)
                    ev[("gst", u)] = st_gst[par].n

        @blk.gpsimd
        def _(e):
            wge(e, st_y16, ev[("syst", NTC - 1)])
            s_cc.inc(e.collective_compute(
                "AllGather", OP.bypass,
                replica_groups=[[0, 1, 2, 3], [4, 5, 6, 7]],
                ins=[y_in.ap().opt()],
                outs=[y_out.ap().opt()],
            ), 1)
            ev["ag2"] = s_cc.n

        for tc in range(NTCQ):
            @blk.sync
            def _(e, tc=tc):
                wge(e, s_t, ev[("d1m", NTC * 16 - 1)])
                if tc > 0:
                    wge(e, s_t, ev[("e_m", tc - 1, NDO - 1)])
                wge(e, st_gst[0], ev[("gst", NTC * 16 - 2)])
                wge(e, st_gst[1], ev[("gst", NTC * 16 - 1)])
                wge(e, st_xn, ev[("Axn", NTCQ - 1, DC - 1)])
                for b8 in range(8):
                    src = g_sc.ap()[b8 * 8 * 128:(b8 + 1) * 8 * 128, tc * TC:(tc + 1) * TC]
                    src = src.rearrange("(a p) b -> p a b", p=128)
                    m_egl.inc(e.dma_start(out=big[:, b8 * 8 * TC:(b8 + 1) * 8 * TC], in_=src), 16)
                ev[("e_gl", tc)] = m_egl.n

            for do in range(NDO):
                for hh in range(2):
                    @blk.sync
                    def _(e, tc=tc, do=do, hh=hh):
                        key = ("e_mid" if hh == 0 else "e_m")
                        wge(e, s_t, evget((key, tc, do - 1), (key, tc - 1, NDO - 1)))
                        wge(e, qg_w1, ev["w1cast"])
                        base = do * NJ * 128
                        src = w1_sc.ap()[base + hh * 32 * 128: base + (hh + 1) * 32 * 128, :]
                        src = src.rearrange("(a p) b -> p a b", p=128)
                        m_w1b[hh].inc(e.dma_start(out=w1buf[:, hh * 4096:(hh + 1) * 4096], in_=src), 16)
                        ev[("e_wl", tc, do, hh)] = m_w1b[hh].n

                @blk.tensor
                def _(e, tc=tc, do=do):
                    bank = 2 if do % 2 == 0 else 3
                    wge(e, m_egl, ev[("e_gl", tc)])
                    wge(e, s_s, evget(("e_ev", tc, do - 2), ("e_ev", tc - 1, NDO - 2 + (do % 2))))
                    wge(e, m_w1b[0], ev[("e_wl", tc, do, 0)])
                    for j in range(NJ):
                        hh = j // 32
                        if j == 32:
                            wge(e, m_w1b[1], ev[("e_wl", tc, do, 1)])
                        i = e.matmul(pb[bank][:, :],
                                     w1buf[:, hh * 4096 + (j - hh * 32) * 128: hh * 4096 + (j - hh * 32 + 1) * 128],
                                     big[:, j * TC:(j + 1) * TC],
                                     start=(j == 0), stop=(j == NJ - 1))
                        if j == 31:
                            s_t.inc(i, 1)
                            ev[("e_mid", tc, do)] = s_t.n
                    s_t.inc(i, 1)
                    ev[("e_m", tc, do)] = s_t.n

                @blk.scalar
                def _(e, tc=tc, do=do):
                    bank = 2 if do % 2 == 0 else 3
                    par = do % 2
                    wge(e, s_t, ev[("e_m", tc, do)])
                    wge(e, st_f[par], evget(("e_fs", tc, do - 2), ("e_fs", tc - 1, NDO - 2 + (do % 2))))
                    s_s.inc(e.activation(fstage[:, par * TC:(par + 1) * TC], pb[bank][:, :],
                                         AF.Identity, bias=b1[:, do:do + 1]), 1)
                    ev[("e_ev", tc, do)] = s_s.n
                    wge(e, s_s, ev[("e_ev", tc, do)])
                    st_f[par].inc(e.dma_start(
                        out=f_sc.ap()[do * 128:(do + 1) * 128, tc * TC:(tc + 1) * TC],
                        in_=fstage[:, par * TC:(par + 1) * TC]), 16)
                    ev[("e_fs", tc, do)] = st_f[par].n

        @blk.sync
        def _(e):
            wge(e, s_cc, ev["ag2"])
            wge(e, s_t, ev[("e_m", NTCQ - 1, NDO - 1)])
            pid = e.partition_id()
            qoff = (pid % 4) * TQ
            for dc in range(DC):
                m_cyl.inc(e.dma_start(
                    out=big[:, dc * 2048:(dc + 1) * 2048],
                    in_=y_out.ap()[dc * 128:(dc + 1) * 128, bass.ds(qoff, TQ)]), 16)
            ev["c_yl"] = m_cyl.n

        def prevC(name, do, tc):
            if tc >= 2:
                return ev.get((name, do, tc - 2), 0)
            if do >= 1:
                return ev.get((name, do - 1, tc + 2), 0)
            return 0

        for do in range(NDO):
            @blk.sync
            def _(e, do=do):
                if do < 2:
                    wge(e, qg_wo, ev["wocast"])
                    wge(e, s_t, ev[("d1m", NTC * 16 - 1)])
                else:
                    wge(e, s_t, ev[("c_m", do - 2, NTCQ - 1)])
                src = wout_sc.ap()[do * DC * 128:(do + 1) * DC * 128, :].rearrange("(a p) b -> p a b", p=128)
                m_w0[do % 2].inc(e.dma_start(out=w016[:, (do % 2) * 2048:(do % 2 + 1) * 2048], in_=src), 16)
                ev[("c_wl", do)] = m_w0[do % 2].n

            for tc in range(NTCQ):
                @blk.tensor
                def _(e, do=do, tc=tc):
                    wge(e, m_w0[do % 2], ev[("c_wl", do)])
                    wge(e, m_cyl, ev["c_yl"])
                    if do == 0:
                        wge(e, s_s, ev[("e_ev", NTCQ - 1, NDO - 1)])
                    else:
                        wge(e, s_s, ev[("c_ev", do - 1, tc)])
                    for dc in range(DC):
                        i = e.matmul(pb[tc][:, :],
                                     w016[:, (do % 2) * 2048 + dc * 128:(do % 2) * 2048 + (dc + 1) * 128],
                                     big[:, dc * 2048 + tc * TC: dc * 2048 + (tc + 1) * TC],
                                     start=(dc == 0), stop=(dc == DC - 1))
                    s_t.inc(i, 1)
                    ev[("c_m", do, tc)] = s_t.n

                @blk.scalar
                def _(e, do=do, tc=tc):
                    par = tc % 2
                    wge(e, s_t, ev[("c_m", do, tc)])
                    wge(e, s_v, prevC("c_v", do, tc))
                    s_s.inc(e.activation(a32[:, 1024 + par * TC: 1024 + (par + 1) * TC], pb[tc][:, :],
                                         AF.Relu, bias=wob[:, do:do + 1]), 1)
                    ev[("c_ev", do, tc)] = s_s.n

                @blk.sync
                def _(e, do=do, tc=tc):
                    par = tc % 2
                    wge(e, s_v, prevC("c_v", do, tc))
                    wge(e, st_f[0], ev[("e_fs", NTCQ - 1, NDO - 2)])
                    wge(e, st_f[1], ev[("e_fs", NTCQ - 1, NDO - 1)])
                    m_cld[par].inc(e.dma_start(out=a32[:, par * TC:(par + 1) * TC],
                                               in_=x_rows[do * 128:(do + 1) * 128, tc * TC:(tc + 1) * TC]), 16)
                    m_cld[par].inc(e.dma_start(out=a32[:, 3584 + 256 * par: 3584 + 256 * (par + 1)].bitcast(F16),
                                               in_=xn_sc.ap()[do * 128:(do + 1) * 128, tc * TC:(tc + 1) * TC]), 16)
                    m_cld[par].inc(e.dma_start(out=a32[:, 3072 + 256 * par: 3072 + 256 * (par + 1)].bitcast(F16),
                                               in_=f_sc.ap()[do * 128:(do + 1) * 128, tc * TC:(tc + 1) * TC]), 16)
                    ev[("c_ld", do, tc)] = m_cld[par].n

                @blk.vector
                def _(e, do=do, tc=tc):
                    par = tc % 2
                    wge(e, s_s, ev[("c_ev", do, tc)])
                    wge(e, m_cld[par], ev[("c_ld", do, tc)])
                    wge(e, st_out[par], prevC("c_st", do, tc))
                    if (do, tc) == (0, 0):
                        wge(e, s_v, ev[("snv", NTC - 1, N16 - 1)])
                    s_v.inc(e.scalar_tensor_tensor(a32[:, 2048 + par * TC: 2048 + (par + 1) * TC],
                                           a32[:, 3584 + 256 * par: 3584 + 256 * (par + 1)].bitcast(F16),
                                           dvec[:, do:do + 1],
                                           a32[:, 1024 + par * TC: 1024 + (par + 1) * TC],
                                           OP.mult, OP.add), 1)
                    wge(e, s_v, s_v.n)
                    s_v.inc(e.tensor_tensor(a32[:, 2048 + par * TC: 2048 + (par + 1) * TC],
                                    a32[:, 2048 + par * TC: 2048 + (par + 1) * TC],
                                    a32[:, par * TC:(par + 1) * TC], OP.add), 1)
                    wge(e, s_v, s_v.n)
                    s_v.inc(e.tensor_tensor(a32[:, 2048 + par * TC: 2048 + (par + 1) * TC],
                                            a32[:, 2048 + par * TC: 2048 + (par + 1) * TC],
                                            a32[:, 3072 + 256 * par: 3072 + 256 * (par + 1)].bitcast(F16), OP.add), 1)
                    ev[("c_v", do, tc)] = s_v.n

                @blk.scalar
                def _(e, do=do, tc=tc):
                    par = tc % 2
                    wge(e, s_v, ev[("c_v", do, tc)])
                    st_out[par].inc(e.dma_start(
                        out=out_d[do * 128:(do + 1) * 128, tc * TC:(tc + 1) * TC],
                        in_=a32[:, 2048 + par * TC: 2048 + (par + 1) * TC]), 16)
                    ev[("c_st", do, tc)] = st_out[par].n

        if DEBUG:
            @blk.sync
            def _(e):
                wge(e, st_out[0], st_out[0].n)
                wge(e, st_out[1], st_out[1].n)
                for dst, src in [(dbg_yin, y_in.ap()), (dbg_yout, y_out.ap()),
                                 (dbg_f, f_sc.ap()), (dbg_pout, p_out.ap()),
                                 (dbg_xn, xn_sc.ap())]:
                    st_dbg.inc(e.dma_start(out=dst, in_=src), 16)
                wge(e, st_dbg, st_dbg.n)

        @blk.sync
        def _(e):
            e.wait_ge(st_out[0].h, st_out[0].n)
            e.wait_ge(st_out[1].h, st_out[1].n)
            e.wait_ge(st_y16.h, st_y16.n)
            e.wait_ge(st_pin.h, st_pin.n)
            e.wait_ge(st_xn.h, st_xn.n)
            e.wait_ge(st_f[0].h, st_f[0].n)
            e.wait_ge(st_f[1].h, st_f[1].n)
            e.wait_ge(st_gst[0].h, st_gst[0].n)
            e.wait_ge(st_gst[1].h, st_gst[1].n)
            e.wait_ge(qg_w0.h, qg_w0.n)
            e.wait_ge(qg_w1.h, qg_w1.n)
            e.wait_ge(qg_wo.h, qg_wo.n)
            e.wait_ge(qg_sb.h, qg_sb.n)

    return nc


_NC_CACHE = {}


def make_maps_inner(x, xpw, dtpw, dt_bias, A_log, Dv, Wout, outb, W0, bias0, W1, bias1):
    C = np.ascontiguousarray
    wx_t = C(xpw.T)
    w0_lay = C(W0.T.reshape(16, 128, 64, 128).transpose(2, 0, 1, 3).reshape(64 * 16 * 128, 128))
    w1_lay = C(W1.T.reshape(64, 128, 16, 128).transpose(2, 0, 1, 3).reshape(16 * 64 * 128, 128))
    wout_lay = C(Wout.T.reshape(16, 128, 16, 128).transpose(2, 0, 1, 3).reshape(16 * 16 * 128, 128))
    b0_l = C(bias0.reshape(64, 128).T)
    b1_l = C(bias1.reshape(16, 128).T)
    d_l = C(Dv.reshape(16, 128).T)
    woutb_l = C(outb.reshape(16, 128).T)
    in_maps = []
    for c in range(8):
        bi, qi = c // 4, c % 4
        sl = slice(qi * DSL, (qi + 1) * DSL)
        in_maps.append({
            "x_rows": C(x[bi, qi * TQ:(qi + 1) * TQ, :].T),
            "x_scan": C(x[bi, :, sl].T),
            "wx_t": wx_t,
            "dtpw_t": C(dtpw[sl, :].T),
            "dt_b": C(dt_bias[sl].reshape(4, 128).T),
            "a_log": C(A_log[sl].reshape(4, 128, 16).transpose(1, 0, 2).reshape(128, 64)),
            "d_vec": d_l,
            "wout_lay": wout_lay,
            "woutb": woutb_l,
            "w0_lay": w0_lay,
            "b0_in": b0_l,
            "w1_lay": w1_lay,
            "b1_in": b1_l,
        })
    return in_maps


def run_full(in_maps):
    if "nc" not in _NC_CACHE:
        _NC_CACHE["nc"] = build_nc()
    nc = _NC_CACHE["nc"]
    return run_bass_kernel_spmd(nc, in_maps, core_ids=list(range(8)))


def kernel(L0_attn_hidden_states, L0_attn_x_proj_weight, L0_attn_dt_proj_weight,
           L0_attn_dt_proj_bias, L0_attn_A_log, L0_attn_D, L0_attn_out_proj_weight,
           L0_attn_out_proj_bias, L0_ffn_W0, L0_ffn_bias0, L0_ffn_W1, L0_ffn_bias1):
    x = np.asarray(L0_attn_hidden_states, dtype=np.float32)
    in_maps = make_maps_inner(
        x,
        np.asarray(L0_attn_x_proj_weight, dtype=np.float32),
        np.asarray(L0_attn_dt_proj_weight, dtype=np.float32),
        np.asarray(L0_attn_dt_proj_bias, dtype=np.float32),
        np.asarray(L0_attn_A_log, dtype=np.float32),
        np.asarray(L0_attn_D, dtype=np.float32),
        np.asarray(L0_attn_out_proj_weight, dtype=np.float32),
        np.asarray(L0_attn_out_proj_bias, dtype=np.float32),
        np.asarray(L0_ffn_W0, dtype=np.float32),
        np.asarray(L0_ffn_bias0, dtype=np.float32),
        np.asarray(L0_ffn_W1, dtype=np.float32),
        np.asarray(L0_ffn_bias1, dtype=np.float32),
    )
    res = run_full(in_maps)
    out = np.empty((2, T, DM), dtype=np.float32)
    for c in range(8):
        bi, qi = c // 4, c % 4
        out[bi, qi * TQ:(qi + 1) * TQ, :] = np.asarray(res.results[c]["out"]).T
    return out


# version 08
# speedup vs baseline: 1.0815x; 1.0502x over previous
import numpy as np
from contextlib import ExitStack

import concourse.bass as bass
import concourse.mybir as mybir
from concourse.bass_utils import run_bass_kernel_spmd

F32 = mybir.dt.float32
F16 = mybir.dt.float16
AF = mybir.ActivationFunctionType
OP = mybir.AluOpType

DM = 2048
T = 8192
TQ = 2048
DSL = 512
TC = 512
DC = 16
DCS = 4
NTCQ = 4
NTC = 16
N16 = 16
NJ = 64
NDO = 16
PR = 164

DEBUG = False
CAST_BARRIER = False


class Sem:
    def __init__(self, nc, stack, name):
        self.h = stack.enter_context(nc.semaphore(name))
        self.n = 0

    def inc(self, instr, k):
        instr.then_inc(self.h, k)
        self.n += k
        return self.n


def build_nc():
    nc = bass.Bass("TRN2", target_bir_lowering=False)

    x_rows = nc.dram_tensor("x_rows", [DM, TQ], F32, kind="ExternalInput").ap()
    x_scan = nc.dram_tensor("x_scan", [DSL, T], F32, kind="ExternalInput").ap()
    wx_t = nc.dram_tensor("wx_t", [DM, 160], F32, kind="ExternalInput").ap()
    dtpw_t = nc.dram_tensor("dtpw_t", [128, DSL], F32, kind="ExternalInput").ap()
    dt_b = nc.dram_tensor("dt_b", [128, 4], F32, kind="ExternalInput").ap()
    a_log = nc.dram_tensor("a_log", [128, 64], F32, kind="ExternalInput").ap()
    d_vec = nc.dram_tensor("d_vec", [128, 16], F32, kind="ExternalInput").ap()
    wout_lay = nc.dram_tensor("wout_lay", [NDO * DC * 128, 128], F32, kind="ExternalInput").ap()
    woutb = nc.dram_tensor("woutb", [128, 16], F32, kind="ExternalInput").ap()
    w0_lay = nc.dram_tensor("w0_lay", [NJ * DC * 128, 128], F32, kind="ExternalInput").ap()
    b0_in = nc.dram_tensor("b0_in", [128, 64], F32, kind="ExternalInput").ap()
    w1_lay = nc.dram_tensor("w1_lay", [NDO * NJ * 128, 128], F32, kind="ExternalInput").ap()
    b1_in = nc.dram_tensor("b1_in", [128, 16], F32, kind="ExternalInput").ap()

    out_d = nc.dram_tensor("out", [DM, TQ], F32, kind="ExternalOutput").ap()
    if DEBUG:
        dbg_yin = nc.dram_tensor("dbg_yin", [DSL, T], F16, kind="ExternalOutput").ap()
        dbg_yout = nc.dram_tensor("dbg_yout", [DM, T], F16, kind="ExternalOutput").ap()
        dbg_f = nc.dram_tensor("dbg_f", [DM, TQ], F16, kind="ExternalOutput").ap()
        dbg_pout = nc.dram_tensor("dbg_pout", [4 * PR, TQ], F16, kind="ExternalOutput").ap()
        dbg_xn = nc.dram_tensor("dbg_xn", [DM, TQ], F16, kind="ExternalOutput").ap()

    w0_sc = nc.dram_tensor("w0_sc", [NJ * DC * 128, 128], F16)
    w1_sc = nc.dram_tensor("w1_sc", [NDO * NJ * 128, 128], F16)
    wout_sc = nc.dram_tensor("wout_sc", [NDO * DC * 128, 128], F16)
    xn_sc = nc.dram_tensor("xn_sc", [DM, TQ], F16)
    g_sc = nc.dram_tensor("g_sc", [NJ * 128, TQ], F16)
    f_sc = nc.dram_tensor("f_sc", [DM, TQ], F16)
    p_in = nc.dram_tensor("p_in", [PR, TQ], F16)
    p_out = nc.dram_tensor("p_out", [4 * PR, TQ], F16)
    y_in = nc.dram_tensor("y_in", [DSL, T], F16)
    y_out = nc.dram_tensor("y_out", [DM, T], F16)

    with ExitStack() as stack:
        ec = stack.enter_context

        wx16 = ec(nc.sbuf_tensor("wx16", [128, DC * 160], F16))
        dtpw16 = ec(nc.sbuf_tensor("dtpw16", [128, DSL], F16))
        aneg = ec(nc.sbuf_tensor("aneg", [128, 64], F32))
        dtb = ec(nc.sbuf_tensor("dtb", [128, 4], F32))
        dvec = ec(nc.sbuf_tensor("dvec", [128, 16], F32))
        wob = ec(nc.sbuf_tensor("wob", [128, 16], F32))
        b0 = ec(nc.sbuf_tensor("b0", [128, 64], F32))
        b1 = ec(nc.sbuf_tensor("b1", [128, 16], F32))
        onesc = ec(nc.sbuf_tensor("onesc", [128, 1], F16))
        onesr = ec(nc.sbuf_tensor("onesr", [1, 128], F16))
        big = ec(nc.sbuf_tensor("big", [128, 32768], F16))
        w016 = ec(nc.sbuf_tensor("w016", [128, 2 * 2048], F16))
        w1buf = ec(nc.sbuf_tensor("w1buf", [128, 2 * 4096], F16))
        fstage = ec(nc.sbuf_tensor("fstage", [128, 2 * TC], F16))
        d1st = ec(nc.sbuf_tensor("d1st", [128, 16 * TC], F16))
        mis1 = ec(nc.sbuf_tensor("mis1", [128, 64], F32))
        dtl2 = ec(nc.sbuf_tensor("dtl2", [128, 2 * TC], F16))
        delta16 = ec(nc.sbuf_tensor("delta16", [128, DCS * TC], F16))
        dx16 = ec(nc.sbuf_tensor("dx16", [128, DCS * TC], F16))
        xns16 = ec(nc.sbuf_tensor("xns16", [128, TC], F16))
        xsc32 = ec(nc.sbuf_tensor("xsc32", [128, 2 * TC], F32))
        murs = ec(nc.sbuf_tensor("murs", [128, 2 * TC], F16))
        rsrs = ec(nc.sbuf_tensor("rsrs", [128, 2 * TC], F16))
        brep = ec(nc.sbuf_tensor("brep", [128, 2 * TC], F16))
        crep = ec(nc.sbuf_tensor("crep", [128, 2 * TC], F16))
        a32 = ec(nc.sbuf_tensor("a32", [128, 8 * TC], F32))
        u4 = ec(nc.sbuf_tensor("u4", [128, DCS * TC], F16))
        h16 = ec(nc.sbuf_tensor("h16", [128, DCS * TC], F16))
        tmp16 = ec(nc.sbuf_tensor("tmp16", [128, DCS * TC], F16))
        yacc = ec(nc.sbuf_tensor("yacc", [128, DCS * TC], F32))
        y16s = ec(nc.sbuf_tensor("y16s", [128, DCS * TC], F16))
        hstate = ec(nc.sbuf_tensor("hstate", [128, 64], F32))
        sexp32 = ec(nc.sbuf_tensor("sexp32", [128, TC], F16))
        st_a = ec(nc.sbuf_tensor("st_a", [1, TC], F32))
        st_b = ec(nc.sbuf_tensor("st_b", [1, TC], F32))
        st_c = ec(nc.sbuf_tensor("st_c", [1, TC], F32))
        st_d = ec(nc.sbuf_tensor("st_d", [1, TC], F32))
        mu16 = ec(nc.sbuf_tensor("mu16", [1, TC], F16))
        rstd16 = ec(nc.sbuf_tensor("rstd16", [1, TC], F16))
        dtlst = ec(nc.sbuf_tensor("dtlst", [128, TC], F16))
        bcst = ec(nc.sbuf_tensor("bcst", [32, TC], F16))
        xn0 = ec(nc.sbuf_tensor("xn0", [128, TC], F32))

        pb = [ec(nc.psum_tensor(f"pb{i}", [128, 512], F32)) for i in range(8)]

        s_t = Sem(nc, stack, "s_t")
        s_s = Sem(nc, stack, "s_s")
        s_v = Sem(nc, stack, "s_v")
        s_g = Sem(nc, stack, "s_g")
        s_cc = Sem(nc, stack, "s_cc")
        m_cst = Sem(nc, stack, "m_cst")
        m_xsc = [Sem(nc, stack, f"m_xsc{i}") for i in range(2)]
        m_mur = [Sem(nc, stack, f"m_mur{i}") for i in range(2)]
        m_bc = [Sem(nc, stack, f"m_bc{i}") for i in range(2)]
        m_dtl = [Sem(nc, stack, f"m_dtl{i}") for i in range(2)]
        m_w0 = [Sem(nc, stack, f"m_w0{i}") for i in range(2)]
        qg_w0 = Sem(nc, stack, "qg_w0")
        qg_w1 = Sem(nc, stack, "qg_w1")
        qg_wo = Sem(nc, stack, "qg_wo")
        qg_sb = Sem(nc, stack, "qg_sb")
        m_w1b = [Sem(nc, stack, f"m_w1b{i}") for i in range(2)]
        m_egl = Sem(nc, stack, "m_egl")
        m_cyl = Sem(nc, stack, "m_cyl")
        m_cld = [Sem(nc, stack, f"m_cld{i}") for i in range(2)]
        st_pin = Sem(nc, stack, "st_pin")
        st_xn = Sem(nc, stack, "st_xn")
        st_gst = [Sem(nc, stack, f"st_gst{i}") for i in range(2)]
        st_y16 = Sem(nc, stack, "st_y16")
        st_f = [Sem(nc, stack, f"st_f{i}") for i in range(2)]
        st_out = [Sem(nc, stack, f"st_out{i}") for i in range(2)]
        q_g = Sem(nc, stack, "q_g")
        st_dbg = Sem(nc, stack, "st_dbg")

        blk = ec(nc.Block())
        ev = {}

        def wge(eng, sem, val):
            assert val <= sem.n, f"wait {val} > emitted {sem.n}"
            if val > 0:
                eng.wait_ge(sem.h, val)

        def evget(*keys):
            for k in keys:
                if k in ev:
                    return ev[k]
            return 0

        @blk.sync
        def _(e):
            for dst, src in [(dtb, dt_b), (dvec, d_vec), (wob, woutb), (b0, b0_in), (b1, b1_in)]:
                m_cst.inc(e.dma_start(out=dst[:, :], in_=src), 16)
            m_cst.inc(e.dma_start(out=mis1[:, 0:64], in_=a_log), 16)
            ev["c_l1"] = m_cst.n

        @blk.scalar
        def _(e):
            wge(e, m_cst, ev["c_l1"])
            s_s.inc(e.activation(aneg[:, :], mis1[:, 0:64], AF.Exp), 1)
            ev["c_exp"] = s_s.n

        @blk.vector
        def _(e):
            wge(e, s_s, ev["c_exp"])
            s_v.inc(e.tensor_scalar(aneg[:, :], aneg[:, :], -1.0, None, OP.mult), 1)
            ev["c_aneg"] = s_v.n

        @blk.gpsimd
        def _(e):
            s_g.inc(e.memset(onesc[:, :], 1.0), 1)
            s_g.inc(e.memset(onesr[:, :], 1.0), 1)
            qg_sb.inc(e.dma_start(out=dtpw16[:, :], in_=dtpw_t), 16)
            qg_sb.inc(e.dma_start(out=wx16[:, :], in_=wx_t.rearrange("(c p) n -> p c n", p=128)), 16)
            ev["sbcast"] = qg_sb.n
            NB0 = NJ * DC * 128 // 16
            for i in range(16):
                if i >= 2 and i % 2 == 0:
                    e.wait_ge(qg_w0.h, 16 * i)
                qg_w0.inc(e.dma_start(out=w0_sc.ap()[i * NB0:(i + 1) * NB0, :],
                                      in_=w0_lay[i * NB0:(i + 1) * NB0, :]), 16)
            ev["w0cast"] = qg_w0.n
            NBO = NDO * DC * 128 // 4
            for i in range(4):
                if i >= 2 and i % 2 == 0:
                    e.wait_ge(qg_wo.h, 16 * i)
                qg_wo.inc(e.dma_start(out=wout_sc.ap()[i * NBO:(i + 1) * NBO, :],
                                      in_=wout_lay[i * NBO:(i + 1) * NBO, :]), 16)
            ev["wocast"] = qg_wo.n
            NB1 = NDO * NJ * 128 // 16
            for i in range(16):
                if i >= 2 and i % 2 == 0:
                    e.wait_ge(qg_w1.h, 16 * i)
                qg_w1.inc(e.dma_start(out=w1_sc.ap()[i * NB1:(i + 1) * NB1, :],
                                      in_=w1_lay[i * NB1:(i + 1) * NB1, :]), 16)
            ev["w1cast"] = qg_w1.n
        ev["consts_g"] = s_g.n

        if CAST_BARRIER:
            @blk.sync
            def _(e):
                wge(e, qg_w0, ev["w0cast"])
                wge(e, qg_w1, ev["w1cast"])
                wge(e, qg_wo, ev["wocast"])
                wge(e, qg_sb, ev["sbcast"])

        for tc in range(NTCQ):
            for dc in range(DC):
                par = dc % 2

                @blk.sync
                def _(e, tc=tc, dc=dc, par=par):
                    if dc >= 2:
                        wge(e, s_v, ev[("A1v", tc, dc - 2)])
                        wge(e, s_s, ev[("A1s", tc, dc - 2)])
                    elif tc > 0:
                        wge(e, s_v, ev[("A2v_last", tc - 1)])
                    m_xsc[par].inc(e.dma_start(
                        out=xsc32[:, par * TC:(par + 1) * TC],
                        in_=x_rows[dc * 128:(dc + 1) * 128, tc * TC:(tc + 1) * TC]), 16)
                    ev[("A1l", tc, dc)] = m_xsc[par].n

                @blk.vector
                def _(e, tc=tc, dc=dc, par=par):
                    wge(e, m_xsc[par], ev[("A1l", tc, dc)])
                    if dc >= 2:
                        wge(e, s_t, ev[("Astm", tc, dc - 2)])
                    elif tc > 0:
                        wge(e, s_t, ev[("Astatm", tc - 1)])
                    s_v.inc(e.tensor_copy(tmp16[:, par * TC:(par + 1) * TC],
                                          xsc32[:, par * TC:(par + 1) * TC]), 1)
                    ev[("A1v", tc, dc)] = s_v.n

                @blk.scalar
                def _(e, tc=tc, dc=dc, par=par):
                    wge(e, m_xsc[par], ev[("A1l", tc, dc)])
                    if dc >= 2:
                        wge(e, s_t, ev[("Astm", tc, dc - 2)])
                    elif tc > 0:
                        wge(e, s_t, ev[("Astatm", tc - 1)])
                    s_s.inc(e.activation(tmp16[:, (2 + par) * TC:(3 + par) * TC],
                                         xsc32[:, par * TC:(par + 1) * TC], AF.Square), 1)
                    ev[("A1s", tc, dc)] = s_s.n

                @blk.tensor
                def _(e, tc=tc, dc=dc, par=par):
                    wge(e, s_v, ev[("A1v", tc, dc)])
                    wge(e, s_s, ev[("A1s", tc, dc)])
                    if dc == 0:
                        wge(e, s_g, ev["consts_g"])
                        wge(e, qg_sb, ev["sbcast"])
                        if tc > 0:
                            wge(e, s_v, ev[("Astat", tc - 1)])
                    e.matmul(pb[0][0:1, :], onesc[:, :], tmp16[:, par * TC:(par + 1) * TC],
                             start=(dc == 0), stop=(dc == DC - 1), skip_group_check=True)
                    i2 = e.matmul(pb[0][32:33, :], onesc[:, :], tmp16[:, (2 + par) * TC:(3 + par) * TC],
                                  start=(dc == 0), stop=(dc == DC - 1), skip_group_check=True)
                    s_t.inc(i2, 1)
                    ev[("Astm", tc, dc)] = s_t.n
                    if dc == DC - 1:
                        ev[("Astatm", tc)] = s_t.n

            ev[("A1v_last", tc)] = ev[("A1v", tc, DC - 1)]
            ev[("A1s_last", tc)] = ev[("A1s", tc, DC - 1)]

            @blk.vector
            def _(e, tc=tc):
                wge(e, s_t, ev[("Astatm", tc)])
                if tc > 0:
                    wge(e, st_pin, ev[("Apst", tc - 1)])
                    wge(e, s_v, ev[("Astat", tc - 1)])
                s_v.inc(e.tensor_scalar(st_a[:, :], pb[0][0:1, :], 1.0 / DM, None, OP.mult), 1)
                s_v.inc(e.tensor_scalar(st_b[:, :], pb[0][32:33, :], 1.0 / DM, None, OP.mult), 1)
                wge(e, s_v, s_v.n)
                s_v.inc(e.tensor_tensor(st_c[:, :], st_a[:, :], st_a[:, :], OP.mult), 1)
                wge(e, s_v, s_v.n)
                s_v.inc(e.tensor_tensor(st_d[:, :], st_b[:, :], st_c[:, :], OP.subtract), 1)
                wge(e, s_v, s_v.n)
                s_v.inc(e.tensor_scalar(st_d[:, :], st_d[:, :], 1e-5, None, OP.add), 1)
                wge(e, s_v, s_v.n)
                s_v.inc(e.reciprocal(st_c[:, :], st_d[:, :]), 1)
                wge(e, s_v, s_v.n)
                s_v.inc(e.tensor_copy(mu16[:, :], st_a[:, :]), 1)
                ev[("Astat", tc)] = s_v.n

            @blk.scalar
            def _(e, tc=tc):
                wge(e, s_v, ev[("Astat", tc)])
                if tc > 0:
                    wge(e, st_pin, ev[("Apst", tc - 1)])
                s_s.inc(e.activation(rstd16[:, :], st_c[:, :], AF.Sqrt), 1)
                ev[("Arstd", tc)] = s_s.n

            @blk.tensor
            def _(e, tc=tc):
                wge(e, s_s, ev[("Arstd", tc)])
                if tc > 0:
                    wge(e, s_v, ev[("A2v_last", tc - 1)])
                e.matmul(pb[1][:, :], onesr[:, :], mu16[:, :], start=True, stop=True)
                i2 = e.matmul(pb[2][:, :], onesr[:, :], rstd16[:, :], start=True, stop=True)
                s_t.inc(i2, 1)
                ev[("Arep", tc)] = s_t.n

            for dc in range(DC):
                par = dc % 2

                @blk.sync
                def _(e, tc=tc, dc=dc, par=par):
                    if dc >= 2:
                        wge(e, s_v, ev[("A2v", tc, dc - 2)])
                    else:
                        wge(e, s_v, ev[("A1v_last", tc)])
                        wge(e, s_s, ev[("A1s_last", tc)])
                    m_xsc[par].inc(e.dma_start(
                        out=xsc32[:, par * TC:(par + 1) * TC],
                        in_=x_rows[dc * 128:(dc + 1) * 128, tc * TC:(tc + 1) * TC]), 16)
                    ev[("A2l", tc, dc)] = m_xsc[par].n

                @blk.vector
                def _(e, tc=tc, dc=dc, par=par):
                    wge(e, m_xsc[par], ev[("A2l", tc, dc)])
                    wge(e, s_t, ev[("Arep", tc)])
                    if dc > 0:
                        wge(e, s_v, ev[("A2v", tc, dc - 1)])
                    s_v.inc(e.tensor_tensor(xn0[:, :], xsc32[:, par * TC:(par + 1) * TC],
                                            pb[1][:, :], OP.subtract), 1)
                    wge(e, s_v, s_v.n)
                    s_v.inc(e.tensor_tensor(
                        big[:, dc * 2048 + tc * TC: dc * 2048 + (tc + 1) * TC],
                        xn0[:, :], pb[2][:, :], OP.mult), 1)
                    ev[("A2v", tc, dc)] = s_v.n

                @blk.scalar
                def _(e, tc=tc, dc=dc):
                    wge(e, s_v, ev[("A2v", tc, dc)])
                    st_xn.inc(e.dma_start(
                        out=xn_sc.ap()[dc * 128:(dc + 1) * 128, tc * TC:(tc + 1) * TC],
                        in_=big[:, dc * 2048 + tc * TC: dc * 2048 + (tc + 1) * TC]), 16)
                    ev[("Axn", tc, dc)] = st_xn.n

                @blk.tensor
                def _(e, tc=tc, dc=dc):
                    wge(e, s_v, ev[("A2v", tc, dc)])
                    if dc == 0 and tc > 0:
                        wge(e, s_s, ev[("Apev", tc - 1)])
                    xn_t = big[:, dc * 2048 + tc * TC: dc * 2048 + (tc + 1) * TC]
                    e.matmul(pb[3][:, :], wx16[:, dc * 160: dc * 160 + 128], xn_t,
                             start=(dc == 0), stop=(dc == DC - 1), skip_group_check=True)
                    i2 = e.matmul(pb[4][0:32, :], wx16[:, dc * 160 + 128: (dc + 1) * 160], xn_t,
                                  start=(dc == 0), stop=(dc == DC - 1), skip_group_check=True)
                    if dc == DC - 1:
                        s_t.inc(i2, 1)
                        ev[("Apm", tc)] = s_t.n
            ev[("A2v_last", tc)] = s_v.n

            @blk.scalar
            def _(e, tc=tc):
                wge(e, s_t, ev[("Apm", tc)])
                wge(e, s_v, ev[("Astat", tc)])
                wge(e, s_s, ev[("Arstd", tc)])
                if tc > 0:
                    wge(e, st_pin, ev[("Apst", tc - 1)])
                e.activation(dtlst[:, :], pb[3][:, :], AF.Identity)
                s_s.inc(e.activation(bcst[:, :], pb[4][0:32, :], AF.Identity), 1)
                ev[("Apev", tc)] = s_s.n
                wge(e, s_s, ev[("Apev", tc)])
                st_pin.inc(e.dma_start(out=p_in.ap()[0:128, tc * TC:(tc + 1) * TC], in_=dtlst[:, :]), 16)
                st_pin.inc(e.dma_start(out=p_in.ap()[128:160, tc * TC:(tc + 1) * TC], in_=bcst[:, :]), 16)
                st_pin.inc(e.dma_start(out=p_in.ap()[160:161, tc * TC:(tc + 1) * TC], in_=mu16[:, :]), 16)
                st_pin.inc(e.dma_start(out=p_in.ap()[161:162, tc * TC:(tc + 1) * TC], in_=rstd16[:, :]), 16)
                st_pin.inc(e.dma_start(out=p_in.ap()[162:163, tc * TC:(tc + 1) * TC], in_=rstd16[:, :]), 16)
                st_pin.inc(e.dma_start(out=p_in.ap()[163:164, tc * TC:(tc + 1) * TC], in_=rstd16[:, :]), 16)
                ev[("Apst", tc)] = st_pin.n

        @blk.gpsimd
        def _(e):
            wge(e, st_pin, ev[("Apst", NTCQ - 1)])
            s_cc.inc(e.collective_compute(
                "AllGather", OP.bypass,
                replica_groups=[[0, 1, 2, 3], [4, 5, 6, 7]],
                ins=[p_in.ap().opt()],
                outs=[p_out.ap().opt()],
            ), 1)
            ev["ag1"] = s_cc.n

        d1banks = [0, 1, 2, 3, 5, 6]

        for tc in range(NTC):
            rnk = tc // 4
            tl = (tc % 4) * TC
            tpar = tc % 2

            @blk.sync
            def _(e, tc=tc, rnk=rnk, tl=tl, tpar=tpar):
                wge(e, s_cc, ev["ag1"])
                if tc >= 2:
                    wge(e, s_v, ev[("sdxv", tc - 2, DCS - 1)])
                m_mur[tpar].inc(e.dma_start(
                    out=murs[:, tpar * TC:(tpar + 1) * TC],
                    in_=p_out.ap()[rnk * PR + 160: rnk * PR + 161, tl:tl + TC].broadcast_to([128, TC])), 16)
                m_mur[tpar].inc(e.dma_start(
                    out=rsrs[:, tpar * TC:(tpar + 1) * TC],
                    in_=p_out.ap()[rnk * PR + 161: rnk * PR + 162, tl:tl + TC].broadcast_to([128, TC])), 16)
                ev[("mul", tc)] = m_mur[tpar].n
                if tc >= 2:
                    wge(e, s_t, ev[("sdm", tc - 2, DCS - 1)])
                m_dtl[tpar].inc(e.dma_start(
                    out=dtl2[:, tpar * TC:(tpar + 1) * TC],
                    in_=p_out.ap()[rnk * PR: rnk * PR + 128, tl:tl + TC]), 16)
                ev[("dtll", tc)] = m_dtl[tpar].n
                for dcs in range(2):
                    if tc == 0:
                        wge(e, s_v, ev[("A2v_last", NTCQ - 1)])
                    else:
                        wge(e, s_v, ev[("sdxv", tc - 1, dcs + 2)])
                    m_xsc[dcs].inc(e.dma_start(
                        out=xsc32[:, dcs * TC:(dcs + 1) * TC],
                        in_=x_scan[dcs * 128:(dcs + 1) * 128, tc * TC:(tc + 1) * TC]), 16)
                    ev[("sxl", tc, dcs)] = m_xsc[dcs].n
                for n in (0, 1):
                    if tc > 0:
                        wge(e, s_v, ev[("snv", tc - 1, N16 - 2 + n)])
                    m_bc[n].inc(e.dma_start(
                        out=brep[:, n * TC:(n + 1) * TC],
                        in_=p_out.ap()[rnk * PR + 128 + n: rnk * PR + 129 + n, tl:tl + TC].broadcast_to([128, TC])), 16)
                    m_bc[n].inc(e.dma_start(
                        out=crep[:, n * TC:(n + 1) * TC],
                        in_=p_out.ap()[rnk * PR + 144 + n: rnk * PR + 145 + n, tl:tl + TC].broadcast_to([128, TC])), 16)
                    ev[("sbl", tc, n)] = m_bc[n].n

            for dcs in range(DCS):
                @blk.tensor
                def _(e, tc=tc, dcs=dcs, tpar=tpar):
                    bank = 4 if dcs % 2 == 0 else 7
                    wge(e, m_dtl[tpar], ev[("dtll", tc)])
                    wge(e, s_s, evget(("sdev", tc, dcs - 2), ("sdev", tc - 1, dcs + 2)))
                    if (tc, dcs) == (0, 0):
                        wge(e, qg_sb, ev["sbcast"])
                        wge(e, s_s, ev[("Apev", NTCQ - 1)])
                    i = e.matmul(pb[bank][:, :], dtpw16[:, dcs * 128:(dcs + 1) * 128],
                                 dtl2[:, tpar * TC:(tpar + 1) * TC], start=True, stop=True)
                    s_t.inc(i, 1)
                    ev[("sdm", tc, dcs)] = s_t.n

                @blk.scalar
                def _(e, tc=tc, dcs=dcs):
                    bank = 4 if dcs % 2 == 0 else 7
                    wge(e, s_t, ev[("sdm", tc, dcs)])
                    wge(e, s_s, evget(("sdev", tc, dcs - 1), ("sdev", tc - 1, DCS - 1)))
                    if tc > 0:
                        wge(e, s_v, ev[("sdxv", tc - 1, dcs)])
                        wge(e, s_s, ev[("sexp", tc - 1, N16 - 1, dcs)])
                    s_s.inc(e.activation(sexp32[:, :], pb[bank][:, :], AF.Exp, bias=dtb[:, dcs:dcs + 1]), 1)
                    wge(e, s_s, s_s.n)
                    s_s.inc(e.activation(delta16[:, dcs * TC:(dcs + 1) * TC], sexp32[:, :],
                                         AF.Ln, bias=1.0), 1)
                    ev[("sdev", tc, dcs)] = s_s.n

            for dcs in range(DCS):
                @blk.vector
                def _(e, tc=tc, dcs=dcs, tpar=tpar):
                    pxr = dcs % 2
                    wge(e, m_xsc[pxr], ev[("sxl", tc, dcs)])
                    wge(e, m_mur[tpar], ev[("mul", tc)])
                    wge(e, s_s, ev[("sdev", tc, dcs)])
                    if dcs > 0:
                        wge(e, s_v, ev[("sdxv", tc, dcs - 1)])
                    s_v.inc(e.tensor_tensor(xn0[:, :], xsc32[:, pxr * TC:(pxr + 1) * TC],
                                            murs[:, tpar * TC:(tpar + 1) * TC], OP.subtract), 1)
                    wge(e, s_v, s_v.n)
                    s_v.inc(e.tensor_tensor(xns16[:, :], xn0[:, :],
                                            rsrs[:, tpar * TC:(tpar + 1) * TC], OP.mult), 1)
                    wge(e, s_v, s_v.n)
                    s_v.inc(e.tensor_tensor(dx16[:, dcs * TC:(dcs + 1) * TC],
                                            delta16[:, dcs * TC:(dcs + 1) * TC],
                                            xns16[:, :], OP.mult), 1)
                    ev[("sdxv", tc, dcs)] = s_v.n

                if dcs < 2:
                    @blk.sync
                    def _(e, tc=tc, dcs=dcs):
                        m = dcs + 2
                        wge(e, s_v, ev[("sdxv", tc, dcs)])
                        m_xsc[m % 2].inc(e.dma_start(
                            out=xsc32[:, (m % 2) * TC:(m % 2 + 1) * TC],
                            in_=x_scan[m * 128:(m + 1) * 128, tc * TC:(tc + 1) * TC]), 16)
                        ev[("sxl", tc, m)] = m_xsc[m % 2].n

            for n in (0, 1):
                for dcs in range(DCS):
                    @blk.scalar
                    def _(e, tc=tc, n=n, dcs=dcs):
                        slot = (n % 2) * 4 + dcs
                        wge(e, s_v, ev["c_aneg"])
                        wge(e, s_s, ev[("sdev", tc, dcs)])
                        if tc > 0:
                            wge(e, s_v, ev[("sscan", tc - 1, N16 - 2 + n, dcs)])
                        s_s.inc(e.activation(a32[:, slot * TC:(slot + 1) * TC],
                                             delta16[:, dcs * TC:(dcs + 1) * TC],
                                             AF.Exp, scale=aneg[:, dcs * 16 + n:dcs * 16 + n + 1]), 1)
                        ev[("sexp", tc, n, dcs)] = s_s.n

            for n in range(N16):
                np_ = n % 2

                @blk.vector
                def _(e, tc=tc, n=n, np_=np_):
                    wge(e, m_bc[np_], ev[("sbl", tc, n)])
                    wge(e, s_v, ev[("sdxv", tc, DCS - 1)])
                    wge(e, s_g, evget(("shc", tc, n - 1), ("shc", tc - 1, N16 - 1)))
                    if tc > 0:
                        wge(e, s_g, ev[("shc", tc - 1, n)])
                    for dcs in range(DCS):
                        wge(e, s_s, ev[("sexp", tc, n, dcs)])
                        wge(e, s_v, evget(("sscan", tc, n - 1, dcs), ("sscan", tc - 1, N16 - 1, dcs)))
                        s_v.inc(e.tensor_tensor(u4[:, dcs * TC:(dcs + 1) * TC],
                                                dx16[:, dcs * TC:(dcs + 1) * TC],
                                                brep[:, np_ * TC:(np_ + 1) * TC], OP.mult), 1)
                    ev[("su4", tc, n)] = s_v.n
                    wge(e, s_v, ev[("su4", tc, n)])
                    for dcs in range(DCS):
                        slot = np_ * 4 + dcs
                        if n == 0:
                            wge(e, s_g, evget(("sya", tc - 1, N16 - 1, dcs)))
                        else:
                            wge(e, s_g, evget(("sya", tc, n - 1, dcs),
                                              ("sya", tc - 1, N16 - 1, dcs)))
                        wge(e, s_v, evget(("sym", tc, n - 1, dcs), ("sym", tc - 1, N16 - 1, dcs)))
                        if n == 0 and tc > 0:
                            wge(e, s_v, ev[("syc", tc - 1)])
                        if tc == 0 and n == 1:
                            wge(e, s_t, ev[("Astatm", NTCQ - 1)])
                        init = 0.0 if tc == 0 else hstate[:, dcs * 16 + n:dcs * 16 + n + 1]
                        i = e.tensor_tensor_scan(h16[:, dcs * TC:(dcs + 1) * TC],
                                                 a32[:, slot * TC:(slot + 1) * TC],
                                                 u4[:, dcs * TC:(dcs + 1) * TC], init, OP.mult, OP.add)
                        s_v.inc(i, 1)
                        ev[("sscan", tc, n, dcs)] = s_v.n
                        wge(e, s_v, ev[("sscan", tc, n, dcs)])
                        if n == 0:
                            i = e.tensor_tensor(yacc[:, dcs * TC:(dcs + 1) * TC],
                                                h16[:, dcs * TC:(dcs + 1) * TC],
                                                crep[:, np_ * TC:(np_ + 1) * TC], OP.mult)
                        else:
                            i = e.tensor_tensor(tmp16[:, dcs * TC:(dcs + 1) * TC],
                                                h16[:, dcs * TC:(dcs + 1) * TC],
                                                crep[:, np_ * TC:(np_ + 1) * TC], OP.mult)
                        s_v.inc(i, 1)
                        ev[("sym", tc, n, dcs)] = s_v.n
                    ev[("snv", tc, n)] = s_v.n

                @blk.gpsimd
                def _(e, tc=tc, n=n):
                    wge(e, s_v, ev[("sscan", tc, n, DCS - 1)])
                    s_g.inc(e.tensor_copy(hstate[:, n:64:16], h16[:, TC - 1: DCS * TC: TC]), 1)
                    ev[("shc", tc, n)] = s_g.n
                    if n > 0:
                        for dcs in range(DCS):
                            wge(e, s_v, ev[("sym", tc, n, dcs)])
                            wge(e, s_g, evget(("sya", tc, n - 1, dcs)))
                            s_g.inc(e.tensor_tensor(yacc[:, dcs * TC:(dcs + 1) * TC],
                                                    yacc[:, dcs * TC:(dcs + 1) * TC],
                                                    tmp16[:, dcs * TC:(dcs + 1) * TC], OP.add), 1)
                            ev[("sya", tc, n, dcs)] = s_g.n

                if n + 2 < N16:
                    @blk.sync
                    def _(e, tc=tc, n=n, rnk=rnk, tl=tl):
                        m = n + 2
                        wge(e, s_v, ev[("snv", tc, n)])
                        m_bc[m % 2].inc(e.dma_start(
                            out=brep[:, (m % 2) * TC:(m % 2 + 1) * TC],
                            in_=p_out.ap()[rnk * PR + 128 + m: rnk * PR + 129 + m, tl:tl + TC].broadcast_to([128, TC])), 16)
                        m_bc[m % 2].inc(e.dma_start(
                            out=crep[:, (m % 2) * TC:(m % 2 + 1) * TC],
                            in_=p_out.ap()[rnk * PR + 144 + m: rnk * PR + 145 + m, tl:tl + TC].broadcast_to([128, TC])), 16)
                        ev[("sbl", tc, m)] = m_bc[m % 2].n

                    for dcs in range(DCS):
                        @blk.scalar
                        def _(e, tc=tc, n=n, dcs=dcs):
                            m = n + 2
                            wge(e, s_v, ev[("sscan", tc, n, dcs)])
                            slot = (m % 2) * 4 + dcs
                            s_s.inc(e.activation(a32[:, slot * TC:(slot + 1) * TC],
                                                 delta16[:, dcs * TC:(dcs + 1) * TC],
                                                 AF.Exp, scale=aneg[:, dcs * 16 + m:dcs * 16 + m + 1]), 1)
                            ev[("sexp", tc, m, dcs)] = s_s.n

            @blk.vector
            def _(e, tc=tc):
                if tc >= 1:
                    wge(e, st_y16, ev[("syst", tc - 1)])
                for dcs in range(DCS):
                    wge(e, s_g, ev[("sya", tc, N16 - 1, dcs)])
                    s_v.inc(e.tensor_copy(y16s[:, dcs * TC:(dcs + 1) * TC],
                                          yacc[:, dcs * TC:(dcs + 1) * TC]), 1)
                ev[("syc", tc)] = s_v.n

            @blk.scalar
            def _(e, tc=tc):
                wge(e, s_v, ev[("syc", tc)])
                for dcs in range(DCS):
                    st_y16.inc(e.dma_start(
                        out=y_in.ap()[dcs * 128:(dcs + 1) * 128, tc * TC:(tc + 1) * TC],
                        in_=y16s[:, dcs * TC:(dcs + 1) * TC]), 16)
                ev[("syst", tc)] = st_y16.n

            for k in range(16):
                u = tc * 16 + k
                j, tcq = u // 4, u % 4
                bank = d1banks[u % 6]

                if tcq == 0:
                    @blk.sync
                    def _(e, j=j):
                        if j < 2:
                            wge(e, qg_w0, ev["w0cast"])
                        else:
                            wge(e, s_t, ev[("d1m", (j - 2) * 4 + 3)])
                        src = w0_sc.ap()[j * DC * 128:(j + 1) * DC * 128, :].rearrange("(a p) b -> p a b", p=128)
                        m_w0[j % 2].inc(e.dma_start(out=w016[:, (j % 2) * 2048:(j % 2 + 1) * 2048], in_=src), 16)
                        ev[("w0l", j)] = m_w0[j % 2].n

                @blk.tensor
                def _(e, u=u, j=j, tcq=tcq, bank=bank):
                    wge(e, m_w0[j % 2], ev[("w0l", j)])
                    wge(e, s_s, evget(("d1e", u - 6)))
                    if u < 6:
                        wge(e, s_v, ev[("A2v_last", NTCQ - 1)])
                        wge(e, s_s, ev[("Apev", NTCQ - 1)])
                    for dc in range(DC):
                        i = e.matmul(pb[bank][:, :],
                                     w016[:, (j % 2) * 2048 + dc * 128:(j % 2) * 2048 + (dc + 1) * 128],
                                     big[:, dc * 2048 + tcq * TC: dc * 2048 + (tcq + 1) * TC],
                                     start=(dc == 0), stop=(dc == DC - 1))
                    s_t.inc(i, 1)
                    ev[("d1m", u)] = s_t.n

                @blk.scalar
                def _(e, u=u, j=j, tcq=tcq, bank=bank):
                    kk = u % 16
                    par = u % 2
                    wge(e, s_t, ev[("d1m", u)])
                    if u >= 16 and u % 16 == 0:
                        wge(e, st_gst[0], ev[("gst", u - 2)])
                        wge(e, st_gst[1], ev[("gst", u - 1)])
                    s_s.inc(e.activation(d1st[:, kk * TC:(kk + 1) * TC], pb[bank][:, :],
                                         AF.Identity, bias=b0[:, j:j + 1]), 1)
                    ev[("d1e", u)] = s_s.n
                    wge(e, s_s, ev[("d1e", u)])
                    st_gst[par].inc(e.dma_start(
                        out=g_sc.ap()[j * 128:(j + 1) * 128, tcq * TC:(tcq + 1) * TC],
                        in_=d1st[:, kk * TC:(kk + 1) * TC]), 16)
                    ev[("gst", u)] = st_gst[par].n

        @blk.gpsimd
        def _(e):
            wge(e, st_y16, ev[("syst", NTC - 1)])
            s_cc.inc(e.collective_compute(
                "AllGather", OP.bypass,
                replica_groups=[[0, 1, 2, 3], [4, 5, 6, 7]],
                ins=[y_in.ap().opt()],
                outs=[y_out.ap().opt()],
            ), 1)
            ev["ag2"] = s_cc.n

        for tc in range(NTCQ):
            @blk.sync
            def _(e, tc=tc):
                wge(e, s_t, ev[("d1m", NTC * 16 - 1)])
                if tc > 0:
                    wge(e, s_t, ev[("e_m", tc - 1, NDO - 1)])
                wge(e, st_gst[0], ev[("gst", NTC * 16 - 2)])
                wge(e, st_gst[1], ev[("gst", NTC * 16 - 1)])
                wge(e, st_xn, ev[("Axn", NTCQ - 1, DC - 1)])
                for b8 in range(8):
                    src = g_sc.ap()[b8 * 8 * 128:(b8 + 1) * 8 * 128, tc * TC:(tc + 1) * TC]
                    src = src.rearrange("(a p) b -> p a b", p=128)
                    m_egl.inc(e.dma_start(out=big[:, b8 * 8 * TC:(b8 + 1) * 8 * TC], in_=src), 16)
                ev[("e_gl", tc)] = m_egl.n

            @blk.scalar
            def _(e, tc=tc):
                wge(e, m_egl, ev[("e_gl", tc)])
                for b8 in range(8):
                    s_s.inc(e.activation(big[:, b8 * 8 * TC:(b8 + 1) * 8 * TC],
                                         big[:, b8 * 8 * TC:(b8 + 1) * 8 * TC], AF.Gelu), 1)
                ev[("e_gelu", tc)] = s_s.n

            for do in range(NDO):
                for hh in range(2):
                    @blk.sync
                    def _(e, tc=tc, do=do, hh=hh):
                        key = ("e_mid" if hh == 0 else "e_m")
                        wge(e, s_t, evget((key, tc, do - 1), (key, tc - 1, NDO - 1)))
                        wge(e, qg_w1, ev["w1cast"])
                        base = do * NJ * 128
                        src = w1_sc.ap()[base + hh * 32 * 128: base + (hh + 1) * 32 * 128, :]
                        src = src.rearrange("(a p) b -> p a b", p=128)
                        m_w1b[hh].inc(e.dma_start(out=w1buf[:, hh * 4096:(hh + 1) * 4096], in_=src), 16)
                        ev[("e_wl", tc, do, hh)] = m_w1b[hh].n

                @blk.tensor
                def _(e, tc=tc, do=do):
                    bank = 2 if do % 2 == 0 else 3
                    wge(e, m_egl, ev[("e_gl", tc)])
                    wge(e, s_s, ev[("e_gelu", tc)])
                    wge(e, s_s, evget(("e_ev", tc, do - 2), ("e_ev", tc - 1, NDO - 2 + (do % 2))))
                    wge(e, m_w1b[0], ev[("e_wl", tc, do, 0)])
                    for j in range(NJ):
                        hh = j // 32
                        if j == 32:
                            wge(e, m_w1b[1], ev[("e_wl", tc, do, 1)])
                        i = e.matmul(pb[bank][:, :],
                                     w1buf[:, hh * 4096 + (j - hh * 32) * 128: hh * 4096 + (j - hh * 32 + 1) * 128],
                                     big[:, j * TC:(j + 1) * TC],
                                     start=(j == 0), stop=(j == NJ - 1))
                        if j == 31:
                            s_t.inc(i, 1)
                            ev[("e_mid", tc, do)] = s_t.n
                    s_t.inc(i, 1)
                    ev[("e_m", tc, do)] = s_t.n

                @blk.scalar
                def _(e, tc=tc, do=do):
                    bank = 2 if do % 2 == 0 else 3
                    par = do % 2
                    wge(e, s_t, ev[("e_m", tc, do)])
                    wge(e, st_f[par], evget(("e_fs", tc, do - 2), ("e_fs", tc - 1, NDO - 2 + (do % 2))))
                    s_s.inc(e.activation(fstage[:, par * TC:(par + 1) * TC], pb[bank][:, :],
                                         AF.Identity, bias=b1[:, do:do + 1]), 1)
                    ev[("e_ev", tc, do)] = s_s.n
                    wge(e, s_s, ev[("e_ev", tc, do)])
                    st_f[par].inc(e.dma_start(
                        out=f_sc.ap()[do * 128:(do + 1) * 128, tc * TC:(tc + 1) * TC],
                        in_=fstage[:, par * TC:(par + 1) * TC]), 16)
                    ev[("e_fs", tc, do)] = st_f[par].n

        @blk.sync
        def _(e):
            wge(e, s_cc, ev["ag2"])
            wge(e, s_t, ev[("e_m", NTCQ - 1, NDO - 1)])
            pid = e.partition_id()
            qoff = (pid % 4) * TQ
            for dc in range(DC):
                m_cyl.inc(e.dma_start(
                    out=big[:, dc * 2048:(dc + 1) * 2048],
                    in_=y_out.ap()[dc * 128:(dc + 1) * 128, bass.ds(qoff, TQ)]), 16)
            ev["c_yl"] = m_cyl.n

        def prevC(name, do, tc):
            if tc >= 2:
                return ev.get((name, do, tc - 2), 0)
            if do >= 1:
                return ev.get((name, do - 1, tc + 2), 0)
            return 0

        for do in range(NDO):
            @blk.sync
            def _(e, do=do):
                if do < 2:
                    wge(e, qg_wo, ev["wocast"])
                    wge(e, s_t, ev[("d1m", NTC * 16 - 1)])
                else:
                    wge(e, s_t, ev[("c_m", do - 2, NTCQ - 1)])
                src = wout_sc.ap()[do * DC * 128:(do + 1) * DC * 128, :].rearrange("(a p) b -> p a b", p=128)
                m_w0[do % 2].inc(e.dma_start(out=w016[:, (do % 2) * 2048:(do % 2 + 1) * 2048], in_=src), 16)
                ev[("c_wl", do)] = m_w0[do % 2].n

            for tc in range(NTCQ):
                @blk.tensor
                def _(e, do=do, tc=tc):
                    wge(e, m_w0[do % 2], ev[("c_wl", do)])
                    wge(e, m_cyl, ev["c_yl"])
                    if do == 0:
                        wge(e, s_s, ev[("e_ev", NTCQ - 1, NDO - 1)])
                    else:
                        wge(e, s_s, ev[("c_ev", do - 1, tc)])
                    for dc in range(DC):
                        i = e.matmul(pb[tc][:, :],
                                     w016[:, (do % 2) * 2048 + dc * 128:(do % 2) * 2048 + (dc + 1) * 128],
                                     big[:, dc * 2048 + tc * TC: dc * 2048 + (tc + 1) * TC],
                                     start=(dc == 0), stop=(dc == DC - 1))
                    s_t.inc(i, 1)
                    ev[("c_m", do, tc)] = s_t.n

                @blk.scalar
                def _(e, do=do, tc=tc):
                    par = tc % 2
                    wge(e, s_t, ev[("c_m", do, tc)])
                    wge(e, s_v, prevC("c_v", do, tc))
                    s_s.inc(e.activation(a32[:, 1024 + par * TC: 1024 + (par + 1) * TC], pb[tc][:, :],
                                         AF.Relu, bias=wob[:, do:do + 1]), 1)
                    ev[("c_ev", do, tc)] = s_s.n

                @blk.sync
                def _(e, do=do, tc=tc):
                    par = tc % 2
                    wge(e, s_v, prevC("c_v", do, tc))
                    wge(e, st_f[0], ev[("e_fs", NTCQ - 1, NDO - 2)])
                    wge(e, st_f[1], ev[("e_fs", NTCQ - 1, NDO - 1)])
                    m_cld[par].inc(e.dma_start(out=a32[:, par * TC:(par + 1) * TC],
                                               in_=x_rows[do * 128:(do + 1) * 128, tc * TC:(tc + 1) * TC]), 16)
                    m_cld[par].inc(e.dma_start(out=a32[:, 3584 + 256 * par: 3584 + 256 * (par + 1)].bitcast(F16),
                                               in_=xn_sc.ap()[do * 128:(do + 1) * 128, tc * TC:(tc + 1) * TC]), 16)
                    m_cld[par].inc(e.dma_start(out=a32[:, 3072 + 256 * par: 3072 + 256 * (par + 1)].bitcast(F16),
                                               in_=f_sc.ap()[do * 128:(do + 1) * 128, tc * TC:(tc + 1) * TC]), 16)
                    ev[("c_ld", do, tc)] = m_cld[par].n

                @blk.vector
                def _(e, do=do, tc=tc):
                    par = tc % 2
                    wge(e, s_s, ev[("c_ev", do, tc)])
                    wge(e, m_cld[par], ev[("c_ld", do, tc)])
                    wge(e, st_out[par], prevC("c_st", do, tc))
                    if (do, tc) == (0, 0):
                        wge(e, s_v, ev[("snv", NTC - 1, N16 - 1)])
                    s_v.inc(e.scalar_tensor_tensor(a32[:, 2048 + par * TC: 2048 + (par + 1) * TC],
                                           a32[:, 3584 + 256 * par: 3584 + 256 * (par + 1)].bitcast(F16),
                                           dvec[:, do:do + 1],
                                           a32[:, 1024 + par * TC: 1024 + (par + 1) * TC],
                                           OP.mult, OP.add), 1)
                    wge(e, s_v, s_v.n)
                    s_v.inc(e.tensor_tensor(a32[:, 2048 + par * TC: 2048 + (par + 1) * TC],
                                    a32[:, 2048 + par * TC: 2048 + (par + 1) * TC],
                                    a32[:, par * TC:(par + 1) * TC], OP.add), 1)
                    wge(e, s_v, s_v.n)
                    s_v.inc(e.tensor_tensor(a32[:, 2048 + par * TC: 2048 + (par + 1) * TC],
                                            a32[:, 2048 + par * TC: 2048 + (par + 1) * TC],
                                            a32[:, 3072 + 256 * par: 3072 + 256 * (par + 1)].bitcast(F16), OP.add), 1)
                    ev[("c_v", do, tc)] = s_v.n

                @blk.scalar
                def _(e, do=do, tc=tc):
                    par = tc % 2
                    wge(e, s_v, ev[("c_v", do, tc)])
                    st_out[par].inc(e.dma_start(
                        out=out_d[do * 128:(do + 1) * 128, tc * TC:(tc + 1) * TC],
                        in_=a32[:, 2048 + par * TC: 2048 + (par + 1) * TC]), 16)
                    ev[("c_st", do, tc)] = st_out[par].n

        if DEBUG:
            @blk.sync
            def _(e):
                wge(e, st_out[0], st_out[0].n)
                wge(e, st_out[1], st_out[1].n)
                for dst, src in [(dbg_yin, y_in.ap()), (dbg_yout, y_out.ap()),
                                 (dbg_f, f_sc.ap()), (dbg_pout, p_out.ap()),
                                 (dbg_xn, xn_sc.ap())]:
                    st_dbg.inc(e.dma_start(out=dst, in_=src), 16)
                wge(e, st_dbg, st_dbg.n)

        @blk.sync
        def _(e):
            e.wait_ge(st_out[0].h, st_out[0].n)
            e.wait_ge(st_out[1].h, st_out[1].n)
            e.wait_ge(st_y16.h, st_y16.n)
            e.wait_ge(st_pin.h, st_pin.n)
            e.wait_ge(st_xn.h, st_xn.n)
            e.wait_ge(st_f[0].h, st_f[0].n)
            e.wait_ge(st_f[1].h, st_f[1].n)
            e.wait_ge(st_gst[0].h, st_gst[0].n)
            e.wait_ge(st_gst[1].h, st_gst[1].n)
            e.wait_ge(qg_w0.h, qg_w0.n)
            e.wait_ge(qg_w1.h, qg_w1.n)
            e.wait_ge(qg_wo.h, qg_wo.n)
            e.wait_ge(qg_sb.h, qg_sb.n)

    return nc


_NC_CACHE = {}


def make_maps_inner(x, xpw, dtpw, dt_bias, A_log, Dv, Wout, outb, W0, bias0, W1, bias1):
    C = np.ascontiguousarray
    wx_t = C(xpw.T)
    w0_lay = C(W0.T.reshape(16, 128, 64, 128).transpose(2, 0, 1, 3).reshape(64 * 16 * 128, 128))
    w1_lay = C(W1.T.reshape(64, 128, 16, 128).transpose(2, 0, 1, 3).reshape(16 * 64 * 128, 128))
    wout_lay = C(Wout.T.reshape(16, 128, 16, 128).transpose(2, 0, 1, 3).reshape(16 * 16 * 128, 128))
    b0_l = C(bias0.reshape(64, 128).T)
    b1_l = C(bias1.reshape(16, 128).T)
    d_l = C(Dv.reshape(16, 128).T)
    woutb_l = C(outb.reshape(16, 128).T)
    in_maps = []
    for c in range(8):
        bi, qi = c // 4, c % 4
        sl = slice(qi * DSL, (qi + 1) * DSL)
        in_maps.append({
            "x_rows": C(x[bi, qi * TQ:(qi + 1) * TQ, :].T),
            "x_scan": C(x[bi, :, sl].T),
            "wx_t": wx_t,
            "dtpw_t": C(dtpw[sl, :].T),
            "dt_b": C(dt_bias[sl].reshape(4, 128).T),
            "a_log": C(A_log[sl].reshape(4, 128, 16).transpose(1, 0, 2).reshape(128, 64)),
            "d_vec": d_l,
            "wout_lay": wout_lay,
            "woutb": woutb_l,
            "w0_lay": w0_lay,
            "b0_in": b0_l,
            "w1_lay": w1_lay,
            "b1_in": b1_l,
        })
    return in_maps


def run_full(in_maps):
    if "nc" not in _NC_CACHE:
        _NC_CACHE["nc"] = build_nc()
    nc = _NC_CACHE["nc"]
    return run_bass_kernel_spmd(nc, in_maps, core_ids=list(range(8)))


def kernel(L0_attn_hidden_states, L0_attn_x_proj_weight, L0_attn_dt_proj_weight,
           L0_attn_dt_proj_bias, L0_attn_A_log, L0_attn_D, L0_attn_out_proj_weight,
           L0_attn_out_proj_bias, L0_ffn_W0, L0_ffn_bias0, L0_ffn_W1, L0_ffn_bias1):
    x = np.asarray(L0_attn_hidden_states, dtype=np.float32)
    in_maps = make_maps_inner(
        x,
        np.asarray(L0_attn_x_proj_weight, dtype=np.float32),
        np.asarray(L0_attn_dt_proj_weight, dtype=np.float32),
        np.asarray(L0_attn_dt_proj_bias, dtype=np.float32),
        np.asarray(L0_attn_A_log, dtype=np.float32),
        np.asarray(L0_attn_D, dtype=np.float32),
        np.asarray(L0_attn_out_proj_weight, dtype=np.float32),
        np.asarray(L0_attn_out_proj_bias, dtype=np.float32),
        np.asarray(L0_ffn_W0, dtype=np.float32),
        np.asarray(L0_ffn_bias0, dtype=np.float32),
        np.asarray(L0_ffn_W1, dtype=np.float32),
        np.asarray(L0_ffn_bias1, dtype=np.float32),
    )
    res = run_full(in_maps)
    out = np.empty((2, T, DM), dtype=np.float32)
    for c in range(8):
        bi, qi = c // 4, c % 4
        out[bi, qi * TQ:(qi + 1) * TQ, :] = np.asarray(res.results[c]["out"]).T
    return out


# version 09
# speedup vs baseline: 1.1212x; 1.0367x over previous
import numpy as np
from contextlib import ExitStack

import concourse.bass as bass
import concourse.mybir as mybir
from concourse.bass_utils import run_bass_kernel_spmd

F32 = mybir.dt.float32
F16 = mybir.dt.float16
AF = mybir.ActivationFunctionType
OP = mybir.AluOpType

DM = 2048
T = 8192
TQ = 2048
DSL = 512
TC = 512
DC = 16
DCS = 4
NTCQ = 4
NTC = 16
N16 = 16
NJ = 64
NDO = 16
PR = 164

DEBUG = False
CAST_BARRIER = False


class Sem:
    def __init__(self, nc, stack, name):
        self.h = stack.enter_context(nc.semaphore(name))
        self.n = 0

    def inc(self, instr, k):
        instr.then_inc(self.h, k)
        self.n += k
        return self.n


def build_nc():
    nc = bass.Bass("TRN2", target_bir_lowering=False)

    x_rows = nc.dram_tensor("x_rows", [DM, TQ], F32, kind="ExternalInput").ap()
    x_scan = nc.dram_tensor("x_scan", [DSL, T], F32, kind="ExternalInput").ap()
    wx_t = nc.dram_tensor("wx_t", [DM, 160], F32, kind="ExternalInput").ap()
    dtpw_t = nc.dram_tensor("dtpw_t", [128, DSL], F32, kind="ExternalInput").ap()
    dt_b = nc.dram_tensor("dt_b", [128, 4], F32, kind="ExternalInput").ap()
    a_log = nc.dram_tensor("a_log", [128, 64], F32, kind="ExternalInput").ap()
    d_vec = nc.dram_tensor("d_vec", [128, 16], F32, kind="ExternalInput").ap()
    wout_lay = nc.dram_tensor("wout_lay", [NDO * DC * 128, 128], F32, kind="ExternalInput").ap()
    woutb = nc.dram_tensor("woutb", [128, 16], F32, kind="ExternalInput").ap()
    w0_lay = nc.dram_tensor("w0_lay", [NJ * DC * 128, 128], F32, kind="ExternalInput").ap()
    b0_in = nc.dram_tensor("b0_in", [128, 64], F32, kind="ExternalInput").ap()
    w1_lay = nc.dram_tensor("w1_lay", [NDO * NJ * 128, 128], F32, kind="ExternalInput").ap()
    b1_in = nc.dram_tensor("b1_in", [128, 16], F32, kind="ExternalInput").ap()

    out_d = nc.dram_tensor("out", [DM, TQ], F32, kind="ExternalOutput").ap()
    if DEBUG:
        dbg_yin = nc.dram_tensor("dbg_yin", [DSL, T], F16, kind="ExternalOutput").ap()
        dbg_yout = nc.dram_tensor("dbg_yout", [DM, T], F16, kind="ExternalOutput").ap()
        dbg_f = nc.dram_tensor("dbg_f", [DM, TQ], F16, kind="ExternalOutput").ap()
        dbg_pout = nc.dram_tensor("dbg_pout", [4 * PR, TQ], F16, kind="ExternalOutput").ap()
        dbg_xn = nc.dram_tensor("dbg_xn", [DM, TQ], F16, kind="ExternalOutput").ap()

    w0_sc = nc.dram_tensor("w0_sc", [NJ * DC * 128, 128], F16)
    w1_sc = nc.dram_tensor("w1_sc", [NDO * NJ * 128, 128], F16)
    wout_sc = nc.dram_tensor("wout_sc", [NDO * DC * 128, 128], F16)
    xn_sc = nc.dram_tensor("xn_sc", [DM, TQ], F16)
    g_sc = nc.dram_tensor("g_sc", [NJ * 128, TQ], F16)
    f_sc = nc.dram_tensor("f_sc", [DM, TQ], F16)
    p_in = nc.dram_tensor("p_in", [PR, TQ], F16)
    p_out = nc.dram_tensor("p_out", [4 * PR, TQ], F16)
    y_in = nc.dram_tensor("y_in", [DSL, T], F16)
    y_out = nc.dram_tensor("y_out", [DM, T], F16)

    with ExitStack() as stack:
        ec = stack.enter_context

        wx16 = ec(nc.sbuf_tensor("wx16", [128, DC * 160], F16))
        dtpw16 = ec(nc.sbuf_tensor("dtpw16", [128, DSL], F16))
        aneg = ec(nc.sbuf_tensor("aneg", [128, 64], F32))
        dtb = ec(nc.sbuf_tensor("dtb", [128, 4], F32))
        dvec = ec(nc.sbuf_tensor("dvec", [128, 16], F32))
        wob = ec(nc.sbuf_tensor("wob", [128, 16], F32))
        b0 = ec(nc.sbuf_tensor("b0", [128, 64], F32))
        b1 = ec(nc.sbuf_tensor("b1", [128, 16], F32))
        onesc = ec(nc.sbuf_tensor("onesc", [128, 1], F16))
        onesr = ec(nc.sbuf_tensor("onesr", [1, 128], F16))
        big = ec(nc.sbuf_tensor("big", [128, 32768], F16))
        w016 = ec(nc.sbuf_tensor("w016", [128, 2 * 2048], F16))
        w1buf = ec(nc.sbuf_tensor("w1buf", [128, 2 * 4096], F16))
        fstage = ec(nc.sbuf_tensor("fstage", [128, 2 * TC], F16))
        d1st = ec(nc.sbuf_tensor("d1st", [128, 16 * TC], F16))
        mis1 = ec(nc.sbuf_tensor("mis1", [128, 64], F32))
        dtl2 = ec(nc.sbuf_tensor("dtl2", [128, 2 * TC], F16))
        delta16 = ec(nc.sbuf_tensor("delta16", [128, DCS * TC], F16))
        dx16 = ec(nc.sbuf_tensor("dx16", [128, DCS * TC], F16))
        xns16 = ec(nc.sbuf_tensor("xns16", [128, TC], F16))
        xsc32 = ec(nc.sbuf_tensor("xsc32", [128, 2 * TC], F32))
        murs = ec(nc.sbuf_tensor("murs", [128, 2 * TC], F16))
        rsrs = ec(nc.sbuf_tensor("rsrs", [128, 2 * TC], F16))
        brep = ec(nc.sbuf_tensor("brep", [128, 2 * TC], F16))
        crep = ec(nc.sbuf_tensor("crep", [128, 2 * TC], F16))
        a32 = ec(nc.sbuf_tensor("a32", [128, 8 * TC], F32))
        u4 = ec(nc.sbuf_tensor("u4", [128, DCS * TC], F16))
        h16 = ec(nc.sbuf_tensor("h16", [128, DCS * TC], F16))
        tmp16 = ec(nc.sbuf_tensor("tmp16", [128, DCS * TC], F16))
        yacc = ec(nc.sbuf_tensor("yacc", [128, DCS * TC], F32))
        y16s = ec(nc.sbuf_tensor("y16s", [128, DCS * TC], F16))
        hstate = ec(nc.sbuf_tensor("hstate", [128, 64], F32))
        sexp32 = ec(nc.sbuf_tensor("sexp32", [128, TC], F16))
        st_a = ec(nc.sbuf_tensor("st_a", [1, TC], F32))
        st_b = ec(nc.sbuf_tensor("st_b", [1, TC], F32))
        st_c = ec(nc.sbuf_tensor("st_c", [1, TC], F32))
        st_d = ec(nc.sbuf_tensor("st_d", [1, TC], F32))
        mu16 = ec(nc.sbuf_tensor("mu16", [1, TC], F16))
        rstd16 = ec(nc.sbuf_tensor("rstd16", [1, TC], F16))
        dtlst = ec(nc.sbuf_tensor("dtlst", [128, TC], F16))
        bcst = ec(nc.sbuf_tensor("bcst", [32, TC], F16))
        xn0 = ec(nc.sbuf_tensor("xn0", [128, TC], F32))

        pb = [ec(nc.psum_tensor(f"pb{i}", [128, 512], F32)) for i in range(8)]

        s_t = Sem(nc, stack, "s_t")
        s_s = Sem(nc, stack, "s_s")
        s_v = Sem(nc, stack, "s_v")
        s_g = Sem(nc, stack, "s_g")
        s_cc = Sem(nc, stack, "s_cc")
        m_cst = Sem(nc, stack, "m_cst")
        m_xsc = [Sem(nc, stack, f"m_xsc{i}") for i in range(2)]
        m_mur = [Sem(nc, stack, f"m_mur{i}") for i in range(2)]
        m_bc = [Sem(nc, stack, f"m_bc{i}") for i in range(2)]
        m_dtl = [Sem(nc, stack, f"m_dtl{i}") for i in range(2)]
        m_w0 = [Sem(nc, stack, f"m_w0{i}") for i in range(2)]
        qg_w0 = Sem(nc, stack, "qg_w0")
        qg_w1 = Sem(nc, stack, "qg_w1")
        qg_wo = Sem(nc, stack, "qg_wo")
        qg_sb = Sem(nc, stack, "qg_sb")
        m_w1b = [Sem(nc, stack, f"m_w1b{i}") for i in range(2)]
        m_egl = Sem(nc, stack, "m_egl")
        m_cyl = Sem(nc, stack, "m_cyl")
        m_cld = [Sem(nc, stack, f"m_cld{i}") for i in range(2)]
        st_pin = Sem(nc, stack, "st_pin")
        st_xn = Sem(nc, stack, "st_xn")
        st_gst = [Sem(nc, stack, f"st_gst{i}") for i in range(2)]
        st_y16 = Sem(nc, stack, "st_y16")
        st_f = [Sem(nc, stack, f"st_f{i}") for i in range(2)]
        st_out = [Sem(nc, stack, f"st_out{i}") for i in range(2)]
        q_g = Sem(nc, stack, "q_g")
        st_dbg = Sem(nc, stack, "st_dbg")

        blk = ec(nc.Block())
        ev = {}

        def wge(eng, sem, val):
            assert val <= sem.n, f"wait {val} > emitted {sem.n}"
            if val > 0:
                eng.wait_ge(sem.h, val)

        def evget(*keys):
            for k in keys:
                if k in ev:
                    return ev[k]
            return 0

        @blk.sync
        def _(e):
            for dst, src in [(dtb, dt_b), (dvec, d_vec), (wob, woutb), (b0, b0_in), (b1, b1_in)]:
                m_cst.inc(e.dma_start(out=dst[:, :], in_=src), 16)
            m_cst.inc(e.dma_start(out=mis1[:, 0:64], in_=a_log), 16)
            ev["c_l1"] = m_cst.n

        @blk.scalar
        def _(e):
            wge(e, m_cst, ev["c_l1"])
            s_s.inc(e.activation(aneg[:, :], mis1[:, 0:64], AF.Exp), 1)
            ev["c_exp"] = s_s.n

        @blk.vector
        def _(e):
            wge(e, s_s, ev["c_exp"])
            s_v.inc(e.tensor_scalar(aneg[:, :], aneg[:, :], -1.0, None, OP.mult), 1)
            ev["c_aneg"] = s_v.n

        @blk.gpsimd
        def _(e):
            s_g.inc(e.memset(onesc[:, :], 1.0), 1)
            s_g.inc(e.memset(onesr[:, :], 1.0), 1)
            qg_sb.inc(e.dma_start(out=dtpw16[:, :], in_=dtpw_t), 16)
            qg_sb.inc(e.dma_start(out=wx16[:, :], in_=wx_t.rearrange("(c p) n -> p c n", p=128)), 16)
            ev["sbcast"] = qg_sb.n
            NB0 = NJ * DC * 128 // 16
            for i in range(16):
                if i >= 2 and i % 2 == 0:
                    e.wait_ge(qg_w0.h, 16 * i)
                qg_w0.inc(e.dma_start(out=w0_sc.ap()[i * NB0:(i + 1) * NB0, :],
                                      in_=w0_lay[i * NB0:(i + 1) * NB0, :]), 16)
            ev["w0cast"] = qg_w0.n
            NBO = NDO * DC * 128 // 4
            for i in range(4):
                if i >= 2 and i % 2 == 0:
                    e.wait_ge(qg_wo.h, 16 * i)
                qg_wo.inc(e.dma_start(out=wout_sc.ap()[i * NBO:(i + 1) * NBO, :],
                                      in_=wout_lay[i * NBO:(i + 1) * NBO, :]), 16)
            ev["wocast"] = qg_wo.n
            NB1 = NDO * NJ * 128 // 16
            for i in range(16):
                if i >= 2 and i % 2 == 0:
                    e.wait_ge(qg_w1.h, 16 * i)
                qg_w1.inc(e.dma_start(out=w1_sc.ap()[i * NB1:(i + 1) * NB1, :],
                                      in_=w1_lay[i * NB1:(i + 1) * NB1, :]), 16)
            ev["w1cast"] = qg_w1.n
        ev["consts_g"] = s_g.n

        if CAST_BARRIER:
            @blk.sync
            def _(e):
                wge(e, qg_w0, ev["w0cast"])
                wge(e, qg_w1, ev["w1cast"])
                wge(e, qg_wo, ev["wocast"])
                wge(e, qg_sb, ev["sbcast"])

        for tc in range(NTCQ):
            for dc in range(DC):
                par = dc % 2

                @blk.sync
                def _(e, tc=tc, dc=dc, par=par):
                    if dc >= 2:
                        wge(e, s_v, ev[("A1v", tc, dc - 2)])
                        wge(e, s_s, ev[("A1s", tc, dc - 2)])
                    elif tc > 0:
                        wge(e, s_v, ev[("A2v_last", tc - 1)])
                    m_xsc[par].inc(e.dma_start(
                        out=xsc32[:, par * TC:(par + 1) * TC],
                        in_=x_rows[dc * 128:(dc + 1) * 128, tc * TC:(tc + 1) * TC]), 16)
                    ev[("A1l", tc, dc)] = m_xsc[par].n

                @blk.vector
                def _(e, tc=tc, dc=dc, par=par):
                    wge(e, m_xsc[par], ev[("A1l", tc, dc)])
                    if dc >= 2:
                        wge(e, s_t, ev[("Astm", tc, dc - 2)])
                    elif tc > 0:
                        wge(e, s_t, ev[("Astatm", tc - 1)])
                    s_v.inc(e.tensor_copy(tmp16[:, par * TC:(par + 1) * TC],
                                          xsc32[:, par * TC:(par + 1) * TC]), 1)
                    ev[("A1v", tc, dc)] = s_v.n

                @blk.scalar
                def _(e, tc=tc, dc=dc, par=par):
                    wge(e, m_xsc[par], ev[("A1l", tc, dc)])
                    if dc >= 2:
                        wge(e, s_t, ev[("Astm", tc, dc - 2)])
                    elif tc > 0:
                        wge(e, s_t, ev[("Astatm", tc - 1)])
                    s_s.inc(e.activation(tmp16[:, (2 + par) * TC:(3 + par) * TC],
                                         xsc32[:, par * TC:(par + 1) * TC], AF.Square), 1)
                    ev[("A1s", tc, dc)] = s_s.n

                @blk.tensor
                def _(e, tc=tc, dc=dc, par=par):
                    wge(e, s_v, ev[("A1v", tc, dc)])
                    wge(e, s_s, ev[("A1s", tc, dc)])
                    if dc == 0:
                        wge(e, s_g, ev["consts_g"])
                        wge(e, qg_sb, ev["sbcast"])
                        if tc > 0:
                            wge(e, s_v, ev[("Astat", tc - 1)])
                    e.matmul(pb[0][0:1, :], onesc[:, :], tmp16[:, par * TC:(par + 1) * TC],
                             start=(dc == 0), stop=(dc == DC - 1), skip_group_check=True)
                    i2 = e.matmul(pb[0][32:33, :], onesc[:, :], tmp16[:, (2 + par) * TC:(3 + par) * TC],
                                  start=(dc == 0), stop=(dc == DC - 1), skip_group_check=True)
                    s_t.inc(i2, 1)
                    ev[("Astm", tc, dc)] = s_t.n
                    if dc == DC - 1:
                        ev[("Astatm", tc)] = s_t.n

            ev[("A1v_last", tc)] = ev[("A1v", tc, DC - 1)]
            ev[("A1s_last", tc)] = ev[("A1s", tc, DC - 1)]

            @blk.vector
            def _(e, tc=tc):
                wge(e, s_t, ev[("Astatm", tc)])
                if tc > 0:
                    wge(e, st_pin, ev[("Apst", tc - 1)])
                    wge(e, s_v, ev[("Astat", tc - 1)])
                s_v.inc(e.tensor_scalar(st_a[:, :], pb[0][0:1, :], 1.0 / DM, None, OP.mult), 1)
                s_v.inc(e.tensor_scalar(st_b[:, :], pb[0][32:33, :], 1.0 / DM, None, OP.mult), 1)
                wge(e, s_v, s_v.n)
                s_v.inc(e.tensor_tensor(st_c[:, :], st_a[:, :], st_a[:, :], OP.mult), 1)
                wge(e, s_v, s_v.n)
                s_v.inc(e.tensor_tensor(st_d[:, :], st_b[:, :], st_c[:, :], OP.subtract), 1)
                wge(e, s_v, s_v.n)
                s_v.inc(e.tensor_scalar(st_d[:, :], st_d[:, :], 1e-5, None, OP.add), 1)
                wge(e, s_v, s_v.n)
                s_v.inc(e.reciprocal(st_c[:, :], st_d[:, :]), 1)
                wge(e, s_v, s_v.n)
                s_v.inc(e.tensor_copy(mu16[:, :], st_a[:, :]), 1)
                ev[("Astat", tc)] = s_v.n

            @blk.scalar
            def _(e, tc=tc):
                wge(e, s_v, ev[("Astat", tc)])
                if tc > 0:
                    wge(e, st_pin, ev[("Apst", tc - 1)])
                s_s.inc(e.activation(rstd16[:, :], st_c[:, :], AF.Sqrt), 1)
                ev[("Arstd", tc)] = s_s.n

            @blk.tensor
            def _(e, tc=tc):
                wge(e, s_s, ev[("Arstd", tc)])
                if tc > 0:
                    wge(e, s_v, ev[("A2v_last", tc - 1)])
                e.matmul(pb[1][:, :], onesr[:, :], mu16[:, :], start=True, stop=True)
                i2 = e.matmul(pb[2][:, :], onesr[:, :], rstd16[:, :], start=True, stop=True)
                s_t.inc(i2, 1)
                ev[("Arep", tc)] = s_t.n

            for dc in range(DC):
                par = dc % 2

                @blk.sync
                def _(e, tc=tc, dc=dc, par=par):
                    if dc >= 2:
                        wge(e, s_v, ev[("A2v", tc, dc - 2)])
                    else:
                        wge(e, s_v, ev[("A1v_last", tc)])
                        wge(e, s_s, ev[("A1s_last", tc)])
                    m_xsc[par].inc(e.dma_start(
                        out=xsc32[:, par * TC:(par + 1) * TC],
                        in_=x_rows[dc * 128:(dc + 1) * 128, tc * TC:(tc + 1) * TC]), 16)
                    ev[("A2l", tc, dc)] = m_xsc[par].n

                @blk.vector
                def _(e, tc=tc, dc=dc, par=par):
                    wge(e, m_xsc[par], ev[("A2l", tc, dc)])
                    wge(e, s_t, ev[("Arep", tc)])
                    if dc > 0:
                        wge(e, s_v, ev[("A2v", tc, dc - 1)])
                    s_v.inc(e.tensor_tensor(xn0[:, :], xsc32[:, par * TC:(par + 1) * TC],
                                            pb[1][:, :], OP.subtract), 1)
                    wge(e, s_v, s_v.n)
                    s_v.inc(e.tensor_tensor(
                        big[:, dc * 2048 + tc * TC: dc * 2048 + (tc + 1) * TC],
                        xn0[:, :], pb[2][:, :], OP.mult), 1)
                    ev[("A2v", tc, dc)] = s_v.n

                @blk.scalar
                def _(e, tc=tc, dc=dc):
                    wge(e, s_v, ev[("A2v", tc, dc)])
                    st_xn.inc(e.dma_start(
                        out=xn_sc.ap()[dc * 128:(dc + 1) * 128, tc * TC:(tc + 1) * TC],
                        in_=big[:, dc * 2048 + tc * TC: dc * 2048 + (tc + 1) * TC]), 16)
                    ev[("Axn", tc, dc)] = st_xn.n

                @blk.tensor
                def _(e, tc=tc, dc=dc):
                    wge(e, s_v, ev[("A2v", tc, dc)])
                    if dc == 0 and tc > 0:
                        wge(e, s_s, ev[("Apev", tc - 1)])
                    xn_t = big[:, dc * 2048 + tc * TC: dc * 2048 + (tc + 1) * TC]
                    e.matmul(pb[3][:, :], wx16[:, dc * 160: dc * 160 + 128], xn_t,
                             start=(dc == 0), stop=(dc == DC - 1), skip_group_check=True)
                    i2 = e.matmul(pb[4][0:32, :], wx16[:, dc * 160 + 128: (dc + 1) * 160], xn_t,
                                  start=(dc == 0), stop=(dc == DC - 1), skip_group_check=True)
                    if dc == DC - 1:
                        s_t.inc(i2, 1)
                        ev[("Apm", tc)] = s_t.n
            ev[("A2v_last", tc)] = s_v.n

            @blk.scalar
            def _(e, tc=tc):
                wge(e, s_t, ev[("Apm", tc)])
                wge(e, s_v, ev[("Astat", tc)])
                wge(e, s_s, ev[("Arstd", tc)])
                if tc > 0:
                    wge(e, st_pin, ev[("Apst", tc - 1)])
                e.activation(dtlst[:, :], pb[3][:, :], AF.Identity)
                s_s.inc(e.activation(bcst[:, :], pb[4][0:32, :], AF.Identity), 1)
                ev[("Apev", tc)] = s_s.n
                wge(e, s_s, ev[("Apev", tc)])
                st_pin.inc(e.dma_start(out=p_in.ap()[0:128, tc * TC:(tc + 1) * TC], in_=dtlst[:, :]), 16)
                st_pin.inc(e.dma_start(out=p_in.ap()[128:160, tc * TC:(tc + 1) * TC], in_=bcst[:, :]), 16)
                st_pin.inc(e.dma_start(out=p_in.ap()[160:161, tc * TC:(tc + 1) * TC], in_=mu16[:, :]), 16)
                st_pin.inc(e.dma_start(out=p_in.ap()[161:162, tc * TC:(tc + 1) * TC], in_=rstd16[:, :]), 16)
                st_pin.inc(e.dma_start(out=p_in.ap()[162:163, tc * TC:(tc + 1) * TC], in_=rstd16[:, :]), 16)
                st_pin.inc(e.dma_start(out=p_in.ap()[163:164, tc * TC:(tc + 1) * TC], in_=rstd16[:, :]), 16)
                ev[("Apst", tc)] = st_pin.n

        @blk.gpsimd
        def _(e):
            wge(e, st_pin, ev[("Apst", NTCQ - 1)])
            s_cc.inc(e.collective_compute(
                "AllGather", OP.bypass,
                replica_groups=[[0, 1, 2, 3], [4, 5, 6, 7]],
                ins=[p_in.ap().opt()],
                outs=[p_out.ap().opt()],
            ), 1)
            ev["ag1"] = s_cc.n

        d1banks = [0, 1, 2, 3, 5, 6]

        for tc in range(NTC):
            rnk = tc // 4
            tl = (tc % 4) * TC
            tpar = tc % 2

            @blk.sync
            def _(e, tc=tc, rnk=rnk, tl=tl, tpar=tpar):
                wge(e, s_cc, ev["ag1"])
                if tc >= 2:
                    wge(e, s_v, ev[("sdxv", tc - 2, DCS - 1)])
                m_mur[tpar].inc(e.dma_start(
                    out=murs[:, tpar * TC:(tpar + 1) * TC],
                    in_=p_out.ap()[rnk * PR + 160: rnk * PR + 161, tl:tl + TC].broadcast_to([128, TC])), 16)
                m_mur[tpar].inc(e.dma_start(
                    out=rsrs[:, tpar * TC:(tpar + 1) * TC],
                    in_=p_out.ap()[rnk * PR + 161: rnk * PR + 162, tl:tl + TC].broadcast_to([128, TC])), 16)
                ev[("mul", tc)] = m_mur[tpar].n
                if tc >= 2:
                    wge(e, s_t, ev[("sdm", tc - 2, DCS - 1)])
                m_dtl[tpar].inc(e.dma_start(
                    out=dtl2[:, tpar * TC:(tpar + 1) * TC],
                    in_=p_out.ap()[rnk * PR: rnk * PR + 128, tl:tl + TC]), 16)
                ev[("dtll", tc)] = m_dtl[tpar].n
                for dcs in range(2):
                    if tc == 0:
                        wge(e, s_v, ev[("A2v_last", NTCQ - 1)])
                    else:
                        wge(e, s_v, ev[("sdxv", tc - 1, dcs + 2)])
                    m_xsc[dcs].inc(e.dma_start(
                        out=xsc32[:, dcs * TC:(dcs + 1) * TC],
                        in_=x_scan[dcs * 128:(dcs + 1) * 128, tc * TC:(tc + 1) * TC]), 16)
                    ev[("sxl", tc, dcs)] = m_xsc[dcs].n
                for n in (0, 1):
                    if tc > 0:
                        wge(e, s_v, ev[("snv", tc - 1, N16 - 2 + n)])
                    m_bc[n].inc(e.dma_start(
                        out=brep[:, n * TC:(n + 1) * TC],
                        in_=p_out.ap()[rnk * PR + 128 + n: rnk * PR + 129 + n, tl:tl + TC].broadcast_to([128, TC])), 16)
                    m_bc[n].inc(e.dma_start(
                        out=crep[:, n * TC:(n + 1) * TC],
                        in_=p_out.ap()[rnk * PR + 144 + n: rnk * PR + 145 + n, tl:tl + TC].broadcast_to([128, TC])), 16)
                    ev[("sbl", tc, n)] = m_bc[n].n

            for dcs in range(DCS):
                @blk.tensor
                def _(e, tc=tc, dcs=dcs, tpar=tpar):
                    bank = 4 if dcs % 2 == 0 else 7
                    wge(e, m_dtl[tpar], ev[("dtll", tc)])
                    wge(e, s_s, evget(("sdev", tc, dcs - 2), ("sdev", tc - 1, dcs + 2)))
                    if (tc, dcs) == (0, 0):
                        wge(e, qg_sb, ev["sbcast"])
                        wge(e, s_s, ev[("Apev", NTCQ - 1)])
                    i = e.matmul(pb[bank][:, :], dtpw16[:, dcs * 128:(dcs + 1) * 128],
                                 dtl2[:, tpar * TC:(tpar + 1) * TC], start=True, stop=True)
                    s_t.inc(i, 1)
                    ev[("sdm", tc, dcs)] = s_t.n

                @blk.scalar
                def _(e, tc=tc, dcs=dcs):
                    bank = 4 if dcs % 2 == 0 else 7
                    wge(e, s_t, ev[("sdm", tc, dcs)])
                    wge(e, s_s, evget(("sdev", tc, dcs - 1), ("sdev", tc - 1, DCS - 1)))
                    if tc > 0:
                        wge(e, s_v, ev[("sdxv", tc - 1, dcs)])
                        wge(e, s_s, ev[("sexp", tc - 1, N16 - 1, dcs)])
                    s_s.inc(e.activation(sexp32[:, :], pb[bank][:, :], AF.Exp, bias=dtb[:, dcs:dcs + 1]), 1)
                    wge(e, s_s, s_s.n)
                    s_s.inc(e.activation(delta16[:, dcs * TC:(dcs + 1) * TC], sexp32[:, :],
                                         AF.Ln, bias=1.0), 1)
                    ev[("sdev", tc, dcs)] = s_s.n

            for dcs in range(DCS):
                @blk.vector
                def _(e, tc=tc, dcs=dcs, tpar=tpar):
                    pxr = dcs % 2
                    wge(e, m_xsc[pxr], ev[("sxl", tc, dcs)])
                    wge(e, m_mur[tpar], ev[("mul", tc)])
                    wge(e, s_s, ev[("sdev", tc, dcs)])
                    if dcs > 0:
                        wge(e, s_v, ev[("sdxv", tc, dcs - 1)])
                    s_v.inc(e.tensor_tensor(xn0[:, :], xsc32[:, pxr * TC:(pxr + 1) * TC],
                                            murs[:, tpar * TC:(tpar + 1) * TC], OP.subtract), 1)
                    wge(e, s_v, s_v.n)
                    s_v.inc(e.tensor_tensor(xns16[:, :], xn0[:, :],
                                            rsrs[:, tpar * TC:(tpar + 1) * TC], OP.mult), 1)
                    wge(e, s_v, s_v.n)
                    s_v.inc(e.tensor_tensor(dx16[:, dcs * TC:(dcs + 1) * TC],
                                            delta16[:, dcs * TC:(dcs + 1) * TC],
                                            xns16[:, :], OP.mult), 1)
                    ev[("sdxv", tc, dcs)] = s_v.n

                if dcs < 2:
                    @blk.sync
                    def _(e, tc=tc, dcs=dcs):
                        m = dcs + 2
                        wge(e, s_v, ev[("sdxv", tc, dcs)])
                        m_xsc[m % 2].inc(e.dma_start(
                            out=xsc32[:, (m % 2) * TC:(m % 2 + 1) * TC],
                            in_=x_scan[m * 128:(m + 1) * 128, tc * TC:(tc + 1) * TC]), 16)
                        ev[("sxl", tc, m)] = m_xsc[m % 2].n

            for n in (0, 1):
                for dcs in range(DCS):
                    @blk.scalar
                    def _(e, tc=tc, n=n, dcs=dcs):
                        slot = (n % 2) * 4 + dcs
                        wge(e, s_v, ev["c_aneg"])
                        wge(e, s_s, ev[("sdev", tc, dcs)])
                        if tc > 0:
                            wge(e, s_v, ev[("sscan", tc - 1, N16 - 2 + n, dcs)])
                        s_s.inc(e.activation(a32[:, slot * TC:(slot + 1) * TC],
                                             delta16[:, dcs * TC:(dcs + 1) * TC],
                                             AF.Exp, scale=aneg[:, dcs * 16 + n:dcs * 16 + n + 1]), 1)
                        ev[("sexp", tc, n, dcs)] = s_s.n

            for n in range(N16):
                np_ = n % 2

                @blk.vector
                def _(e, tc=tc, n=n, np_=np_):
                    wge(e, m_bc[np_], ev[("sbl", tc, n)])
                    wge(e, s_v, ev[("sdxv", tc, DCS - 1)])
                    wge(e, s_g, evget(("shc", tc, n - 1), ("shc", tc - 1, N16 - 1)))
                    if tc > 0:
                        wge(e, s_g, ev[("shc", tc - 1, n)])
                    wge(e, s_s, ev[("sexp", tc, n, DCS - 1)])
                    wge(e, s_v, evget(("sscan", tc, n - 1, DCS - 1),
                                      ("sscan", tc - 1, N16 - 1, DCS - 1)))
                    for dcs in range(DCS):
                        s_v.inc(e.tensor_tensor(u4[:, dcs * TC:(dcs + 1) * TC],
                                                dx16[:, dcs * TC:(dcs + 1) * TC],
                                                brep[:, np_ * TC:(np_ + 1) * TC], OP.mult), 1)
                    ev[("su4", tc, n)] = s_v.n
                    wge(e, s_v, ev[("su4", tc, n)])
                    wge(e, s_v, evget(("sym", tc, n - 1, DCS - 1),
                                      ("sym", tc - 1, N16 - 1, DCS - 1)))
                    wge(e, s_g, evget(("sya", tc, n - 1, DCS - 1),
                                      ("sya", tc - 1, N16 - 1, DCS - 1)))
                    if n == 0 and tc > 0:
                        wge(e, s_v, ev[("syc", tc - 1)])
                    if tc == 0 and n == 1:
                        wge(e, s_t, ev[("Astatm", NTCQ - 1)])
                    for dcs in range(DCS):
                        slot = np_ * 4 + dcs
                        init = 0.0 if tc == 0 else hstate[:, dcs * 16 + n:dcs * 16 + n + 1]
                        i = e.tensor_tensor_scan(h16[:, dcs * TC:(dcs + 1) * TC],
                                                 a32[:, slot * TC:(slot + 1) * TC],
                                                 u4[:, dcs * TC:(dcs + 1) * TC], init, OP.mult, OP.add)
                        s_v.inc(i, 1)
                        ev[("sscan", tc, n, dcs)] = s_v.n
                    wge(e, s_v, ev[("sscan", tc, n, DCS - 1)])
                    for dcs in range(DCS):
                        if n == 0:
                            i = e.tensor_tensor(yacc[:, dcs * TC:(dcs + 1) * TC],
                                                h16[:, dcs * TC:(dcs + 1) * TC],
                                                crep[:, np_ * TC:(np_ + 1) * TC], OP.mult)
                        else:
                            i = e.tensor_tensor(tmp16[:, dcs * TC:(dcs + 1) * TC],
                                                h16[:, dcs * TC:(dcs + 1) * TC],
                                                crep[:, np_ * TC:(np_ + 1) * TC], OP.mult)
                        s_v.inc(i, 1)
                        ev[("sym", tc, n, dcs)] = s_v.n
                    ev[("snv", tc, n)] = s_v.n

                @blk.gpsimd
                def _(e, tc=tc, n=n):
                    wge(e, s_v, ev[("sscan", tc, n, DCS - 1)])
                    s_g.inc(e.tensor_copy(hstate[:, n:64:16], h16[:, TC - 1: DCS * TC: TC]), 1)
                    ev[("shc", tc, n)] = s_g.n
                    if n > 0:
                        for dcs in range(DCS):
                            wge(e, s_v, ev[("sym", tc, n, dcs)])
                            wge(e, s_g, evget(("sya", tc, n - 1, dcs)))
                            s_g.inc(e.tensor_tensor(yacc[:, dcs * TC:(dcs + 1) * TC],
                                                    yacc[:, dcs * TC:(dcs + 1) * TC],
                                                    tmp16[:, dcs * TC:(dcs + 1) * TC], OP.add), 1)
                            ev[("sya", tc, n, dcs)] = s_g.n

                if n + 2 < N16:
                    @blk.sync
                    def _(e, tc=tc, n=n, rnk=rnk, tl=tl):
                        m = n + 2
                        wge(e, s_v, ev[("snv", tc, n)])
                        m_bc[m % 2].inc(e.dma_start(
                            out=brep[:, (m % 2) * TC:(m % 2 + 1) * TC],
                            in_=p_out.ap()[rnk * PR + 128 + m: rnk * PR + 129 + m, tl:tl + TC].broadcast_to([128, TC])), 16)
                        m_bc[m % 2].inc(e.dma_start(
                            out=crep[:, (m % 2) * TC:(m % 2 + 1) * TC],
                            in_=p_out.ap()[rnk * PR + 144 + m: rnk * PR + 145 + m, tl:tl + TC].broadcast_to([128, TC])), 16)
                        ev[("sbl", tc, m)] = m_bc[m % 2].n

                    for dcs in range(DCS):
                        @blk.scalar
                        def _(e, tc=tc, n=n, dcs=dcs):
                            m = n + 2
                            wge(e, s_v, ev[("sscan", tc, n, dcs)])
                            slot = (m % 2) * 4 + dcs
                            s_s.inc(e.activation(a32[:, slot * TC:(slot + 1) * TC],
                                                 delta16[:, dcs * TC:(dcs + 1) * TC],
                                                 AF.Exp, scale=aneg[:, dcs * 16 + m:dcs * 16 + m + 1]), 1)
                            ev[("sexp", tc, m, dcs)] = s_s.n

            @blk.vector
            def _(e, tc=tc):
                if tc >= 1:
                    wge(e, st_y16, ev[("syst", tc - 1)])
                for dcs in range(DCS):
                    wge(e, s_g, ev[("sya", tc, N16 - 1, dcs)])
                    s_v.inc(e.tensor_copy(y16s[:, dcs * TC:(dcs + 1) * TC],
                                          yacc[:, dcs * TC:(dcs + 1) * TC]), 1)
                ev[("syc", tc)] = s_v.n

            @blk.scalar
            def _(e, tc=tc):
                wge(e, s_v, ev[("syc", tc)])
                for dcs in range(DCS):
                    st_y16.inc(e.dma_start(
                        out=y_in.ap()[dcs * 128:(dcs + 1) * 128, tc * TC:(tc + 1) * TC],
                        in_=y16s[:, dcs * TC:(dcs + 1) * TC]), 16)
                ev[("syst", tc)] = st_y16.n

            for k in range(16):
                u = tc * 16 + k
                j, tcq = u // 4, u % 4
                bank = d1banks[u % 6]

                if tcq == 0:
                    @blk.sync
                    def _(e, j=j):
                        if j < 2:
                            wge(e, qg_w0, ev["w0cast"])
                        else:
                            wge(e, s_t, ev[("d1m", (j - 2) * 4 + 3)])
                        src = w0_sc.ap()[j * DC * 128:(j + 1) * DC * 128, :].rearrange("(a p) b -> p a b", p=128)
                        m_w0[j % 2].inc(e.dma_start(out=w016[:, (j % 2) * 2048:(j % 2 + 1) * 2048], in_=src), 16)
                        ev[("w0l", j)] = m_w0[j % 2].n

                @blk.tensor
                def _(e, u=u, j=j, tcq=tcq, bank=bank):
                    wge(e, m_w0[j % 2], ev[("w0l", j)])
                    wge(e, s_s, evget(("d1e", u - 6)))
                    if u < 6:
                        wge(e, s_v, ev[("A2v_last", NTCQ - 1)])
                        wge(e, s_s, ev[("Apev", NTCQ - 1)])
                    for dc in range(DC):
                        i = e.matmul(pb[bank][:, :],
                                     w016[:, (j % 2) * 2048 + dc * 128:(j % 2) * 2048 + (dc + 1) * 128],
                                     big[:, dc * 2048 + tcq * TC: dc * 2048 + (tcq + 1) * TC],
                                     start=(dc == 0), stop=(dc == DC - 1))
                    s_t.inc(i, 1)
                    ev[("d1m", u)] = s_t.n

                @blk.scalar
                def _(e, u=u, j=j, tcq=tcq, bank=bank):
                    kk = u % 16
                    par = u % 2
                    wge(e, s_t, ev[("d1m", u)])
                    if u >= 16 and u % 16 == 0:
                        wge(e, st_gst[0], ev[("gst", u - 2)])
                        wge(e, st_gst[1], ev[("gst", u - 1)])
                    s_s.inc(e.activation(d1st[:, kk * TC:(kk + 1) * TC], pb[bank][:, :],
                                         AF.Identity, bias=b0[:, j:j + 1]), 1)
                    ev[("d1e", u)] = s_s.n
                    wge(e, s_s, ev[("d1e", u)])
                    st_gst[par].inc(e.dma_start(
                        out=g_sc.ap()[j * 128:(j + 1) * 128, tcq * TC:(tcq + 1) * TC],
                        in_=d1st[:, kk * TC:(kk + 1) * TC]), 16)
                    ev[("gst", u)] = st_gst[par].n

        @blk.gpsimd
        def _(e):
            wge(e, st_y16, ev[("syst", NTC - 1)])
            s_cc.inc(e.collective_compute(
                "AllGather", OP.bypass,
                replica_groups=[[0, 1, 2, 3], [4, 5, 6, 7]],
                ins=[y_in.ap().opt()],
                outs=[y_out.ap().opt()],
            ), 1)
            ev["ag2"] = s_cc.n

        for tc in range(NTCQ):
            @blk.sync
            def _(e, tc=tc):
                wge(e, s_t, ev[("d1m", NTC * 16 - 1)])
                if tc > 0:
                    wge(e, s_t, ev[("e_m", tc - 1, NDO - 1)])
                wge(e, st_gst[0], ev[("gst", NTC * 16 - 2)])
                wge(e, st_gst[1], ev[("gst", NTC * 16 - 1)])
                wge(e, st_xn, ev[("Axn", NTCQ - 1, DC - 1)])
                for b8 in range(8):
                    src = g_sc.ap()[b8 * 8 * 128:(b8 + 1) * 8 * 128, tc * TC:(tc + 1) * TC]
                    src = src.rearrange("(a p) b -> p a b", p=128)
                    m_egl.inc(e.dma_start(out=big[:, b8 * 8 * TC:(b8 + 1) * 8 * TC], in_=src), 16)
                ev[("e_gl", tc)] = m_egl.n

            @blk.scalar
            def _(e, tc=tc):
                wge(e, m_egl, ev[("e_gl", tc)])
                for b8 in range(8):
                    s_s.inc(e.activation(big[:, b8 * 8 * TC:(b8 + 1) * 8 * TC],
                                         big[:, b8 * 8 * TC:(b8 + 1) * 8 * TC], AF.Gelu), 1)
                ev[("e_gelu", tc)] = s_s.n

            for do in range(NDO):
                for hh in range(2):
                    @blk.sync
                    def _(e, tc=tc, do=do, hh=hh):
                        key = ("e_mid" if hh == 0 else "e_m")
                        wge(e, s_t, evget((key, tc, do - 1), (key, tc - 1, NDO - 1)))
                        wge(e, qg_w1, ev["w1cast"])
                        base = do * NJ * 128
                        src = w1_sc.ap()[base + hh * 32 * 128: base + (hh + 1) * 32 * 128, :]
                        src = src.rearrange("(a p) b -> p a b", p=128)
                        m_w1b[hh].inc(e.dma_start(out=w1buf[:, hh * 4096:(hh + 1) * 4096], in_=src), 16)
                        ev[("e_wl", tc, do, hh)] = m_w1b[hh].n

                @blk.tensor
                def _(e, tc=tc, do=do):
                    bank = 2 if do % 2 == 0 else 3
                    wge(e, m_egl, ev[("e_gl", tc)])
                    wge(e, s_s, ev[("e_gelu", tc)])
                    wge(e, s_s, evget(("e_ev", tc, do - 2), ("e_ev", tc - 1, NDO - 2 + (do % 2))))
                    wge(e, m_w1b[0], ev[("e_wl", tc, do, 0)])
                    for j in range(NJ):
                        hh = j // 32
                        if j == 32:
                            wge(e, m_w1b[1], ev[("e_wl", tc, do, 1)])
                        i = e.matmul(pb[bank][:, :],
                                     w1buf[:, hh * 4096 + (j - hh * 32) * 128: hh * 4096 + (j - hh * 32 + 1) * 128],
                                     big[:, j * TC:(j + 1) * TC],
                                     start=(j == 0), stop=(j == NJ - 1))
                        if j == 31:
                            s_t.inc(i, 1)
                            ev[("e_mid", tc, do)] = s_t.n
                    s_t.inc(i, 1)
                    ev[("e_m", tc, do)] = s_t.n

                @blk.scalar
                def _(e, tc=tc, do=do):
                    bank = 2 if do % 2 == 0 else 3
                    par = do % 2
                    wge(e, s_t, ev[("e_m", tc, do)])
                    wge(e, st_f[par], evget(("e_fs", tc, do - 2), ("e_fs", tc - 1, NDO - 2 + (do % 2))))
                    s_s.inc(e.activation(fstage[:, par * TC:(par + 1) * TC], pb[bank][:, :],
                                         AF.Identity, bias=b1[:, do:do + 1]), 1)
                    ev[("e_ev", tc, do)] = s_s.n
                    wge(e, s_s, ev[("e_ev", tc, do)])
                    st_f[par].inc(e.dma_start(
                        out=f_sc.ap()[do * 128:(do + 1) * 128, tc * TC:(tc + 1) * TC],
                        in_=fstage[:, par * TC:(par + 1) * TC]), 16)
                    ev[("e_fs", tc, do)] = st_f[par].n

        @blk.sync
        def _(e):
            wge(e, s_cc, ev["ag2"])
            wge(e, s_t, ev[("e_m", NTCQ - 1, NDO - 1)])
            pid = e.partition_id()
            qoff = (pid % 4) * TQ
            for dc in range(DC):
                m_cyl.inc(e.dma_start(
                    out=big[:, dc * 2048:(dc + 1) * 2048],
                    in_=y_out.ap()[dc * 128:(dc + 1) * 128, bass.ds(qoff, TQ)]), 16)
            ev["c_yl"] = m_cyl.n

        def prevC(name, do, tc):
            if tc >= 2:
                return ev.get((name, do, tc - 2), 0)
            if do >= 1:
                return ev.get((name, do - 1, tc + 2), 0)
            return 0

        for do in range(NDO):
            @blk.sync
            def _(e, do=do):
                if do < 2:
                    wge(e, qg_wo, ev["wocast"])
                    wge(e, s_t, ev[("d1m", NTC * 16 - 1)])
                else:
                    wge(e, s_t, ev[("c_m", do - 2, NTCQ - 1)])
                src = wout_sc.ap()[do * DC * 128:(do + 1) * DC * 128, :].rearrange("(a p) b -> p a b", p=128)
                m_w0[do % 2].inc(e.dma_start(out=w016[:, (do % 2) * 2048:(do % 2 + 1) * 2048], in_=src), 16)
                ev[("c_wl", do)] = m_w0[do % 2].n

            for tc in range(NTCQ):
                @blk.tensor
                def _(e, do=do, tc=tc):
                    wge(e, m_w0[do % 2], ev[("c_wl", do)])
                    wge(e, m_cyl, ev["c_yl"])
                    if do == 0:
                        wge(e, s_s, ev[("e_ev", NTCQ - 1, NDO - 1)])
                    else:
                        wge(e, s_s, ev[("c_ev", do - 1, tc)])
                    for dc in range(DC):
                        i = e.matmul(pb[tc][:, :],
                                     w016[:, (do % 2) * 2048 + dc * 128:(do % 2) * 2048 + (dc + 1) * 128],
                                     big[:, dc * 2048 + tc * TC: dc * 2048 + (tc + 1) * TC],
                                     start=(dc == 0), stop=(dc == DC - 1))
                    s_t.inc(i, 1)
                    ev[("c_m", do, tc)] = s_t.n

                @blk.scalar
                def _(e, do=do, tc=tc):
                    par = tc % 2
                    wge(e, s_t, ev[("c_m", do, tc)])
                    wge(e, s_v, prevC("c_v", do, tc))
                    s_s.inc(e.activation(a32[:, 1024 + par * TC: 1024 + (par + 1) * TC], pb[tc][:, :],
                                         AF.Relu, bias=wob[:, do:do + 1]), 1)
                    ev[("c_ev", do, tc)] = s_s.n

                @blk.sync
                def _(e, do=do, tc=tc):
                    par = tc % 2
                    wge(e, s_v, prevC("c_v", do, tc))
                    wge(e, st_f[0], ev[("e_fs", NTCQ - 1, NDO - 2)])
                    wge(e, st_f[1], ev[("e_fs", NTCQ - 1, NDO - 1)])
                    m_cld[par].inc(e.dma_start(out=a32[:, par * TC:(par + 1) * TC],
                                               in_=x_rows[do * 128:(do + 1) * 128, tc * TC:(tc + 1) * TC]), 16)
                    m_cld[par].inc(e.dma_start(out=a32[:, 3584 + 256 * par: 3584 + 256 * (par + 1)].bitcast(F16),
                                               in_=xn_sc.ap()[do * 128:(do + 1) * 128, tc * TC:(tc + 1) * TC]), 16)
                    m_cld[par].inc(e.dma_start(out=a32[:, 3072 + 256 * par: 3072 + 256 * (par + 1)].bitcast(F16),
                                               in_=f_sc.ap()[do * 128:(do + 1) * 128, tc * TC:(tc + 1) * TC]), 16)
                    ev[("c_ld", do, tc)] = m_cld[par].n

                @blk.vector
                def _(e, do=do, tc=tc):
                    par = tc % 2
                    wge(e, s_s, ev[("c_ev", do, tc)])
                    wge(e, m_cld[par], ev[("c_ld", do, tc)])
                    wge(e, st_out[par], prevC("c_st", do, tc))
                    if (do, tc) == (0, 0):
                        wge(e, s_v, ev[("snv", NTC - 1, N16 - 1)])
                    s_v.inc(e.scalar_tensor_tensor(a32[:, 2048 + par * TC: 2048 + (par + 1) * TC],
                                           a32[:, 3584 + 256 * par: 3584 + 256 * (par + 1)].bitcast(F16),
                                           dvec[:, do:do + 1],
                                           a32[:, 1024 + par * TC: 1024 + (par + 1) * TC],
                                           OP.mult, OP.add), 1)
                    wge(e, s_v, s_v.n)
                    s_v.inc(e.tensor_tensor(a32[:, 2048 + par * TC: 2048 + (par + 1) * TC],
                                    a32[:, 2048 + par * TC: 2048 + (par + 1) * TC],
                                    a32[:, par * TC:(par + 1) * TC], OP.add), 1)
                    wge(e, s_v, s_v.n)
                    s_v.inc(e.tensor_tensor(a32[:, 2048 + par * TC: 2048 + (par + 1) * TC],
                                            a32[:, 2048 + par * TC: 2048 + (par + 1) * TC],
                                            a32[:, 3072 + 256 * par: 3072 + 256 * (par + 1)].bitcast(F16), OP.add), 1)
                    ev[("c_v", do, tc)] = s_v.n

                @blk.scalar
                def _(e, do=do, tc=tc):
                    par = tc % 2
                    wge(e, s_v, ev[("c_v", do, tc)])
                    st_out[par].inc(e.dma_start(
                        out=out_d[do * 128:(do + 1) * 128, tc * TC:(tc + 1) * TC],
                        in_=a32[:, 2048 + par * TC: 2048 + (par + 1) * TC]), 16)
                    ev[("c_st", do, tc)] = st_out[par].n

        if DEBUG:
            @blk.sync
            def _(e):
                wge(e, st_out[0], st_out[0].n)
                wge(e, st_out[1], st_out[1].n)
                for dst, src in [(dbg_yin, y_in.ap()), (dbg_yout, y_out.ap()),
                                 (dbg_f, f_sc.ap()), (dbg_pout, p_out.ap()),
                                 (dbg_xn, xn_sc.ap())]:
                    st_dbg.inc(e.dma_start(out=dst, in_=src), 16)
                wge(e, st_dbg, st_dbg.n)

        @blk.sync
        def _(e):
            e.wait_ge(st_out[0].h, st_out[0].n)
            e.wait_ge(st_out[1].h, st_out[1].n)
            e.wait_ge(st_y16.h, st_y16.n)
            e.wait_ge(st_pin.h, st_pin.n)
            e.wait_ge(st_xn.h, st_xn.n)
            e.wait_ge(st_f[0].h, st_f[0].n)
            e.wait_ge(st_f[1].h, st_f[1].n)
            e.wait_ge(st_gst[0].h, st_gst[0].n)
            e.wait_ge(st_gst[1].h, st_gst[1].n)
            e.wait_ge(qg_w0.h, qg_w0.n)
            e.wait_ge(qg_w1.h, qg_w1.n)
            e.wait_ge(qg_wo.h, qg_wo.n)
            e.wait_ge(qg_sb.h, qg_sb.n)

    return nc


_NC_CACHE = {}


def make_maps_inner(x, xpw, dtpw, dt_bias, A_log, Dv, Wout, outb, W0, bias0, W1, bias1):
    C = np.ascontiguousarray
    wx_t = C(xpw.T)
    w0_lay = C(W0.T.reshape(16, 128, 64, 128).transpose(2, 0, 1, 3).reshape(64 * 16 * 128, 128))
    w1_lay = C(W1.T.reshape(64, 128, 16, 128).transpose(2, 0, 1, 3).reshape(16 * 64 * 128, 128))
    wout_lay = C(Wout.T.reshape(16, 128, 16, 128).transpose(2, 0, 1, 3).reshape(16 * 16 * 128, 128))
    b0_l = C(bias0.reshape(64, 128).T)
    b1_l = C(bias1.reshape(16, 128).T)
    d_l = C(Dv.reshape(16, 128).T)
    woutb_l = C(outb.reshape(16, 128).T)
    in_maps = []
    for c in range(8):
        bi, qi = c // 4, c % 4
        sl = slice(qi * DSL, (qi + 1) * DSL)
        in_maps.append({
            "x_rows": C(x[bi, qi * TQ:(qi + 1) * TQ, :].T),
            "x_scan": C(x[bi, :, sl].T),
            "wx_t": wx_t,
            "dtpw_t": C(dtpw[sl, :].T),
            "dt_b": C(dt_bias[sl].reshape(4, 128).T),
            "a_log": C(A_log[sl].reshape(4, 128, 16).transpose(1, 0, 2).reshape(128, 64)),
            "d_vec": d_l,
            "wout_lay": wout_lay,
            "woutb": woutb_l,
            "w0_lay": w0_lay,
            "b0_in": b0_l,
            "w1_lay": w1_lay,
            "b1_in": b1_l,
        })
    return in_maps


def run_full(in_maps):
    if "nc" not in _NC_CACHE:
        _NC_CACHE["nc"] = build_nc()
    nc = _NC_CACHE["nc"]
    return run_bass_kernel_spmd(nc, in_maps, core_ids=list(range(8)))


def kernel(L0_attn_hidden_states, L0_attn_x_proj_weight, L0_attn_dt_proj_weight,
           L0_attn_dt_proj_bias, L0_attn_A_log, L0_attn_D, L0_attn_out_proj_weight,
           L0_attn_out_proj_bias, L0_ffn_W0, L0_ffn_bias0, L0_ffn_W1, L0_ffn_bias1):
    x = np.asarray(L0_attn_hidden_states, dtype=np.float32)
    in_maps = make_maps_inner(
        x,
        np.asarray(L0_attn_x_proj_weight, dtype=np.float32),
        np.asarray(L0_attn_dt_proj_weight, dtype=np.float32),
        np.asarray(L0_attn_dt_proj_bias, dtype=np.float32),
        np.asarray(L0_attn_A_log, dtype=np.float32),
        np.asarray(L0_attn_D, dtype=np.float32),
        np.asarray(L0_attn_out_proj_weight, dtype=np.float32),
        np.asarray(L0_attn_out_proj_bias, dtype=np.float32),
        np.asarray(L0_ffn_W0, dtype=np.float32),
        np.asarray(L0_ffn_bias0, dtype=np.float32),
        np.asarray(L0_ffn_W1, dtype=np.float32),
        np.asarray(L0_ffn_bias1, dtype=np.float32),
    )
    res = run_full(in_maps)
    out = np.empty((2, T, DM), dtype=np.float32)
    for c in range(8):
        bi, qi = c // 4, c % 4
        out[bi, qi * TQ:(qi + 1) * TQ, :] = np.asarray(res.results[c]["out"]).T
    return out


# version 10
# speedup vs baseline: 1.1364x; 1.0136x over previous
import numpy as np
from contextlib import ExitStack

import concourse.bass as bass
import concourse.mybir as mybir
from concourse.bass_utils import run_bass_kernel_spmd

F32 = mybir.dt.float32
F16 = mybir.dt.float16
AF = mybir.ActivationFunctionType
OP = mybir.AluOpType

DM = 2048
T = 8192
TQ = 2048
DSL = 512
TC = 512
DC = 16
DCS = 4
NTCQ = 4
NTC = 16
N16 = 16
NJ = 64
NDO = 16
PR = 164

DEBUG = False
CAST_BARRIER = False


class Sem:
    def __init__(self, nc, stack, name):
        self.h = stack.enter_context(nc.semaphore(name))
        self.n = 0

    def inc(self, instr, k):
        instr.then_inc(self.h, k)
        self.n += k
        return self.n


def build_nc():
    nc = bass.Bass("TRN2", target_bir_lowering=False)

    x_rows = nc.dram_tensor("x_rows", [DM, TQ], F32, kind="ExternalInput").ap()
    x_scan = nc.dram_tensor("x_scan", [DSL, T], F32, kind="ExternalInput").ap()
    wx_t = nc.dram_tensor("wx_t", [DM, 160], F32, kind="ExternalInput").ap()
    dtpw_t = nc.dram_tensor("dtpw_t", [128, DSL], F32, kind="ExternalInput").ap()
    dt_b = nc.dram_tensor("dt_b", [128, 4], F32, kind="ExternalInput").ap()
    a_log = nc.dram_tensor("a_log", [128, 64], F32, kind="ExternalInput").ap()
    d_vec = nc.dram_tensor("d_vec", [128, 16], F32, kind="ExternalInput").ap()
    wout_lay = nc.dram_tensor("wout_lay", [NDO * DC * 128, 128], F32, kind="ExternalInput").ap()
    woutb = nc.dram_tensor("woutb", [128, 16], F32, kind="ExternalInput").ap()
    w0_lay = nc.dram_tensor("w0_lay", [NJ * DC * 128, 128], F32, kind="ExternalInput").ap()
    b0_in = nc.dram_tensor("b0_in", [128, 64], F32, kind="ExternalInput").ap()
    w1_lay = nc.dram_tensor("w1_lay", [NDO * NJ * 128, 128], F32, kind="ExternalInput").ap()
    b1_in = nc.dram_tensor("b1_in", [128, 16], F32, kind="ExternalInput").ap()

    out_d = nc.dram_tensor("out", [DM, TQ], F32, kind="ExternalOutput").ap()
    if DEBUG:
        dbg_yin = nc.dram_tensor("dbg_yin", [DSL, T], F16, kind="ExternalOutput").ap()
        dbg_yout = nc.dram_tensor("dbg_yout", [DM, T], F16, kind="ExternalOutput").ap()
        dbg_f = nc.dram_tensor("dbg_f", [DM, TQ], F16, kind="ExternalOutput").ap()
        dbg_pout = nc.dram_tensor("dbg_pout", [4 * PR, TQ], F16, kind="ExternalOutput").ap()
        dbg_xn = nc.dram_tensor("dbg_xn", [DM, TQ], F16, kind="ExternalOutput").ap()

    w0_sc = nc.dram_tensor("w0_sc", [NJ * DC * 128, 128], F16)
    w1_sc = nc.dram_tensor("w1_sc", [NDO * NJ * 128, 128], F16)
    wout_sc = nc.dram_tensor("wout_sc", [NDO * DC * 128, 128], F16)
    xn_sc = nc.dram_tensor("xn_sc", [DM, TQ], F16)
    g_sc = nc.dram_tensor("g_sc", [NJ * 128, TQ], F16)
    f_sc = nc.dram_tensor("f_sc", [DM, TQ], F16)
    p_in = nc.dram_tensor("p_in", [PR, TQ], F16)
    p_out = nc.dram_tensor("p_out", [4 * PR, TQ], F16)
    y_in = nc.dram_tensor("y_in", [DSL, T], F16)
    y_out = nc.dram_tensor("y_out", [DM, T], F16)

    with ExitStack() as stack:
        ec = stack.enter_context

        wx16 = ec(nc.sbuf_tensor("wx16", [128, DC * 160], F16))
        dtpw16 = ec(nc.sbuf_tensor("dtpw16", [128, DSL], F16))
        aneg = ec(nc.sbuf_tensor("aneg", [128, 64], F32))
        dtb = ec(nc.sbuf_tensor("dtb", [128, 4], F32))
        dvec = ec(nc.sbuf_tensor("dvec", [128, 16], F32))
        wob = ec(nc.sbuf_tensor("wob", [128, 16], F32))
        b0 = ec(nc.sbuf_tensor("b0", [128, 64], F32))
        b1 = ec(nc.sbuf_tensor("b1", [128, 16], F32))
        onesc = ec(nc.sbuf_tensor("onesc", [128, 1], F16))
        onesr = ec(nc.sbuf_tensor("onesr", [1, 128], F16))
        big = ec(nc.sbuf_tensor("big", [128, 32768], F16))
        w016 = ec(nc.sbuf_tensor("w016", [128, 2 * 2048], F16))
        w1buf = ec(nc.sbuf_tensor("w1buf", [128, 2 * 4096], F16))
        fstage = ec(nc.sbuf_tensor("fstage", [128, 2 * TC], F16))
        d1st = ec(nc.sbuf_tensor("d1st", [128, 16 * TC], F16))
        mis1 = ec(nc.sbuf_tensor("mis1", [128, 64], F32))
        dtl2 = ec(nc.sbuf_tensor("dtl2", [128, 2 * TC], F16))
        delta16 = ec(nc.sbuf_tensor("delta16", [128, DCS * TC], F16))
        dx16 = ec(nc.sbuf_tensor("dx16", [128, DCS * TC], F16))
        xns16 = ec(nc.sbuf_tensor("xns16", [128, TC], F16))
        xsc32 = ec(nc.sbuf_tensor("xsc32", [128, 2 * TC], F32))
        murs = ec(nc.sbuf_tensor("murs", [128, 2 * TC], F16))
        rsrs = ec(nc.sbuf_tensor("rsrs", [128, 2 * TC], F16))
        brep = ec(nc.sbuf_tensor("brep", [128, 2 * TC], F16))
        crep = ec(nc.sbuf_tensor("crep", [128, 2 * TC], F16))
        a32 = ec(nc.sbuf_tensor("a32", [128, 8 * TC], F32))
        u4 = ec(nc.sbuf_tensor("u4", [128, DCS * TC], F16))
        h16 = ec(nc.sbuf_tensor("h16", [128, DCS * TC], F16))
        tmp16 = ec(nc.sbuf_tensor("tmp16", [128, DCS * TC], F16))
        yacc = ec(nc.sbuf_tensor("yacc", [128, DCS * TC], F32))
        y16s = ec(nc.sbuf_tensor("y16s", [128, DCS * TC], F16))
        hstate = ec(nc.sbuf_tensor("hstate", [128, 64], F32))
        sexp32 = ec(nc.sbuf_tensor("sexp32", [128, TC], F16))
        st_a = ec(nc.sbuf_tensor("st_a", [1, TC], F32))
        st_b = ec(nc.sbuf_tensor("st_b", [1, TC], F32))
        st_c = ec(nc.sbuf_tensor("st_c", [1, TC], F32))
        st_d = ec(nc.sbuf_tensor("st_d", [1, TC], F32))
        mu16 = ec(nc.sbuf_tensor("mu16", [1, TC], F16))
        rstd16 = ec(nc.sbuf_tensor("rstd16", [1, TC], F16))
        dtlst = ec(nc.sbuf_tensor("dtlst", [128, TC], F16))
        bcst = ec(nc.sbuf_tensor("bcst", [32, TC], F16))
        xn0 = ec(nc.sbuf_tensor("xn0", [128, TC], F32))

        pb = [ec(nc.psum_tensor(f"pb{i}", [128, 512], F32)) for i in range(8)]

        s_t = Sem(nc, stack, "s_t")
        s_s = Sem(nc, stack, "s_s")
        s_v = Sem(nc, stack, "s_v")
        s_g = Sem(nc, stack, "s_g")
        s_cc = Sem(nc, stack, "s_cc")
        m_cst = Sem(nc, stack, "m_cst")
        m_xsc = [Sem(nc, stack, f"m_xsc{i}") for i in range(2)]
        m_mur = [Sem(nc, stack, f"m_mur{i}") for i in range(2)]
        m_bc = [Sem(nc, stack, f"m_bc{i}") for i in range(2)]
        m_dtl = [Sem(nc, stack, f"m_dtl{i}") for i in range(2)]
        m_w0 = [Sem(nc, stack, f"m_w0{i}") for i in range(2)]
        qg_w0 = Sem(nc, stack, "qg_w0")
        qg_w1 = Sem(nc, stack, "qg_w1")
        qg_wo = Sem(nc, stack, "qg_wo")
        qg_sb = Sem(nc, stack, "qg_sb")
        m_w1b = [Sem(nc, stack, f"m_w1b{i}") for i in range(2)]
        m_egl = Sem(nc, stack, "m_egl")
        m_cyl = Sem(nc, stack, "m_cyl")
        m_cld = [Sem(nc, stack, f"m_cld{i}") for i in range(2)]
        st_pin = Sem(nc, stack, "st_pin")
        st_xn = Sem(nc, stack, "st_xn")
        st_gst = [Sem(nc, stack, f"st_gst{i}") for i in range(2)]
        st_y16 = Sem(nc, stack, "st_y16")
        st_f = [Sem(nc, stack, f"st_f{i}") for i in range(2)]
        st_out = [Sem(nc, stack, f"st_out{i}") for i in range(2)]
        q_g = Sem(nc, stack, "q_g")
        st_dbg = Sem(nc, stack, "st_dbg")

        blk = ec(nc.Block())
        ev = {}

        def wge(eng, sem, val):
            assert val <= sem.n, f"wait {val} > emitted {sem.n}"
            if val > 0:
                eng.wait_ge(sem.h, val)

        def evget(*keys):
            for k in keys:
                if k in ev:
                    return ev[k]
            return 0

        @blk.sync
        def _(e):
            for dst, src in [(dtb, dt_b), (dvec, d_vec), (wob, woutb), (b0, b0_in), (b1, b1_in)]:
                m_cst.inc(e.dma_start(out=dst[:, :], in_=src), 16)
            m_cst.inc(e.dma_start(out=mis1[:, 0:64], in_=a_log), 16)
            ev["c_l1"] = m_cst.n

        @blk.scalar
        def _(e):
            wge(e, m_cst, ev["c_l1"])
            s_s.inc(e.activation(aneg[:, :], mis1[:, 0:64], AF.Exp), 1)
            ev["c_exp"] = s_s.n

        @blk.vector
        def _(e):
            wge(e, s_s, ev["c_exp"])
            s_v.inc(e.tensor_scalar(aneg[:, :], aneg[:, :], -1.0, None, OP.mult), 1)
            ev["c_aneg"] = s_v.n

        @blk.gpsimd
        def _(e):
            s_g.inc(e.memset(onesc[:, :], 1.0), 1)
            s_g.inc(e.memset(onesr[:, :], 1.0), 1)
            qg_sb.inc(e.dma_start(out=dtpw16[:, :], in_=dtpw_t), 16)
            qg_sb.inc(e.dma_start(out=wx16[:, :], in_=wx_t.rearrange("(c p) n -> p c n", p=128)), 16)
            ev["sbcast"] = qg_sb.n
            NB0 = NJ * DC * 128 // 16
            for i in range(16):
                if i >= 2 and i % 2 == 0:
                    e.wait_ge(qg_w0.h, 16 * i)
                qg_w0.inc(e.dma_start(out=w0_sc.ap()[i * NB0:(i + 1) * NB0, :],
                                      in_=w0_lay[i * NB0:(i + 1) * NB0, :]), 16)
            ev["w0cast"] = qg_w0.n
            NBO = NDO * DC * 128 // 4
            for i in range(4):
                if i >= 2 and i % 2 == 0:
                    e.wait_ge(qg_wo.h, 16 * i)
                qg_wo.inc(e.dma_start(out=wout_sc.ap()[i * NBO:(i + 1) * NBO, :],
                                      in_=wout_lay[i * NBO:(i + 1) * NBO, :]), 16)
            ev["wocast"] = qg_wo.n
            NB1 = NDO * NJ * 128 // 16
            for i in range(16):
                if i >= 2 and i % 2 == 0:
                    e.wait_ge(qg_w1.h, 16 * i)
                qg_w1.inc(e.dma_start(out=w1_sc.ap()[i * NB1:(i + 1) * NB1, :],
                                      in_=w1_lay[i * NB1:(i + 1) * NB1, :]), 16)
            ev["w1cast"] = qg_w1.n
        ev["consts_g"] = s_g.n

        if CAST_BARRIER:
            @blk.sync
            def _(e):
                wge(e, qg_w0, ev["w0cast"])
                wge(e, qg_w1, ev["w1cast"])
                wge(e, qg_wo, ev["wocast"])
                wge(e, qg_sb, ev["sbcast"])

        for tc in range(NTCQ):
            for dc in range(DC):
                par = dc % 2

                @blk.sync
                def _(e, tc=tc, dc=dc, par=par):
                    if dc >= 2:
                        wge(e, s_v, ev[("A1v", tc, dc - 2)])
                        wge(e, s_s, ev[("A1s", tc, dc - 2)])
                    elif tc > 0:
                        wge(e, s_v, ev[("A2v_last", tc - 1)])
                    m_xsc[par].inc(e.dma_start(
                        out=xsc32[:, par * TC:(par + 1) * TC],
                        in_=x_rows[dc * 128:(dc + 1) * 128, tc * TC:(tc + 1) * TC]), 16)
                    ev[("A1l", tc, dc)] = m_xsc[par].n

                @blk.vector
                def _(e, tc=tc, dc=dc, par=par):
                    wge(e, m_xsc[par], ev[("A1l", tc, dc)])
                    if dc >= 2:
                        wge(e, s_t, ev[("Astm", tc, dc - 2)])
                    elif tc > 0:
                        wge(e, s_t, ev[("Astatm", tc - 1)])
                    s_v.inc(e.tensor_copy(tmp16[:, par * TC:(par + 1) * TC],
                                          xsc32[:, par * TC:(par + 1) * TC]), 1)
                    ev[("A1v", tc, dc)] = s_v.n

                @blk.scalar
                def _(e, tc=tc, dc=dc, par=par):
                    wge(e, m_xsc[par], ev[("A1l", tc, dc)])
                    if dc >= 2:
                        wge(e, s_t, ev[("Astm", tc, dc - 2)])
                    elif tc > 0:
                        wge(e, s_t, ev[("Astatm", tc - 1)])
                    s_s.inc(e.activation(tmp16[:, (2 + par) * TC:(3 + par) * TC],
                                         xsc32[:, par * TC:(par + 1) * TC], AF.Square), 1)
                    ev[("A1s", tc, dc)] = s_s.n

                @blk.tensor
                def _(e, tc=tc, dc=dc, par=par):
                    wge(e, s_v, ev[("A1v", tc, dc)])
                    wge(e, s_s, ev[("A1s", tc, dc)])
                    if dc == 0:
                        wge(e, s_g, ev["consts_g"])
                        wge(e, qg_sb, ev["sbcast"])
                        if tc > 0:
                            wge(e, s_v, ev[("Astat", tc - 1)])
                    e.matmul(pb[0][0:1, :], onesc[:, :], tmp16[:, par * TC:(par + 1) * TC],
                             start=(dc == 0), stop=(dc == DC - 1), skip_group_check=True)
                    i2 = e.matmul(pb[0][32:33, :], onesc[:, :], tmp16[:, (2 + par) * TC:(3 + par) * TC],
                                  start=(dc == 0), stop=(dc == DC - 1), skip_group_check=True)
                    s_t.inc(i2, 1)
                    ev[("Astm", tc, dc)] = s_t.n
                    if dc == DC - 1:
                        ev[("Astatm", tc)] = s_t.n

            ev[("A1v_last", tc)] = ev[("A1v", tc, DC - 1)]
            ev[("A1s_last", tc)] = ev[("A1s", tc, DC - 1)]

            @blk.vector
            def _(e, tc=tc):
                wge(e, s_t, ev[("Astatm", tc)])
                if tc > 0:
                    wge(e, st_pin, ev[("Apst", tc - 1)])
                    wge(e, s_v, ev[("Astat", tc - 1)])
                s_v.inc(e.tensor_scalar(st_a[:, :], pb[0][0:1, :], 1.0 / DM, None, OP.mult), 1)
                s_v.inc(e.tensor_scalar(st_b[:, :], pb[0][32:33, :], 1.0 / DM, None, OP.mult), 1)
                wge(e, s_v, s_v.n)
                s_v.inc(e.tensor_tensor(st_c[:, :], st_a[:, :], st_a[:, :], OP.mult), 1)
                wge(e, s_v, s_v.n)
                s_v.inc(e.tensor_tensor(st_d[:, :], st_b[:, :], st_c[:, :], OP.subtract), 1)
                wge(e, s_v, s_v.n)
                s_v.inc(e.tensor_scalar(st_d[:, :], st_d[:, :], 1e-5, None, OP.add), 1)
                wge(e, s_v, s_v.n)
                s_v.inc(e.reciprocal(st_c[:, :], st_d[:, :]), 1)
                wge(e, s_v, s_v.n)
                s_v.inc(e.tensor_copy(mu16[:, :], st_a[:, :]), 1)
                ev[("Astat", tc)] = s_v.n

            @blk.scalar
            def _(e, tc=tc):
                wge(e, s_v, ev[("Astat", tc)])
                if tc > 0:
                    wge(e, st_pin, ev[("Apst", tc - 1)])
                s_s.inc(e.activation(rstd16[:, :], st_c[:, :], AF.Sqrt), 1)
                ev[("Arstd", tc)] = s_s.n

            @blk.tensor
            def _(e, tc=tc):
                wge(e, s_s, ev[("Arstd", tc)])
                if tc > 0:
                    wge(e, s_v, ev[("A2v_last", tc - 1)])
                e.matmul(pb[1][:, :], onesr[:, :], mu16[:, :], start=True, stop=True)
                i2 = e.matmul(pb[2][:, :], onesr[:, :], rstd16[:, :], start=True, stop=True)
                s_t.inc(i2, 1)
                ev[("Arep", tc)] = s_t.n

            for dc in range(DC):
                par = dc % 2

                @blk.sync
                def _(e, tc=tc, dc=dc, par=par):
                    if dc >= 2:
                        wge(e, s_v, ev[("A2v", tc, dc - 2)])
                    else:
                        wge(e, s_v, ev[("A1v_last", tc)])
                        wge(e, s_s, ev[("A1s_last", tc)])
                    m_xsc[par].inc(e.dma_start(
                        out=xsc32[:, par * TC:(par + 1) * TC],
                        in_=x_rows[dc * 128:(dc + 1) * 128, tc * TC:(tc + 1) * TC]), 16)
                    ev[("A2l", tc, dc)] = m_xsc[par].n

                @blk.vector
                def _(e, tc=tc, dc=dc, par=par):
                    wge(e, m_xsc[par], ev[("A2l", tc, dc)])
                    wge(e, s_t, ev[("Arep", tc)])
                    if dc > 0:
                        wge(e, s_v, ev[("A2v", tc, dc - 1)])
                    s_v.inc(e.tensor_tensor(xn0[:, :], xsc32[:, par * TC:(par + 1) * TC],
                                            pb[1][:, :], OP.subtract), 1)
                    wge(e, s_v, s_v.n)
                    s_v.inc(e.tensor_tensor(
                        big[:, dc * 2048 + tc * TC: dc * 2048 + (tc + 1) * TC],
                        xn0[:, :], pb[2][:, :], OP.mult), 1)
                    ev[("A2v", tc, dc)] = s_v.n

                @blk.scalar
                def _(e, tc=tc, dc=dc):
                    wge(e, s_v, ev[("A2v", tc, dc)])
                    st_xn.inc(e.dma_start(
                        out=xn_sc.ap()[dc * 128:(dc + 1) * 128, tc * TC:(tc + 1) * TC],
                        in_=big[:, dc * 2048 + tc * TC: dc * 2048 + (tc + 1) * TC]), 16)
                    ev[("Axn", tc, dc)] = st_xn.n

                @blk.tensor
                def _(e, tc=tc, dc=dc):
                    wge(e, s_v, ev[("A2v", tc, dc)])
                    if dc == 0 and tc > 0:
                        wge(e, s_s, ev[("Apev", tc - 1)])
                    xn_t = big[:, dc * 2048 + tc * TC: dc * 2048 + (tc + 1) * TC]
                    e.matmul(pb[3][:, :], wx16[:, dc * 160: dc * 160 + 128], xn_t,
                             start=(dc == 0), stop=(dc == DC - 1), skip_group_check=True)
                    i2 = e.matmul(pb[4][0:32, :], wx16[:, dc * 160 + 128: (dc + 1) * 160], xn_t,
                                  start=(dc == 0), stop=(dc == DC - 1), skip_group_check=True)
                    if dc == DC - 1:
                        s_t.inc(i2, 1)
                        ev[("Apm", tc)] = s_t.n
            ev[("A2v_last", tc)] = s_v.n

            @blk.scalar
            def _(e, tc=tc):
                wge(e, s_t, ev[("Apm", tc)])
                wge(e, s_v, ev[("Astat", tc)])
                wge(e, s_s, ev[("Arstd", tc)])
                if tc > 0:
                    wge(e, st_pin, ev[("Apst", tc - 1)])
                e.activation(dtlst[:, :], pb[3][:, :], AF.Identity)
                s_s.inc(e.activation(bcst[:, :], pb[4][0:32, :], AF.Identity), 1)
                ev[("Apev", tc)] = s_s.n
                wge(e, s_s, ev[("Apev", tc)])
                st_pin.inc(e.dma_start(out=p_in.ap()[0:128, tc * TC:(tc + 1) * TC], in_=dtlst[:, :]), 16)
                st_pin.inc(e.dma_start(out=p_in.ap()[128:160, tc * TC:(tc + 1) * TC], in_=bcst[:, :]), 16)
                st_pin.inc(e.dma_start(out=p_in.ap()[160:161, tc * TC:(tc + 1) * TC], in_=mu16[:, :]), 16)
                st_pin.inc(e.dma_start(out=p_in.ap()[161:162, tc * TC:(tc + 1) * TC], in_=rstd16[:, :]), 16)
                st_pin.inc(e.dma_start(out=p_in.ap()[162:163, tc * TC:(tc + 1) * TC], in_=rstd16[:, :]), 16)
                st_pin.inc(e.dma_start(out=p_in.ap()[163:164, tc * TC:(tc + 1) * TC], in_=rstd16[:, :]), 16)
                ev[("Apst", tc)] = st_pin.n

        @blk.gpsimd
        def _(e):
            wge(e, st_pin, ev[("Apst", NTCQ - 1)])
            s_cc.inc(e.collective_compute(
                "AllGather", OP.bypass,
                replica_groups=[[0, 1, 2, 3], [4, 5, 6, 7]],
                ins=[p_in.ap().opt()],
                outs=[p_out.ap().opt()],
            ), 1)
            ev["ag1"] = s_cc.n

        d1banks = [0, 1, 2, 3, 5, 6]

        for tc in range(NTC):
            rnk = tc // 4
            tl = (tc % 4) * TC
            tpar = tc % 2

            @blk.sync
            def _(e, tc=tc, rnk=rnk, tl=tl, tpar=tpar):
                wge(e, s_cc, ev["ag1"])
                if tc >= 2:
                    wge(e, s_v, ev[("sdxv", tc - 2, DCS - 1)])
                m_mur[tpar].inc(e.dma_start(
                    out=murs[:, tpar * TC:(tpar + 1) * TC],
                    in_=p_out.ap()[rnk * PR + 160: rnk * PR + 161, tl:tl + TC].broadcast_to([128, TC])), 16)
                m_mur[tpar].inc(e.dma_start(
                    out=rsrs[:, tpar * TC:(tpar + 1) * TC],
                    in_=p_out.ap()[rnk * PR + 161: rnk * PR + 162, tl:tl + TC].broadcast_to([128, TC])), 16)
                ev[("mul", tc)] = m_mur[tpar].n
                if tc >= 2:
                    wge(e, s_t, ev[("sdm", tc - 2, DCS - 1)])
                m_dtl[tpar].inc(e.dma_start(
                    out=dtl2[:, tpar * TC:(tpar + 1) * TC],
                    in_=p_out.ap()[rnk * PR: rnk * PR + 128, tl:tl + TC]), 16)
                ev[("dtll", tc)] = m_dtl[tpar].n
                for dcs in range(2):
                    if tc == 0:
                        wge(e, s_v, ev[("A2v_last", NTCQ - 1)])
                    else:
                        wge(e, s_v, ev[("sdxv", tc - 1, dcs + 2)])
                    m_xsc[dcs].inc(e.dma_start(
                        out=xsc32[:, dcs * TC:(dcs + 1) * TC],
                        in_=x_scan[dcs * 128:(dcs + 1) * 128, tc * TC:(tc + 1) * TC]), 16)
                    ev[("sxl", tc, dcs)] = m_xsc[dcs].n
                for n in (0, 1):
                    if tc > 0:
                        wge(e, s_v, ev[("snv", tc - 1, N16 - 2 + n)])
                    m_bc[n].inc(e.dma_start(
                        out=brep[:, n * TC:(n + 1) * TC],
                        in_=p_out.ap()[rnk * PR + 128 + n: rnk * PR + 129 + n, tl:tl + TC].broadcast_to([128, TC])), 16)
                    m_bc[n].inc(e.dma_start(
                        out=crep[:, n * TC:(n + 1) * TC],
                        in_=p_out.ap()[rnk * PR + 144 + n: rnk * PR + 145 + n, tl:tl + TC].broadcast_to([128, TC])), 16)
                    ev[("sbl", tc, n)] = m_bc[n].n

            for dcs in range(DCS):
                @blk.tensor
                def _(e, tc=tc, dcs=dcs, tpar=tpar):
                    bank = 4 if dcs % 2 == 0 else 7
                    wge(e, m_dtl[tpar], ev[("dtll", tc)])
                    wge(e, s_s, evget(("sdev", tc, dcs - 2), ("sdev", tc - 1, dcs + 2)))
                    if (tc, dcs) == (0, 0):
                        wge(e, qg_sb, ev["sbcast"])
                        wge(e, s_s, ev[("Apev", NTCQ - 1)])
                    i = e.matmul(pb[bank][:, :], dtpw16[:, dcs * 128:(dcs + 1) * 128],
                                 dtl2[:, tpar * TC:(tpar + 1) * TC], start=True, stop=True)
                    s_t.inc(i, 1)
                    ev[("sdm", tc, dcs)] = s_t.n

                @blk.scalar
                def _(e, tc=tc, dcs=dcs):
                    bank = 4 if dcs % 2 == 0 else 7
                    wge(e, s_t, ev[("sdm", tc, dcs)])
                    wge(e, s_s, evget(("sdev", tc, dcs - 1), ("sdev", tc - 1, DCS - 1)))
                    if tc > 0:
                        wge(e, s_v, ev[("sdxv", tc - 1, dcs)])
                        wge(e, s_s, ev[("sexp", tc - 1, N16 - 1, dcs)])
                    s_s.inc(e.activation(sexp32[:, :], pb[bank][:, :], AF.Exp, bias=dtb[:, dcs:dcs + 1]), 1)
                    wge(e, s_s, s_s.n)
                    s_s.inc(e.activation(delta16[:, dcs * TC:(dcs + 1) * TC], sexp32[:, :],
                                         AF.Ln, bias=1.0), 1)
                    ev[("sdev", tc, dcs)] = s_s.n

            for dcs in range(DCS):
                @blk.vector
                def _(e, tc=tc, dcs=dcs, tpar=tpar):
                    pxr = dcs % 2
                    wge(e, m_xsc[pxr], ev[("sxl", tc, dcs)])
                    wge(e, m_mur[tpar], ev[("mul", tc)])
                    wge(e, s_s, ev[("sdev", tc, dcs)])
                    if dcs > 0:
                        wge(e, s_v, ev[("sdxv", tc, dcs - 1)])
                    s_v.inc(e.tensor_tensor(xn0[:, :], xsc32[:, pxr * TC:(pxr + 1) * TC],
                                            murs[:, tpar * TC:(tpar + 1) * TC], OP.subtract), 1)
                    wge(e, s_v, s_v.n)
                    s_v.inc(e.tensor_tensor(xns16[:, :], xn0[:, :],
                                            rsrs[:, tpar * TC:(tpar + 1) * TC], OP.mult), 1)
                    wge(e, s_v, s_v.n)
                    s_v.inc(e.tensor_tensor(dx16[:, dcs * TC:(dcs + 1) * TC],
                                            delta16[:, dcs * TC:(dcs + 1) * TC],
                                            xns16[:, :], OP.mult), 1)
                    ev[("sdxv", tc, dcs)] = s_v.n

                if dcs < 2:
                    @blk.sync
                    def _(e, tc=tc, dcs=dcs):
                        m = dcs + 2
                        wge(e, s_v, ev[("sdxv", tc, dcs)])
                        m_xsc[m % 2].inc(e.dma_start(
                            out=xsc32[:, (m % 2) * TC:(m % 2 + 1) * TC],
                            in_=x_scan[m * 128:(m + 1) * 128, tc * TC:(tc + 1) * TC]), 16)
                        ev[("sxl", tc, m)] = m_xsc[m % 2].n

            for n in (0, 1):
                for dcs in range(DCS):
                    @blk.scalar
                    def _(e, tc=tc, n=n, dcs=dcs):
                        slot = (n % 2) * 4 + dcs
                        wge(e, s_v, ev["c_aneg"])
                        wge(e, s_s, ev[("sdev", tc, dcs)])
                        if tc > 0:
                            wge(e, s_v, ev[("sscan", tc - 1, N16 - 2 + n, dcs)])
                        s_s.inc(e.activation(a32[:, slot * TC:(slot + 1) * TC],
                                             delta16[:, dcs * TC:(dcs + 1) * TC],
                                             AF.Exp, scale=aneg[:, dcs * 16 + n:dcs * 16 + n + 1]), 1)
                        ev[("sexp", tc, n, dcs)] = s_s.n

            for n in range(N16):
                np_ = n % 2

                @blk.vector
                def _(e, tc=tc, n=n, np_=np_):
                    if tc == 0 and n == 1:
                        wge(e, s_t, ev[("Astatm", NTCQ - 1)])
                    if n < 2:
                        wge(e, m_bc[np_], ev[("sbl", tc, n)])
                        wge(e, s_s, ev[("sexp", tc, n, DCS - 1)])
                    w_u4 = max(evget(("sscan", tc, n - 1, DCS - 1),
                                     ("sscan", tc - 1, N16 - 1, DCS - 1)),
                               ev[("sdxv", tc, DCS - 1)])
                    w_hst = max(evget(("shc", tc, n - 1), ("shc", tc - 1, N16 - 1)),
                                evget(("shc", tc - 1, n)))
                    for dcs in range(DCS):
                        i = e.tensor_tensor(u4[:, dcs * TC:(dcs + 1) * TC],
                                            dx16[:, dcs * TC:(dcs + 1) * TC],
                                            brep[:, np_ * TC:(np_ + 1) * TC], OP.mult)
                        if dcs == 0 and w_u4 > 0:
                            i._wait_ge(s_v.h, w_u4)
                        if dcs == DCS - 1 and w_hst > 0:
                            i._wait_ge(s_g.h, w_hst)
                        s_v.inc(i, 1)
                    ev[("su4", tc, n)] = s_v.n
                    w_sya = evget(("sya", tc, n - 1, DCS - 1),
                                  ("sya", tc - 1, N16 - 1, DCS - 1))
                    for dcs in range(DCS):
                        slot = np_ * 4 + dcs
                        init = 0.0 if tc == 0 else hstate[:, dcs * 16 + n:dcs * 16 + n + 1]
                        i = e.tensor_tensor_scan(h16[:, dcs * TC:(dcs + 1) * TC],
                                                 a32[:, slot * TC:(slot + 1) * TC],
                                                 u4[:, dcs * TC:(dcs + 1) * TC], init, OP.mult, OP.add)
                        if dcs == 0:
                            i._wait_ge(s_v.h, ev[("su4", tc, n)])
                        if dcs == DCS - 1 and w_sya > 0:
                            i._wait_ge(s_g.h, w_sya)
                        s_v.inc(i, 1)
                        ev[("sscan", tc, n, dcs)] = s_v.n
                    for dcs in range(DCS):
                        if n == 0:
                            i = e.tensor_tensor(yacc[:, dcs * TC:(dcs + 1) * TC],
                                                h16[:, dcs * TC:(dcs + 1) * TC],
                                                crep[:, np_ * TC:(np_ + 1) * TC], OP.mult)
                        else:
                            i = e.tensor_tensor(tmp16[:, dcs * TC:(dcs + 1) * TC],
                                                h16[:, dcs * TC:(dcs + 1) * TC],
                                                crep[:, np_ * TC:(np_ + 1) * TC], OP.mult)
                        if dcs == 0:
                            i._wait_ge(s_v.h, ev[("sscan", tc, n, DCS - 1)])
                        if dcs == 1 and n + 1 < N16:
                            i._wait_ge(m_bc[(n + 1) % 2].h, ev[("sbl", tc, n + 1)])
                        if dcs == 2 and n + 1 < N16:
                            i._wait_ge(s_s.h, ev[("sexp", tc, n + 1, DCS - 1)])
                        s_v.inc(i, 1)
                        ev[("sym", tc, n, dcs)] = s_v.n
                    ev[("snv", tc, n)] = s_v.n

                @blk.gpsimd
                def _(e, tc=tc, n=n):
                    i = e.tensor_copy(hstate[:, n:64:16], h16[:, TC - 1: DCS * TC: TC])
                    i._wait_ge(s_v.h, ev[("sscan", tc, n, DCS - 1)])
                    s_g.inc(i, 1)
                    ev[("shc", tc, n)] = s_g.n
                    if n > 0:
                        for dcs in range(DCS):
                            i = e.tensor_tensor(yacc[:, dcs * TC:(dcs + 1) * TC],
                                                yacc[:, dcs * TC:(dcs + 1) * TC],
                                                tmp16[:, dcs * TC:(dcs + 1) * TC], OP.add)
                            if dcs == 0:
                                i._wait_ge(s_v.h, ev[("sym", tc, n, DCS - 1)])
                            s_g.inc(i, 1)
                            ev[("sya", tc, n, dcs)] = s_g.n

                if n + 2 < N16:
                    @blk.sync
                    def _(e, tc=tc, n=n, rnk=rnk, tl=tl):
                        m = n + 2
                        wge(e, s_v, ev[("snv", tc, n)])
                        m_bc[m % 2].inc(e.dma_start(
                            out=brep[:, (m % 2) * TC:(m % 2 + 1) * TC],
                            in_=p_out.ap()[rnk * PR + 128 + m: rnk * PR + 129 + m, tl:tl + TC].broadcast_to([128, TC])), 16)
                        m_bc[m % 2].inc(e.dma_start(
                            out=crep[:, (m % 2) * TC:(m % 2 + 1) * TC],
                            in_=p_out.ap()[rnk * PR + 144 + m: rnk * PR + 145 + m, tl:tl + TC].broadcast_to([128, TC])), 16)
                        ev[("sbl", tc, m)] = m_bc[m % 2].n

                    @blk.scalar
                    def _(e, tc=tc, n=n):
                        m = n + 2
                        for dcs in range(DCS):
                            slot = (m % 2) * 4 + dcs
                            i = e.activation(a32[:, slot * TC:(slot + 1) * TC],
                                             delta16[:, dcs * TC:(dcs + 1) * TC],
                                             AF.Exp, scale=aneg[:, dcs * 16 + m:dcs * 16 + m + 1])
                            if dcs == 0:
                                i._wait_ge(s_v.h, ev[("sscan", tc, n, DCS - 1)])
                            s_s.inc(i, 1)
                            ev[("sexp", tc, m, dcs)] = s_s.n

            @blk.vector
            def _(e, tc=tc):
                if tc >= 1:
                    wge(e, st_y16, ev[("syst", tc - 1)])
                for dcs in range(DCS):
                    wge(e, s_g, ev[("sya", tc, N16 - 1, dcs)])
                    s_v.inc(e.tensor_copy(y16s[:, dcs * TC:(dcs + 1) * TC],
                                          yacc[:, dcs * TC:(dcs + 1) * TC]), 1)
                ev[("syc", tc)] = s_v.n

            @blk.scalar
            def _(e, tc=tc):
                wge(e, s_v, ev[("syc", tc)])
                for dcs in range(DCS):
                    st_y16.inc(e.dma_start(
                        out=y_in.ap()[dcs * 128:(dcs + 1) * 128, tc * TC:(tc + 1) * TC],
                        in_=y16s[:, dcs * TC:(dcs + 1) * TC]), 16)
                ev[("syst", tc)] = st_y16.n

            for k in range(16):
                u = tc * 16 + k
                j, tcq = u // 4, u % 4
                bank = d1banks[u % 6]

                if tcq == 0:
                    @blk.sync
                    def _(e, j=j):
                        if j < 2:
                            wge(e, qg_w0, ev["w0cast"])
                        else:
                            wge(e, s_t, ev[("d1m", (j - 2) * 4 + 3)])
                        src = w0_sc.ap()[j * DC * 128:(j + 1) * DC * 128, :].rearrange("(a p) b -> p a b", p=128)
                        m_w0[j % 2].inc(e.dma_start(out=w016[:, (j % 2) * 2048:(j % 2 + 1) * 2048], in_=src), 16)
                        ev[("w0l", j)] = m_w0[j % 2].n

                @blk.tensor
                def _(e, u=u, j=j, tcq=tcq, bank=bank):
                    wge(e, m_w0[j % 2], ev[("w0l", j)])
                    wge(e, s_s, evget(("d1e", u - 6)))
                    if u < 6:
                        wge(e, s_v, ev[("A2v_last", NTCQ - 1)])
                        wge(e, s_s, ev[("Apev", NTCQ - 1)])
                    for dc in range(DC):
                        i = e.matmul(pb[bank][:, :],
                                     w016[:, (j % 2) * 2048 + dc * 128:(j % 2) * 2048 + (dc + 1) * 128],
                                     big[:, dc * 2048 + tcq * TC: dc * 2048 + (tcq + 1) * TC],
                                     start=(dc == 0), stop=(dc == DC - 1))
                    s_t.inc(i, 1)
                    ev[("d1m", u)] = s_t.n

                @blk.scalar
                def _(e, u=u, j=j, tcq=tcq, bank=bank):
                    kk = u % 16
                    par = u % 2
                    wge(e, s_t, ev[("d1m", u)])
                    if u >= 16 and u % 16 == 0:
                        wge(e, st_gst[0], ev[("gst", u - 2)])
                        wge(e, st_gst[1], ev[("gst", u - 1)])
                    s_s.inc(e.activation(d1st[:, kk * TC:(kk + 1) * TC], pb[bank][:, :],
                                         AF.Identity, bias=b0[:, j:j + 1]), 1)
                    ev[("d1e", u)] = s_s.n
                    wge(e, s_s, ev[("d1e", u)])
                    st_gst[par].inc(e.dma_start(
                        out=g_sc.ap()[j * 128:(j + 1) * 128, tcq * TC:(tcq + 1) * TC],
                        in_=d1st[:, kk * TC:(kk + 1) * TC]), 16)
                    ev[("gst", u)] = st_gst[par].n

        @blk.gpsimd
        def _(e):
            wge(e, st_y16, ev[("syst", NTC - 1)])
            s_cc.inc(e.collective_compute(
                "AllGather", OP.bypass,
                replica_groups=[[0, 1, 2, 3], [4, 5, 6, 7]],
                ins=[y_in.ap().opt()],
                outs=[y_out.ap().opt()],
            ), 1)
            ev["ag2"] = s_cc.n

        for tc in range(NTCQ):
            @blk.sync
            def _(e, tc=tc):
                wge(e, s_t, ev[("d1m", NTC * 16 - 1)])
                if tc > 0:
                    wge(e, s_t, ev[("e_m", tc - 1, NDO - 1)])
                wge(e, st_gst[0], ev[("gst", NTC * 16 - 2)])
                wge(e, st_gst[1], ev[("gst", NTC * 16 - 1)])
                wge(e, st_xn, ev[("Axn", NTCQ - 1, DC - 1)])
                for b8 in range(8):
                    src = g_sc.ap()[b8 * 8 * 128:(b8 + 1) * 8 * 128, tc * TC:(tc + 1) * TC]
                    src = src.rearrange("(a p) b -> p a b", p=128)
                    m_egl.inc(e.dma_start(out=big[:, b8 * 8 * TC:(b8 + 1) * 8 * TC], in_=src), 16)
                ev[("e_gl", tc)] = m_egl.n

            @blk.scalar
            def _(e, tc=tc):
                wge(e, m_egl, ev[("e_gl", tc)])
                for b8 in range(8):
                    s_s.inc(e.activation(big[:, b8 * 8 * TC:(b8 + 1) * 8 * TC],
                                         big[:, b8 * 8 * TC:(b8 + 1) * 8 * TC], AF.Gelu), 1)
                ev[("e_gelu", tc)] = s_s.n

            for do in range(NDO):
                for hh in range(2):
                    @blk.sync
                    def _(e, tc=tc, do=do, hh=hh):
                        key = ("e_mid" if hh == 0 else "e_m")
                        wge(e, s_t, evget((key, tc, do - 1), (key, tc - 1, NDO - 1)))
                        wge(e, qg_w1, ev["w1cast"])
                        base = do * NJ * 128
                        src = w1_sc.ap()[base + hh * 32 * 128: base + (hh + 1) * 32 * 128, :]
                        src = src.rearrange("(a p) b -> p a b", p=128)
                        m_w1b[hh].inc(e.dma_start(out=w1buf[:, hh * 4096:(hh + 1) * 4096], in_=src), 16)
                        ev[("e_wl", tc, do, hh)] = m_w1b[hh].n

                @blk.tensor
                def _(e, tc=tc, do=do):
                    bank = 2 if do % 2 == 0 else 3
                    wge(e, m_egl, ev[("e_gl", tc)])
                    wge(e, s_s, ev[("e_gelu", tc)])
                    wge(e, s_s, evget(("e_ev", tc, do - 2), ("e_ev", tc - 1, NDO - 2 + (do % 2))))
                    wge(e, m_w1b[0], ev[("e_wl", tc, do, 0)])
                    for j in range(NJ):
                        hh = j // 32
                        if j == 32:
                            wge(e, m_w1b[1], ev[("e_wl", tc, do, 1)])
                        i = e.matmul(pb[bank][:, :],
                                     w1buf[:, hh * 4096 + (j - hh * 32) * 128: hh * 4096 + (j - hh * 32 + 1) * 128],
                                     big[:, j * TC:(j + 1) * TC],
                                     start=(j == 0), stop=(j == NJ - 1))
                        if j == 31:
                            s_t.inc(i, 1)
                            ev[("e_mid", tc, do)] = s_t.n
                    s_t.inc(i, 1)
                    ev[("e_m", tc, do)] = s_t.n

                @blk.scalar
                def _(e, tc=tc, do=do):
                    bank = 2 if do % 2 == 0 else 3
                    par = do % 2
                    wge(e, s_t, ev[("e_m", tc, do)])
                    wge(e, st_f[par], evget(("e_fs", tc, do - 2), ("e_fs", tc - 1, NDO - 2 + (do % 2))))
                    s_s.inc(e.activation(fstage[:, par * TC:(par + 1) * TC], pb[bank][:, :],
                                         AF.Identity, bias=b1[:, do:do + 1]), 1)
                    ev[("e_ev", tc, do)] = s_s.n
                    wge(e, s_s, ev[("e_ev", tc, do)])
                    st_f[par].inc(e.dma_start(
                        out=f_sc.ap()[do * 128:(do + 1) * 128, tc * TC:(tc + 1) * TC],
                        in_=fstage[:, par * TC:(par + 1) * TC]), 16)
                    ev[("e_fs", tc, do)] = st_f[par].n

        @blk.sync
        def _(e):
            wge(e, s_cc, ev["ag2"])
            wge(e, s_t, ev[("e_m", NTCQ - 1, NDO - 1)])
            pid = e.partition_id()
            qoff = (pid % 4) * TQ
            for dc in range(DC):
                m_cyl.inc(e.dma_start(
                    out=big[:, dc * 2048:(dc + 1) * 2048],
                    in_=y_out.ap()[dc * 128:(dc + 1) * 128, bass.ds(qoff, TQ)]), 16)
            ev["c_yl"] = m_cyl.n

        def prevC(name, do, tc):
            if tc >= 2:
                return ev.get((name, do, tc - 2), 0)
            if do >= 1:
                return ev.get((name, do - 1, tc + 2), 0)
            return 0

        for do in range(NDO):
            @blk.sync
            def _(e, do=do):
                if do < 2:
                    wge(e, qg_wo, ev["wocast"])
                    wge(e, s_t, ev[("d1m", NTC * 16 - 1)])
                else:
                    wge(e, s_t, ev[("c_m", do - 2, NTCQ - 1)])
                src = wout_sc.ap()[do * DC * 128:(do + 1) * DC * 128, :].rearrange("(a p) b -> p a b", p=128)
                m_w0[do % 2].inc(e.dma_start(out=w016[:, (do % 2) * 2048:(do % 2 + 1) * 2048], in_=src), 16)
                ev[("c_wl", do)] = m_w0[do % 2].n

            for tc in range(NTCQ):
                @blk.tensor
                def _(e, do=do, tc=tc):
                    wge(e, m_w0[do % 2], ev[("c_wl", do)])
                    wge(e, m_cyl, ev["c_yl"])
                    if do == 0:
                        wge(e, s_s, ev[("e_ev", NTCQ - 1, NDO - 1)])
                    else:
                        wge(e, s_s, ev[("c_ev", do - 1, tc)])
                    for dc in range(DC):
                        i = e.matmul(pb[tc][:, :],
                                     w016[:, (do % 2) * 2048 + dc * 128:(do % 2) * 2048 + (dc + 1) * 128],
                                     big[:, dc * 2048 + tc * TC: dc * 2048 + (tc + 1) * TC],
                                     start=(dc == 0), stop=(dc == DC - 1))
                    s_t.inc(i, 1)
                    ev[("c_m", do, tc)] = s_t.n

                @blk.scalar
                def _(e, do=do, tc=tc):
                    par = tc % 2
                    wge(e, s_t, ev[("c_m", do, tc)])
                    wge(e, s_v, prevC("c_v", do, tc))
                    s_s.inc(e.activation(a32[:, 1024 + par * TC: 1024 + (par + 1) * TC], pb[tc][:, :],
                                         AF.Relu, bias=wob[:, do:do + 1]), 1)
                    ev[("c_ev", do, tc)] = s_s.n

                @blk.sync
                def _(e, do=do, tc=tc):
                    par = tc % 2
                    wge(e, s_v, prevC("c_v", do, tc))
                    wge(e, st_f[0], ev[("e_fs", NTCQ - 1, NDO - 2)])
                    wge(e, st_f[1], ev[("e_fs", NTCQ - 1, NDO - 1)])
                    m_cld[par].inc(e.dma_start(out=a32[:, par * TC:(par + 1) * TC],
                                               in_=x_rows[do * 128:(do + 1) * 128, tc * TC:(tc + 1) * TC]), 16)
                    m_cld[par].inc(e.dma_start(out=a32[:, 3584 + 256 * par: 3584 + 256 * (par + 1)].bitcast(F16),
                                               in_=xn_sc.ap()[do * 128:(do + 1) * 128, tc * TC:(tc + 1) * TC]), 16)
                    m_cld[par].inc(e.dma_start(out=a32[:, 3072 + 256 * par: 3072 + 256 * (par + 1)].bitcast(F16),
                                               in_=f_sc.ap()[do * 128:(do + 1) * 128, tc * TC:(tc + 1) * TC]), 16)
                    ev[("c_ld", do, tc)] = m_cld[par].n

                @blk.vector
                def _(e, do=do, tc=tc):
                    par = tc % 2
                    wge(e, s_s, ev[("c_ev", do, tc)])
                    wge(e, m_cld[par], ev[("c_ld", do, tc)])
                    wge(e, st_out[par], prevC("c_st", do, tc))
                    if (do, tc) == (0, 0):
                        wge(e, s_v, ev[("snv", NTC - 1, N16 - 1)])
                    s_v.inc(e.scalar_tensor_tensor(a32[:, 2048 + par * TC: 2048 + (par + 1) * TC],
                                           a32[:, 3584 + 256 * par: 3584 + 256 * (par + 1)].bitcast(F16),
                                           dvec[:, do:do + 1],
                                           a32[:, 1024 + par * TC: 1024 + (par + 1) * TC],
                                           OP.mult, OP.add), 1)
                    wge(e, s_v, s_v.n)
                    s_v.inc(e.tensor_tensor(a32[:, 2048 + par * TC: 2048 + (par + 1) * TC],
                                    a32[:, 2048 + par * TC: 2048 + (par + 1) * TC],
                                    a32[:, par * TC:(par + 1) * TC], OP.add), 1)
                    wge(e, s_v, s_v.n)
                    s_v.inc(e.tensor_tensor(a32[:, 2048 + par * TC: 2048 + (par + 1) * TC],
                                            a32[:, 2048 + par * TC: 2048 + (par + 1) * TC],
                                            a32[:, 3072 + 256 * par: 3072 + 256 * (par + 1)].bitcast(F16), OP.add), 1)
                    ev[("c_v", do, tc)] = s_v.n

                @blk.scalar
                def _(e, do=do, tc=tc):
                    par = tc % 2
                    wge(e, s_v, ev[("c_v", do, tc)])
                    st_out[par].inc(e.dma_start(
                        out=out_d[do * 128:(do + 1) * 128, tc * TC:(tc + 1) * TC],
                        in_=a32[:, 2048 + par * TC: 2048 + (par + 1) * TC]), 16)
                    ev[("c_st", do, tc)] = st_out[par].n

        if DEBUG:
            @blk.sync
            def _(e):
                wge(e, st_out[0], st_out[0].n)
                wge(e, st_out[1], st_out[1].n)
                for dst, src in [(dbg_yin, y_in.ap()), (dbg_yout, y_out.ap()),
                                 (dbg_f, f_sc.ap()), (dbg_pout, p_out.ap()),
                                 (dbg_xn, xn_sc.ap())]:
                    st_dbg.inc(e.dma_start(out=dst, in_=src), 16)
                wge(e, st_dbg, st_dbg.n)

        @blk.sync
        def _(e):
            e.wait_ge(st_out[0].h, st_out[0].n)
            e.wait_ge(st_out[1].h, st_out[1].n)
            e.wait_ge(st_y16.h, st_y16.n)
            e.wait_ge(st_pin.h, st_pin.n)
            e.wait_ge(st_xn.h, st_xn.n)
            e.wait_ge(st_f[0].h, st_f[0].n)
            e.wait_ge(st_f[1].h, st_f[1].n)
            e.wait_ge(st_gst[0].h, st_gst[0].n)
            e.wait_ge(st_gst[1].h, st_gst[1].n)
            e.wait_ge(qg_w0.h, qg_w0.n)
            e.wait_ge(qg_w1.h, qg_w1.n)
            e.wait_ge(qg_wo.h, qg_wo.n)
            e.wait_ge(qg_sb.h, qg_sb.n)

    return nc


_NC_CACHE = {}


def make_maps_inner(x, xpw, dtpw, dt_bias, A_log, Dv, Wout, outb, W0, bias0, W1, bias1):
    C = np.ascontiguousarray
    wx_t = C(xpw.T)
    w0_lay = C(W0.T.reshape(16, 128, 64, 128).transpose(2, 0, 1, 3).reshape(64 * 16 * 128, 128))
    w1_lay = C(W1.T.reshape(64, 128, 16, 128).transpose(2, 0, 1, 3).reshape(16 * 64 * 128, 128))
    wout_lay = C(Wout.T.reshape(16, 128, 16, 128).transpose(2, 0, 1, 3).reshape(16 * 16 * 128, 128))
    b0_l = C(bias0.reshape(64, 128).T)
    b1_l = C(bias1.reshape(16, 128).T)
    d_l = C(Dv.reshape(16, 128).T)
    woutb_l = C(outb.reshape(16, 128).T)
    in_maps = []
    for c in range(8):
        bi, qi = c // 4, c % 4
        sl = slice(qi * DSL, (qi + 1) * DSL)
        in_maps.append({
            "x_rows": C(x[bi, qi * TQ:(qi + 1) * TQ, :].T),
            "x_scan": C(x[bi, :, sl].T),
            "wx_t": wx_t,
            "dtpw_t": C(dtpw[sl, :].T),
            "dt_b": C(dt_bias[sl].reshape(4, 128).T),
            "a_log": C(A_log[sl].reshape(4, 128, 16).transpose(1, 0, 2).reshape(128, 64)),
            "d_vec": d_l,
            "wout_lay": wout_lay,
            "woutb": woutb_l,
            "w0_lay": w0_lay,
            "b0_in": b0_l,
            "w1_lay": w1_lay,
            "b1_in": b1_l,
        })
    return in_maps


def run_full(in_maps):
    if "nc" not in _NC_CACHE:
        _NC_CACHE["nc"] = build_nc()
    nc = _NC_CACHE["nc"]
    return run_bass_kernel_spmd(nc, in_maps, core_ids=list(range(8)))


def kernel(L0_attn_hidden_states, L0_attn_x_proj_weight, L0_attn_dt_proj_weight,
           L0_attn_dt_proj_bias, L0_attn_A_log, L0_attn_D, L0_attn_out_proj_weight,
           L0_attn_out_proj_bias, L0_ffn_W0, L0_ffn_bias0, L0_ffn_W1, L0_ffn_bias1):
    x = np.asarray(L0_attn_hidden_states, dtype=np.float32)
    in_maps = make_maps_inner(
        x,
        np.asarray(L0_attn_x_proj_weight, dtype=np.float32),
        np.asarray(L0_attn_dt_proj_weight, dtype=np.float32),
        np.asarray(L0_attn_dt_proj_bias, dtype=np.float32),
        np.asarray(L0_attn_A_log, dtype=np.float32),
        np.asarray(L0_attn_D, dtype=np.float32),
        np.asarray(L0_attn_out_proj_weight, dtype=np.float32),
        np.asarray(L0_attn_out_proj_bias, dtype=np.float32),
        np.asarray(L0_ffn_W0, dtype=np.float32),
        np.asarray(L0_ffn_bias0, dtype=np.float32),
        np.asarray(L0_ffn_W1, dtype=np.float32),
        np.asarray(L0_ffn_bias1, dtype=np.float32),
    )
    res = run_full(in_maps)
    out = np.empty((2, T, DM), dtype=np.float32)
    for c in range(8):
        bi, qi = c // 4, c % 4
        out[bi, qi * TQ:(qi + 1) * TQ, :] = np.asarray(res.results[c]["out"]).T
    return out


# version 11 (final)
# speedup vs baseline: 1.1510x; 1.0128x over previous
import numpy as np
from contextlib import ExitStack

import concourse.bass as bass
import concourse.mybir as mybir
from concourse.bass_utils import run_bass_kernel_spmd

F32 = mybir.dt.float32
F16 = mybir.dt.float16
AF = mybir.ActivationFunctionType
OP = mybir.AluOpType

DM = 2048
T = 8192
TQ = 2048
DSL = 512
TC = 512
DC = 16
DCS = 4
NTCQ = 4
NTC = 16
N16 = 16
NJ = 64
NDO = 16
PR = 164

DEBUG = False
CAST_BARRIER = False


class Sem:
    def __init__(self, nc, stack, name):
        self.h = stack.enter_context(nc.semaphore(name))
        self.n = 0

    def inc(self, instr, k):
        instr.then_inc(self.h, k)
        self.n += k
        return self.n


def build_nc():
    nc = bass.Bass("TRN2", target_bir_lowering=False)

    x_rows = nc.dram_tensor("x_rows", [DM, TQ], F32, kind="ExternalInput").ap()
    x_scan = nc.dram_tensor("x_scan", [DSL, T], F32, kind="ExternalInput").ap()
    wx_t = nc.dram_tensor("wx_t", [DM, 160], F32, kind="ExternalInput").ap()
    dtpw_t = nc.dram_tensor("dtpw_t", [128, DSL], F32, kind="ExternalInput").ap()
    dt_b = nc.dram_tensor("dt_b", [128, 4], F32, kind="ExternalInput").ap()
    a_log = nc.dram_tensor("a_log", [128, 64], F32, kind="ExternalInput").ap()
    d_vec = nc.dram_tensor("d_vec", [128, 16], F32, kind="ExternalInput").ap()
    wout_lay = nc.dram_tensor("wout_lay", [NDO * DC * 128, 128], F32, kind="ExternalInput").ap()
    woutb = nc.dram_tensor("woutb", [128, 16], F32, kind="ExternalInput").ap()
    w0_lay = nc.dram_tensor("w0_lay", [NJ * DC * 128, 128], F32, kind="ExternalInput").ap()
    b0_in = nc.dram_tensor("b0_in", [128, 64], F32, kind="ExternalInput").ap()
    w1_lay = nc.dram_tensor("w1_lay", [NDO * NJ * 128, 128], F32, kind="ExternalInput").ap()
    b1_in = nc.dram_tensor("b1_in", [128, 16], F32, kind="ExternalInput").ap()

    out_d = nc.dram_tensor("out", [DM, TQ], F32, kind="ExternalOutput").ap()
    if DEBUG:
        dbg_yin = nc.dram_tensor("dbg_yin", [DSL, T], F16, kind="ExternalOutput").ap()
        dbg_yout = nc.dram_tensor("dbg_yout", [DM, T], F16, kind="ExternalOutput").ap()
        dbg_f = nc.dram_tensor("dbg_f", [DM, TQ], F16, kind="ExternalOutput").ap()
        dbg_pout = nc.dram_tensor("dbg_pout", [4 * PR, TQ], F16, kind="ExternalOutput").ap()
        dbg_xn = nc.dram_tensor("dbg_xn", [DM, TQ], F16, kind="ExternalOutput").ap()

    w0_sc = nc.dram_tensor("w0_sc", [NJ * DC * 128, 128], F16)
    w1_sc = nc.dram_tensor("w1_sc", [NDO * NJ * 128, 128], F16)
    wout_sc = nc.dram_tensor("wout_sc", [NDO * DC * 128, 128], F16)
    xn_sc = nc.dram_tensor("xn_sc", [DM, TQ], F16)
    g_sc = nc.dram_tensor("g_sc", [NJ * 128, TQ], F16)
    f_sc = nc.dram_tensor("f_sc", [DM, TQ], F16)
    p_in = nc.dram_tensor("p_in", [PR, TQ], F16)
    p_out = nc.dram_tensor("p_out", [4 * PR, TQ], F16)
    y_in = nc.dram_tensor("y_in", [DSL, T], F16)
    y_out = nc.dram_tensor("y_out", [DM, T], F16)

    with ExitStack() as stack:
        ec = stack.enter_context

        wx16 = ec(nc.sbuf_tensor("wx16", [128, DC * 160], F16))
        dtpw16 = ec(nc.sbuf_tensor("dtpw16", [128, DSL], F16))
        aneg = ec(nc.sbuf_tensor("aneg", [128, 64], F32))
        dtb = ec(nc.sbuf_tensor("dtb", [128, 4], F32))
        dvec = ec(nc.sbuf_tensor("dvec", [128, 16], F32))
        wob = ec(nc.sbuf_tensor("wob", [128, 16], F32))
        b0 = ec(nc.sbuf_tensor("b0", [128, 64], F32))
        b1 = ec(nc.sbuf_tensor("b1", [128, 16], F32))
        onesc = ec(nc.sbuf_tensor("onesc", [128, 1], F16))
        onesr = ec(nc.sbuf_tensor("onesr", [1, 128], F16))
        big = ec(nc.sbuf_tensor("big", [128, 32768], F16))
        w016 = ec(nc.sbuf_tensor("w016", [128, 2 * 2048], F16))
        w1buf = ec(nc.sbuf_tensor("w1buf", [128, 2 * 4096], F16))
        fstage = ec(nc.sbuf_tensor("fstage", [128, 2 * TC], F16))
        d1st = ec(nc.sbuf_tensor("d1st", [128, 16 * TC], F16))
        mis1 = ec(nc.sbuf_tensor("mis1", [128, 64], F32))
        dtl2 = ec(nc.sbuf_tensor("dtl2", [128, 2 * TC], F16))
        delta16 = ec(nc.sbuf_tensor("delta16", [128, DCS * TC], F16))
        dx16 = ec(nc.sbuf_tensor("dx16", [128, DCS * TC], F16))
        xns16 = ec(nc.sbuf_tensor("xns16", [128, TC], F16))
        xsc32 = ec(nc.sbuf_tensor("xsc32", [128, 2 * TC], F32))
        murs = ec(nc.sbuf_tensor("murs", [128, 2 * TC], F16))
        rsrs = ec(nc.sbuf_tensor("rsrs", [128, 2 * TC], F16))
        brep = ec(nc.sbuf_tensor("brep", [128, 2 * TC], F16))
        crep = ec(nc.sbuf_tensor("crep", [128, 2 * TC], F16))
        a32 = ec(nc.sbuf_tensor("a32", [128, 8 * TC], F32))
        u4 = ec(nc.sbuf_tensor("u4", [128, DCS * TC], F16))
        h16 = ec(nc.sbuf_tensor("h16", [128, DCS * TC], F16))
        tmp16 = ec(nc.sbuf_tensor("tmp16", [128, DCS * TC], F16))
        yacc = ec(nc.sbuf_tensor("yacc", [128, DCS * TC], F32))
        y16s = ec(nc.sbuf_tensor("y16s", [128, DCS * TC], F16))
        hstate = ec(nc.sbuf_tensor("hstate", [128, 64], F32))
        sexp32 = ec(nc.sbuf_tensor("sexp32", [128, TC], F16))
        st_a = ec(nc.sbuf_tensor("st_a", [1, TC], F32))
        st_b = ec(nc.sbuf_tensor("st_b", [1, TC], F32))
        st_c = ec(nc.sbuf_tensor("st_c", [1, TC], F32))
        st_d = ec(nc.sbuf_tensor("st_d", [1, TC], F32))
        mu16 = ec(nc.sbuf_tensor("mu16", [1, TC], F16))
        rstd16 = ec(nc.sbuf_tensor("rstd16", [1, TC], F16))
        dtlst = ec(nc.sbuf_tensor("dtlst", [128, TC], F16))
        bcst = ec(nc.sbuf_tensor("bcst", [32, TC], F16))
        xn0 = ec(nc.sbuf_tensor("xn0", [128, TC], F32))

        pb = [ec(nc.psum_tensor(f"pb{i}", [128, 512], F32)) for i in range(8)]

        s_t = Sem(nc, stack, "s_t")
        s_s = Sem(nc, stack, "s_s")
        s_v = Sem(nc, stack, "s_v")
        s_g = Sem(nc, stack, "s_g")
        s_cc = Sem(nc, stack, "s_cc")
        m_cst = Sem(nc, stack, "m_cst")
        m_xsc = [Sem(nc, stack, f"m_xsc{i}") for i in range(2)]
        m_mur = [Sem(nc, stack, f"m_mur{i}") for i in range(2)]
        m_bc = [Sem(nc, stack, f"m_bc{i}") for i in range(2)]
        m_dtl = [Sem(nc, stack, f"m_dtl{i}") for i in range(2)]
        m_w0 = [Sem(nc, stack, f"m_w0{i}") for i in range(2)]
        qg_w0 = Sem(nc, stack, "qg_w0")
        qg_w1 = Sem(nc, stack, "qg_w1")
        qg_wo = Sem(nc, stack, "qg_wo")
        qg_sb = Sem(nc, stack, "qg_sb")
        m_w1b = [Sem(nc, stack, f"m_w1b{i}") for i in range(2)]
        m_egl = Sem(nc, stack, "m_egl")
        m_cyl = Sem(nc, stack, "m_cyl")
        m_cld = [Sem(nc, stack, f"m_cld{i}") for i in range(2)]
        st_pin = Sem(nc, stack, "st_pin")
        st_xn = Sem(nc, stack, "st_xn")
        st_gst = [Sem(nc, stack, f"st_gst{i}") for i in range(2)]
        st_y16 = Sem(nc, stack, "st_y16")
        st_f = [Sem(nc, stack, f"st_f{i}") for i in range(2)]
        st_out = [Sem(nc, stack, f"st_out{i}") for i in range(2)]
        q_g = Sem(nc, stack, "q_g")
        st_dbg = Sem(nc, stack, "st_dbg")

        blk = ec(nc.Block())
        ev = {}

        def wge(eng, sem, val):
            assert val <= sem.n, f"wait {val} > emitted {sem.n}"
            if val > 0:
                eng.wait_ge(sem.h, val)

        def evget(*keys):
            for k in keys:
                if k in ev:
                    return ev[k]
            return 0

        @blk.sync
        def _(e):
            for dst, src in [(dtb, dt_b), (dvec, d_vec), (wob, woutb), (b0, b0_in), (b1, b1_in)]:
                m_cst.inc(e.dma_start(out=dst[:, :], in_=src), 16)
            m_cst.inc(e.dma_start(out=mis1[:, 0:64], in_=a_log), 16)
            ev["c_l1"] = m_cst.n

        @blk.scalar
        def _(e):
            wge(e, m_cst, ev["c_l1"])
            s_s.inc(e.activation(aneg[:, :], mis1[:, 0:64], AF.Exp), 1)
            ev["c_exp"] = s_s.n

        @blk.vector
        def _(e):
            wge(e, s_s, ev["c_exp"])
            s_v.inc(e.tensor_scalar(aneg[:, :], aneg[:, :], -1.0, None, OP.mult), 1)
            ev["c_aneg"] = s_v.n

        @blk.gpsimd
        def _(e):
            s_g.inc(e.memset(onesc[:, :], 1.0), 1)
            s_g.inc(e.memset(onesr[:, :], 1.0), 1)
            qg_sb.inc(e.dma_start(out=dtpw16[:, :], in_=dtpw_t), 16)
            qg_sb.inc(e.dma_start(out=wx16[:, :], in_=wx_t.rearrange("(c p) n -> p c n", p=128)), 16)
            ev["sbcast"] = qg_sb.n
            NB0 = NJ * DC * 128 // 16
            for i in range(16):
                if i >= 2 and i % 2 == 0:
                    e.wait_ge(qg_w0.h, 16 * i)
                qg_w0.inc(e.dma_start(out=w0_sc.ap()[i * NB0:(i + 1) * NB0, :],
                                      in_=w0_lay[i * NB0:(i + 1) * NB0, :]), 16)
            ev["w0cast"] = qg_w0.n
            NBO = NDO * DC * 128 // 4
            for i in range(4):
                if i >= 2 and i % 2 == 0:
                    e.wait_ge(qg_wo.h, 16 * i)
                qg_wo.inc(e.dma_start(out=wout_sc.ap()[i * NBO:(i + 1) * NBO, :],
                                      in_=wout_lay[i * NBO:(i + 1) * NBO, :]), 16)
            ev["wocast"] = qg_wo.n
            NB1 = NDO * NJ * 128 // 16
            for i in range(16):
                if i >= 2 and i % 2 == 0:
                    e.wait_ge(qg_w1.h, 16 * i)
                qg_w1.inc(e.dma_start(out=w1_sc.ap()[i * NB1:(i + 1) * NB1, :],
                                      in_=w1_lay[i * NB1:(i + 1) * NB1, :]), 16)
            ev["w1cast"] = qg_w1.n
        ev["consts_g"] = s_g.n

        if CAST_BARRIER:
            @blk.sync
            def _(e):
                wge(e, qg_w0, ev["w0cast"])
                wge(e, qg_w1, ev["w1cast"])
                wge(e, qg_wo, ev["wocast"])
                wge(e, qg_sb, ev["sbcast"])

        for tc in range(NTCQ):
            for dc in range(DC):
                par = dc % 2

                @blk.sync
                def _(e, tc=tc, dc=dc, par=par):
                    if dc >= 2:
                        wge(e, s_v, ev[("A1v", tc, dc - 2)])
                        wge(e, s_s, ev[("A1s", tc, dc - 2)])
                    elif tc > 0:
                        wge(e, s_v, ev[("A2v_last", tc - 1)])
                    m_xsc[par].inc(e.dma_start(
                        out=xsc32[:, par * TC:(par + 1) * TC],
                        in_=x_rows[dc * 128:(dc + 1) * 128, tc * TC:(tc + 1) * TC]), 16)
                    ev[("A1l", tc, dc)] = m_xsc[par].n

                @blk.vector
                def _(e, tc=tc, dc=dc, par=par):
                    wge(e, m_xsc[par], ev[("A1l", tc, dc)])
                    if dc >= 2:
                        wge(e, s_t, ev[("Astm", tc, dc - 2)])
                    elif tc > 0:
                        wge(e, s_t, ev[("Astatm", tc - 1)])
                    s_v.inc(e.tensor_copy(tmp16[:, par * TC:(par + 1) * TC],
                                          xsc32[:, par * TC:(par + 1) * TC]), 1)
                    ev[("A1v", tc, dc)] = s_v.n

                @blk.scalar
                def _(e, tc=tc, dc=dc, par=par):
                    wge(e, m_xsc[par], ev[("A1l", tc, dc)])
                    if dc >= 2:
                        wge(e, s_t, ev[("Astm", tc, dc - 2)])
                    elif tc > 0:
                        wge(e, s_t, ev[("Astatm", tc - 1)])
                    s_s.inc(e.activation(tmp16[:, (2 + par) * TC:(3 + par) * TC],
                                         xsc32[:, par * TC:(par + 1) * TC], AF.Square), 1)
                    ev[("A1s", tc, dc)] = s_s.n

                @blk.tensor
                def _(e, tc=tc, dc=dc, par=par):
                    wge(e, s_v, ev[("A1v", tc, dc)])
                    wge(e, s_s, ev[("A1s", tc, dc)])
                    if dc == 0:
                        wge(e, s_g, ev["consts_g"])
                        wge(e, qg_sb, ev["sbcast"])
                        if tc > 0:
                            wge(e, s_v, ev[("Astat", tc - 1)])
                    e.matmul(pb[0][0:1, :], onesc[:, :], tmp16[:, par * TC:(par + 1) * TC],
                             start=(dc == 0), stop=(dc == DC - 1), skip_group_check=True)
                    i2 = e.matmul(pb[0][32:33, :], onesc[:, :], tmp16[:, (2 + par) * TC:(3 + par) * TC],
                                  start=(dc == 0), stop=(dc == DC - 1), skip_group_check=True)
                    s_t.inc(i2, 1)
                    ev[("Astm", tc, dc)] = s_t.n
                    if dc == DC - 1:
                        ev[("Astatm", tc)] = s_t.n

            ev[("A1v_last", tc)] = ev[("A1v", tc, DC - 1)]
            ev[("A1s_last", tc)] = ev[("A1s", tc, DC - 1)]

            @blk.vector
            def _(e, tc=tc):
                wge(e, s_t, ev[("Astatm", tc)])
                if tc > 0:
                    wge(e, st_pin, ev[("Apst", tc - 1)])
                    wge(e, s_v, ev[("Astat", tc - 1)])
                s_v.inc(e.tensor_scalar(st_a[:, :], pb[0][0:1, :], 1.0 / DM, None, OP.mult), 1)
                s_v.inc(e.tensor_scalar(st_b[:, :], pb[0][32:33, :], 1.0 / DM, None, OP.mult), 1)
                wge(e, s_v, s_v.n)
                s_v.inc(e.tensor_tensor(st_c[:, :], st_a[:, :], st_a[:, :], OP.mult), 1)
                wge(e, s_v, s_v.n)
                s_v.inc(e.tensor_tensor(st_d[:, :], st_b[:, :], st_c[:, :], OP.subtract), 1)
                wge(e, s_v, s_v.n)
                s_v.inc(e.tensor_scalar(st_d[:, :], st_d[:, :], 1e-5, None, OP.add), 1)
                wge(e, s_v, s_v.n)
                s_v.inc(e.reciprocal(st_c[:, :], st_d[:, :]), 1)
                wge(e, s_v, s_v.n)
                s_v.inc(e.tensor_copy(mu16[:, :], st_a[:, :]), 1)
                ev[("Astat", tc)] = s_v.n

            @blk.scalar
            def _(e, tc=tc):
                wge(e, s_v, ev[("Astat", tc)])
                if tc > 0:
                    wge(e, st_pin, ev[("Apst", tc - 1)])
                s_s.inc(e.activation(rstd16[:, :], st_c[:, :], AF.Sqrt), 1)
                ev[("Arstd", tc)] = s_s.n

            @blk.tensor
            def _(e, tc=tc):
                wge(e, s_s, ev[("Arstd", tc)])
                if tc > 0:
                    wge(e, s_v, ev[("A2v_last", tc - 1)])
                e.matmul(pb[1][:, :], onesr[:, :], mu16[:, :], start=True, stop=True)
                i2 = e.matmul(pb[2][:, :], onesr[:, :], rstd16[:, :], start=True, stop=True)
                s_t.inc(i2, 1)
                ev[("Arep", tc)] = s_t.n

            for dc in range(DC):
                par = dc % 2

                @blk.sync
                def _(e, tc=tc, dc=dc, par=par):
                    if dc >= 2:
                        wge(e, s_v, ev[("A2v", tc, dc - 2)])
                    else:
                        wge(e, s_v, ev[("A1v_last", tc)])
                        wge(e, s_s, ev[("A1s_last", tc)])
                    m_xsc[par].inc(e.dma_start(
                        out=xsc32[:, par * TC:(par + 1) * TC],
                        in_=x_rows[dc * 128:(dc + 1) * 128, tc * TC:(tc + 1) * TC]), 16)
                    ev[("A2l", tc, dc)] = m_xsc[par].n

                @blk.vector
                def _(e, tc=tc, dc=dc, par=par):
                    wge(e, m_xsc[par], ev[("A2l", tc, dc)])
                    wge(e, s_t, ev[("Arep", tc)])
                    if dc > 0:
                        wge(e, s_v, ev[("A2v", tc, dc - 1)])
                    s_v.inc(e.tensor_tensor(xn0[:, :], xsc32[:, par * TC:(par + 1) * TC],
                                            pb[1][:, :], OP.subtract), 1)
                    wge(e, s_v, s_v.n)
                    s_v.inc(e.tensor_tensor(
                        big[:, dc * 2048 + tc * TC: dc * 2048 + (tc + 1) * TC],
                        xn0[:, :], pb[2][:, :], OP.mult), 1)
                    ev[("A2v", tc, dc)] = s_v.n

                @blk.scalar
                def _(e, tc=tc, dc=dc):
                    wge(e, s_v, ev[("A2v", tc, dc)])
                    st_xn.inc(e.dma_start(
                        out=xn_sc.ap()[dc * 128:(dc + 1) * 128, tc * TC:(tc + 1) * TC],
                        in_=big[:, dc * 2048 + tc * TC: dc * 2048 + (tc + 1) * TC]), 16)
                    ev[("Axn", tc, dc)] = st_xn.n

                @blk.tensor
                def _(e, tc=tc, dc=dc):
                    wge(e, s_v, ev[("A2v", tc, dc)])
                    if dc == 0 and tc > 0:
                        wge(e, s_s, ev[("Apev", tc - 1)])
                    xn_t = big[:, dc * 2048 + tc * TC: dc * 2048 + (tc + 1) * TC]
                    e.matmul(pb[3][:, :], wx16[:, dc * 160: dc * 160 + 128], xn_t,
                             start=(dc == 0), stop=(dc == DC - 1), skip_group_check=True)
                    i2 = e.matmul(pb[4][0:32, :], wx16[:, dc * 160 + 128: (dc + 1) * 160], xn_t,
                                  start=(dc == 0), stop=(dc == DC - 1), skip_group_check=True)
                    if dc == DC - 1:
                        s_t.inc(i2, 1)
                        ev[("Apm", tc)] = s_t.n
            ev[("A2v_last", tc)] = s_v.n

            @blk.scalar
            def _(e, tc=tc):
                wge(e, s_t, ev[("Apm", tc)])
                wge(e, s_v, ev[("Astat", tc)])
                wge(e, s_s, ev[("Arstd", tc)])
                if tc > 0:
                    wge(e, st_pin, ev[("Apst", tc - 1)])
                e.activation(dtlst[:, :], pb[3][:, :], AF.Identity)
                s_s.inc(e.activation(bcst[:, :], pb[4][0:32, :], AF.Identity), 1)
                ev[("Apev", tc)] = s_s.n
                wge(e, s_s, ev[("Apev", tc)])
                st_pin.inc(e.dma_start(out=p_in.ap()[0:128, tc * TC:(tc + 1) * TC], in_=dtlst[:, :]), 16)
                st_pin.inc(e.dma_start(out=p_in.ap()[128:160, tc * TC:(tc + 1) * TC], in_=bcst[:, :]), 16)
                st_pin.inc(e.dma_start(out=p_in.ap()[160:161, tc * TC:(tc + 1) * TC], in_=mu16[:, :]), 16)
                st_pin.inc(e.dma_start(out=p_in.ap()[161:162, tc * TC:(tc + 1) * TC], in_=rstd16[:, :]), 16)
                st_pin.inc(e.dma_start(out=p_in.ap()[162:163, tc * TC:(tc + 1) * TC], in_=rstd16[:, :]), 16)
                st_pin.inc(e.dma_start(out=p_in.ap()[163:164, tc * TC:(tc + 1) * TC], in_=rstd16[:, :]), 16)
                ev[("Apst", tc)] = st_pin.n

        @blk.gpsimd
        def _(e):
            wge(e, st_pin, ev[("Apst", NTCQ - 1)])
            s_cc.inc(e.collective_compute(
                "AllGather", OP.bypass,
                replica_groups=[[0, 1, 2, 3], [4, 5, 6, 7]],
                ins=[p_in.ap().opt()],
                outs=[p_out.ap().opt()],
            ), 1)
            ev["ag1"] = s_cc.n

        d1banks = [0, 1, 2, 3, 5, 6]

        for tc in range(NTC):
            rnk = tc // 4
            tl = (tc % 4) * TC
            tpar = tc % 2

            @blk.sync
            def _(e, tc=tc, rnk=rnk, tl=tl, tpar=tpar):
                wge(e, s_cc, ev["ag1"])
                if tc >= 2:
                    wge(e, s_v, ev[("sdxv", tc - 2, DCS - 1)])
                m_mur[tpar].inc(e.dma_start(
                    out=murs[:, tpar * TC:(tpar + 1) * TC],
                    in_=p_out.ap()[rnk * PR + 160: rnk * PR + 161, tl:tl + TC].broadcast_to([128, TC])), 16)
                m_mur[tpar].inc(e.dma_start(
                    out=rsrs[:, tpar * TC:(tpar + 1) * TC],
                    in_=p_out.ap()[rnk * PR + 161: rnk * PR + 162, tl:tl + TC].broadcast_to([128, TC])), 16)
                ev[("mul", tc)] = m_mur[tpar].n
                if tc >= 2:
                    wge(e, s_t, ev[("sdm", tc - 2, DCS - 1)])
                m_dtl[tpar].inc(e.dma_start(
                    out=dtl2[:, tpar * TC:(tpar + 1) * TC],
                    in_=p_out.ap()[rnk * PR: rnk * PR + 128, tl:tl + TC]), 16)
                ev[("dtll", tc)] = m_dtl[tpar].n
                for dcs in range(2):
                    if tc == 0:
                        wge(e, s_v, ev[("A2v_last", NTCQ - 1)])
                    else:
                        wge(e, s_v, ev[("sdxv", tc - 1, dcs + 2)])
                    m_xsc[dcs].inc(e.dma_start(
                        out=xsc32[:, dcs * TC:(dcs + 1) * TC],
                        in_=x_scan[dcs * 128:(dcs + 1) * 128, tc * TC:(tc + 1) * TC]), 16)
                    ev[("sxl", tc, dcs)] = m_xsc[dcs].n
                for n in (0, 1):
                    if tc > 0:
                        wge(e, s_v, ev[("snv", tc - 1, N16 - 2 + n)])
                    m_bc[n].inc(e.dma_start(
                        out=brep[:, n * TC:(n + 1) * TC],
                        in_=p_out.ap()[rnk * PR + 128 + n: rnk * PR + 129 + n, tl:tl + TC].broadcast_to([128, TC])), 16)
                    m_bc[n].inc(e.dma_start(
                        out=crep[:, n * TC:(n + 1) * TC],
                        in_=p_out.ap()[rnk * PR + 144 + n: rnk * PR + 145 + n, tl:tl + TC].broadcast_to([128, TC])), 16)
                    ev[("sbl", tc, n)] = m_bc[n].n

            for dcs in range(DCS):
                @blk.tensor
                def _(e, tc=tc, dcs=dcs, tpar=tpar):
                    bank = 4 if dcs % 2 == 0 else 7
                    wge(e, m_dtl[tpar], ev[("dtll", tc)])
                    wge(e, s_s, evget(("sdev", tc, dcs - 2), ("sdev", tc - 1, dcs + 2)))
                    if (tc, dcs) == (0, 0):
                        wge(e, qg_sb, ev["sbcast"])
                        wge(e, s_s, ev[("Apev", NTCQ - 1)])
                    i = e.matmul(pb[bank][:, :], dtpw16[:, dcs * 128:(dcs + 1) * 128],
                                 dtl2[:, tpar * TC:(tpar + 1) * TC], start=True, stop=True)
                    s_t.inc(i, 1)
                    ev[("sdm", tc, dcs)] = s_t.n

                @blk.scalar
                def _(e, tc=tc, dcs=dcs):
                    bank = 4 if dcs % 2 == 0 else 7
                    wge(e, s_t, ev[("sdm", tc, dcs)])
                    wge(e, s_s, evget(("sdev", tc, dcs - 1), ("sdev", tc - 1, DCS - 1)))
                    if tc > 0:
                        wge(e, s_v, ev[("sdxv", tc - 1, dcs)])
                        wge(e, s_s, ev[("sexp", tc - 1, N16 - 1, dcs)])
                    s_s.inc(e.activation(sexp32[:, :], pb[bank][:, :], AF.Exp, bias=dtb[:, dcs:dcs + 1]), 1)
                    wge(e, s_s, s_s.n)
                    s_s.inc(e.activation(delta16[:, dcs * TC:(dcs + 1) * TC], sexp32[:, :],
                                         AF.Ln, bias=1.0), 1)
                    ev[("sdev", tc, dcs)] = s_s.n

            for dcs in range(DCS):
                @blk.vector
                def _(e, tc=tc, dcs=dcs, tpar=tpar):
                    pxr = dcs % 2
                    wge(e, m_xsc[pxr], ev[("sxl", tc, dcs)])
                    wge(e, m_mur[tpar], ev[("mul", tc)])
                    wge(e, s_s, ev[("sdev", tc, dcs)])
                    if dcs > 0:
                        wge(e, s_v, ev[("sdxv", tc, dcs - 1)])
                    s_v.inc(e.tensor_tensor(xn0[:, :], xsc32[:, pxr * TC:(pxr + 1) * TC],
                                            murs[:, tpar * TC:(tpar + 1) * TC], OP.subtract), 1)
                    wge(e, s_v, s_v.n)
                    s_v.inc(e.tensor_tensor(xns16[:, :], xn0[:, :],
                                            rsrs[:, tpar * TC:(tpar + 1) * TC], OP.mult), 1)
                    wge(e, s_v, s_v.n)
                    s_v.inc(e.tensor_tensor(dx16[:, dcs * TC:(dcs + 1) * TC],
                                            delta16[:, dcs * TC:(dcs + 1) * TC],
                                            xns16[:, :], OP.mult), 1)
                    ev[("sdxv", tc, dcs)] = s_v.n

                if dcs < 2:
                    @blk.sync
                    def _(e, tc=tc, dcs=dcs):
                        m = dcs + 2
                        wge(e, s_v, ev[("sdxv", tc, dcs)])
                        m_xsc[m % 2].inc(e.dma_start(
                            out=xsc32[:, (m % 2) * TC:(m % 2 + 1) * TC],
                            in_=x_scan[m * 128:(m + 1) * 128, tc * TC:(tc + 1) * TC]), 16)
                        ev[("sxl", tc, m)] = m_xsc[m % 2].n

            for n in (0, 1):
                for dcs in range(DCS):
                    @blk.scalar
                    def _(e, tc=tc, n=n, dcs=dcs):
                        slot = (n % 2) * 4 + dcs
                        wge(e, s_v, ev["c_aneg"])
                        wge(e, s_s, ev[("sdev", tc, dcs)])
                        if tc > 0:
                            wge(e, s_v, ev[("sscan", tc - 1, N16 - 2 + n, dcs)])
                        s_s.inc(e.activation(a32[:, slot * TC:(slot + 1) * TC],
                                             delta16[:, dcs * TC:(dcs + 1) * TC],
                                             AF.Exp, scale=aneg[:, dcs * 16 + n:dcs * 16 + n + 1]), 1)
                        ev[("sexp", tc, n, dcs)] = s_s.n

            for n in range(N16):
                np_ = n % 2

                @blk.vector
                def _(e, tc=tc, n=n, np_=np_):
                    if tc == 0 and n == 1:
                        wge(e, s_t, ev[("Astatm", NTCQ - 1)])
                    if n < 2:
                        wge(e, m_bc[np_], ev[("sbl", tc, n)])
                        wge(e, s_s, ev[("sexp", tc, n, DCS - 1)])
                    w_u4 = max(evget(("sscan", tc, n - 1, DCS - 1),
                                     ("sscan", tc - 1, N16 - 1, DCS - 1)),
                               ev[("sdxv", tc, DCS - 1)])
                    w_hst = max(evget(("shc", tc, n - 1), ("shc", tc - 1, N16 - 1)),
                                evget(("shc", tc - 1, n)))
                    brep3 = brep[:, np_ * TC:(np_ + 1) * TC].rearrange(
                        "p (a b) -> p a b", a=1).broadcast_to([128, DCS, TC])
                    crep3 = crep[:, np_ * TC:(np_ + 1) * TC].rearrange(
                        "p (a b) -> p a b", a=1).broadcast_to([128, DCS, TC])
                    i = e.tensor_tensor(u4[:, :].rearrange("p (a b) -> p a b", a=DCS),
                                        dx16[:, :].rearrange("p (a b) -> p a b", a=DCS),
                                        brep3, OP.mult)
                    if w_u4 > 0:
                        i._wait_ge(s_v.h, w_u4)
                    s_v.inc(i, 1)
                    ev[("su4", tc, n)] = s_v.n
                    if w_hst > 0:
                        wge(e, s_g, w_hst)
                    w_sya = evget(("sya", tc, n - 1, DCS - 1),
                                  ("sya", tc - 1, N16 - 1, DCS - 1))
                    for dcs in range(DCS):
                        slot = np_ * 4 + dcs
                        init = 0.0 if tc == 0 else hstate[:, dcs * 16 + n:dcs * 16 + n + 1]
                        i = e.tensor_tensor_scan(h16[:, dcs * TC:(dcs + 1) * TC],
                                                 a32[:, slot * TC:(slot + 1) * TC],
                                                 u4[:, dcs * TC:(dcs + 1) * TC], init, OP.mult, OP.add)
                        if dcs == 0:
                            i._wait_ge(s_v.h, ev[("su4", tc, n)])
                        if dcs == 1 and n + 1 < N16:
                            i._wait_ge(m_bc[(n + 1) % 2].h, ev[("sbl", tc, n + 1)])
                        if dcs == 2 and n + 1 < N16:
                            i._wait_ge(s_s.h, ev[("sexp", tc, n + 1, DCS - 1)])
                        if dcs == DCS - 1 and w_sya > 0:
                            i._wait_ge(s_g.h, w_sya)
                        s_v.inc(i, 1)
                        ev[("sscan", tc, n, dcs)] = s_v.n
                    dst = yacc if n == 0 else tmp16
                    i = e.tensor_tensor(dst[:, :].rearrange("p (a b) -> p a b", a=DCS),
                                        h16[:, :].rearrange("p (a b) -> p a b", a=DCS),
                                        crep3, OP.mult)
                    i._wait_ge(s_v.h, ev[("sscan", tc, n, DCS - 1)])
                    s_v.inc(i, 1)
                    for dcs in range(DCS):
                        ev[("sym", tc, n, dcs)] = s_v.n
                    ev[("snv", tc, n)] = s_v.n

                @blk.gpsimd
                def _(e, tc=tc, n=n):
                    i = e.tensor_copy(hstate[:, n:64:16], h16[:, TC - 1: DCS * TC: TC])
                    i._wait_ge(s_v.h, ev[("sscan", tc, n, DCS - 1)])
                    s_g.inc(i, 1)
                    ev[("shc", tc, n)] = s_g.n
                    if n > 0:
                        for dcs in range(DCS):
                            i = e.tensor_tensor(yacc[:, dcs * TC:(dcs + 1) * TC],
                                                yacc[:, dcs * TC:(dcs + 1) * TC],
                                                tmp16[:, dcs * TC:(dcs + 1) * TC], OP.add)
                            if dcs == 0:
                                i._wait_ge(s_v.h, ev[("sym", tc, n, DCS - 1)])
                            s_g.inc(i, 1)
                            ev[("sya", tc, n, dcs)] = s_g.n

                if n + 2 < N16:
                    @blk.sync
                    def _(e, tc=tc, n=n, rnk=rnk, tl=tl):
                        m = n + 2
                        wge(e, s_v, ev[("snv", tc, n)])
                        m_bc[m % 2].inc(e.dma_start(
                            out=brep[:, (m % 2) * TC:(m % 2 + 1) * TC],
                            in_=p_out.ap()[rnk * PR + 128 + m: rnk * PR + 129 + m, tl:tl + TC].broadcast_to([128, TC])), 16)
                        m_bc[m % 2].inc(e.dma_start(
                            out=crep[:, (m % 2) * TC:(m % 2 + 1) * TC],
                            in_=p_out.ap()[rnk * PR + 144 + m: rnk * PR + 145 + m, tl:tl + TC].broadcast_to([128, TC])), 16)
                        ev[("sbl", tc, m)] = m_bc[m % 2].n

                    @blk.scalar
                    def _(e, tc=tc, n=n):
                        m = n + 2
                        for dcs in range(DCS):
                            slot = (m % 2) * 4 + dcs
                            i = e.activation(a32[:, slot * TC:(slot + 1) * TC],
                                             delta16[:, dcs * TC:(dcs + 1) * TC],
                                             AF.Exp, scale=aneg[:, dcs * 16 + m:dcs * 16 + m + 1])
                            if dcs == 0:
                                i._wait_ge(s_v.h, ev[("sscan", tc, n, DCS - 1)])
                            s_s.inc(i, 1)
                            ev[("sexp", tc, m, dcs)] = s_s.n

            @blk.vector
            def _(e, tc=tc):
                if tc >= 1:
                    wge(e, st_y16, ev[("syst", tc - 1)])
                i = e.tensor_copy(y16s[:, :], yacc[:, :])
                i._wait_ge(s_g.h, ev[("sya", tc, N16 - 1, DCS - 1)])
                s_v.inc(i, 1)
                ev[("syc", tc)] = s_v.n

            @blk.scalar
            def _(e, tc=tc):
                wge(e, s_v, ev[("syc", tc)])
                for dcs in range(DCS):
                    st_y16.inc(e.dma_start(
                        out=y_in.ap()[dcs * 128:(dcs + 1) * 128, tc * TC:(tc + 1) * TC],
                        in_=y16s[:, dcs * TC:(dcs + 1) * TC]), 16)
                ev[("syst", tc)] = st_y16.n

            for k in range(16):
                u = tc * 16 + k
                j, tcq = u // 4, u % 4
                bank = d1banks[u % 6]

                if tcq == 0:
                    @blk.sync
                    def _(e, j=j):
                        if j < 2:
                            wge(e, qg_w0, ev["w0cast"])
                        else:
                            wge(e, s_t, ev[("d1m", (j - 2) * 4 + 3)])
                        src = w0_sc.ap()[j * DC * 128:(j + 1) * DC * 128, :].rearrange("(a p) b -> p a b", p=128)
                        m_w0[j % 2].inc(e.dma_start(out=w016[:, (j % 2) * 2048:(j % 2 + 1) * 2048], in_=src), 16)
                        ev[("w0l", j)] = m_w0[j % 2].n

                @blk.tensor
                def _(e, u=u, j=j, tcq=tcq, bank=bank):
                    wge(e, m_w0[j % 2], ev[("w0l", j)])
                    wge(e, s_s, evget(("d1e", u - 6)))
                    if u < 6:
                        wge(e, s_v, ev[("A2v_last", NTCQ - 1)])
                        wge(e, s_s, ev[("Apev", NTCQ - 1)])
                    for dc in range(DC):
                        i = e.matmul(pb[bank][:, :],
                                     w016[:, (j % 2) * 2048 + dc * 128:(j % 2) * 2048 + (dc + 1) * 128],
                                     big[:, dc * 2048 + tcq * TC: dc * 2048 + (tcq + 1) * TC],
                                     start=(dc == 0), stop=(dc == DC - 1))
                    s_t.inc(i, 1)
                    ev[("d1m", u)] = s_t.n

                @blk.scalar
                def _(e, u=u, j=j, tcq=tcq, bank=bank):
                    kk = u % 16
                    par = u % 2
                    wge(e, s_t, ev[("d1m", u)])
                    if u >= 16 and u % 16 == 0:
                        wge(e, st_gst[0], ev[("gst", u - 2)])
                        wge(e, st_gst[1], ev[("gst", u - 1)])
                    s_s.inc(e.activation(d1st[:, kk * TC:(kk + 1) * TC], pb[bank][:, :],
                                         AF.Identity, bias=b0[:, j:j + 1]), 1)
                    ev[("d1e", u)] = s_s.n
                    wge(e, s_s, ev[("d1e", u)])
                    st_gst[par].inc(e.dma_start(
                        out=g_sc.ap()[j * 128:(j + 1) * 128, tcq * TC:(tcq + 1) * TC],
                        in_=d1st[:, kk * TC:(kk + 1) * TC]), 16)
                    ev[("gst", u)] = st_gst[par].n

        @blk.gpsimd
        def _(e):
            wge(e, st_y16, ev[("syst", NTC - 1)])
            s_cc.inc(e.collective_compute(
                "AllGather", OP.bypass,
                replica_groups=[[0, 1, 2, 3], [4, 5, 6, 7]],
                ins=[y_in.ap().opt()],
                outs=[y_out.ap().opt()],
            ), 1)
            ev["ag2"] = s_cc.n

        for tc in range(NTCQ):
            @blk.sync
            def _(e, tc=tc):
                wge(e, s_t, ev[("d1m", NTC * 16 - 1)])
                if tc > 0:
                    wge(e, s_t, ev[("e_m", tc - 1, NDO - 1)])
                wge(e, st_gst[0], ev[("gst", NTC * 16 - 2)])
                wge(e, st_gst[1], ev[("gst", NTC * 16 - 1)])
                wge(e, st_xn, ev[("Axn", NTCQ - 1, DC - 1)])
                for b8 in range(8):
                    src = g_sc.ap()[b8 * 8 * 128:(b8 + 1) * 8 * 128, tc * TC:(tc + 1) * TC]
                    src = src.rearrange("(a p) b -> p a b", p=128)
                    m_egl.inc(e.dma_start(out=big[:, b8 * 8 * TC:(b8 + 1) * 8 * TC], in_=src), 16)
                ev[("e_gl", tc)] = m_egl.n

            @blk.scalar
            def _(e, tc=tc):
                wge(e, m_egl, ev[("e_gl", tc)])
                for b8 in range(8):
                    s_s.inc(e.activation(big[:, b8 * 8 * TC:(b8 + 1) * 8 * TC],
                                         big[:, b8 * 8 * TC:(b8 + 1) * 8 * TC], AF.Gelu), 1)
                ev[("e_gelu", tc)] = s_s.n

            for do in range(NDO):
                for hh in range(2):
                    @blk.sync
                    def _(e, tc=tc, do=do, hh=hh):
                        key = ("e_mid" if hh == 0 else "e_m")
                        wge(e, s_t, evget((key, tc, do - 1), (key, tc - 1, NDO - 1)))
                        wge(e, qg_w1, ev["w1cast"])
                        base = do * NJ * 128
                        src = w1_sc.ap()[base + hh * 32 * 128: base + (hh + 1) * 32 * 128, :]
                        src = src.rearrange("(a p) b -> p a b", p=128)
                        m_w1b[hh].inc(e.dma_start(out=w1buf[:, hh * 4096:(hh + 1) * 4096], in_=src), 16)
                        ev[("e_wl", tc, do, hh)] = m_w1b[hh].n

                @blk.tensor
                def _(e, tc=tc, do=do):
                    bank = 2 if do % 2 == 0 else 3
                    wge(e, m_egl, ev[("e_gl", tc)])
                    wge(e, s_s, ev[("e_gelu", tc)])
                    wge(e, s_s, evget(("e_ev", tc, do - 2), ("e_ev", tc - 1, NDO - 2 + (do % 2))))
                    wge(e, m_w1b[0], ev[("e_wl", tc, do, 0)])
                    for j in range(NJ):
                        hh = j // 32
                        if j == 32:
                            wge(e, m_w1b[1], ev[("e_wl", tc, do, 1)])
                        i = e.matmul(pb[bank][:, :],
                                     w1buf[:, hh * 4096 + (j - hh * 32) * 128: hh * 4096 + (j - hh * 32 + 1) * 128],
                                     big[:, j * TC:(j + 1) * TC],
                                     start=(j == 0), stop=(j == NJ - 1))
                        if j == 31:
                            s_t.inc(i, 1)
                            ev[("e_mid", tc, do)] = s_t.n
                    s_t.inc(i, 1)
                    ev[("e_m", tc, do)] = s_t.n

                @blk.scalar
                def _(e, tc=tc, do=do):
                    bank = 2 if do % 2 == 0 else 3
                    par = do % 2
                    wge(e, s_t, ev[("e_m", tc, do)])
                    wge(e, st_f[par], evget(("e_fs", tc, do - 2), ("e_fs", tc - 1, NDO - 2 + (do % 2))))
                    s_s.inc(e.activation(fstage[:, par * TC:(par + 1) * TC], pb[bank][:, :],
                                         AF.Identity, bias=b1[:, do:do + 1]), 1)
                    ev[("e_ev", tc, do)] = s_s.n
                    wge(e, s_s, ev[("e_ev", tc, do)])
                    st_f[par].inc(e.dma_start(
                        out=f_sc.ap()[do * 128:(do + 1) * 128, tc * TC:(tc + 1) * TC],
                        in_=fstage[:, par * TC:(par + 1) * TC]), 16)
                    ev[("e_fs", tc, do)] = st_f[par].n

        @blk.sync
        def _(e):
            wge(e, s_cc, ev["ag2"])
            wge(e, s_t, ev[("e_m", NTCQ - 1, NDO - 1)])
            pid = e.partition_id()
            qoff = (pid % 4) * TQ
            for dc in range(DC):
                m_cyl.inc(e.dma_start(
                    out=big[:, dc * 2048:(dc + 1) * 2048],
                    in_=y_out.ap()[dc * 128:(dc + 1) * 128, bass.ds(qoff, TQ)]), 16)
            ev["c_yl"] = m_cyl.n

        def prevC(name, do, tc):
            if tc >= 2:
                return ev.get((name, do, tc - 2), 0)
            if do >= 1:
                return ev.get((name, do - 1, tc + 2), 0)
            return 0

        for do in range(NDO):
            @blk.sync
            def _(e, do=do):
                if do < 2:
                    wge(e, qg_wo, ev["wocast"])
                    wge(e, s_t, ev[("d1m", NTC * 16 - 1)])
                else:
                    wge(e, s_t, ev[("c_m", do - 2, NTCQ - 1)])
                src = wout_sc.ap()[do * DC * 128:(do + 1) * DC * 128, :].rearrange("(a p) b -> p a b", p=128)
                m_w0[do % 2].inc(e.dma_start(out=w016[:, (do % 2) * 2048:(do % 2 + 1) * 2048], in_=src), 16)
                ev[("c_wl", do)] = m_w0[do % 2].n

            for tc in range(NTCQ):
                @blk.tensor
                def _(e, do=do, tc=tc):
                    wge(e, m_w0[do % 2], ev[("c_wl", do)])
                    wge(e, m_cyl, ev["c_yl"])
                    if do == 0:
                        wge(e, s_s, ev[("e_ev", NTCQ - 1, NDO - 1)])
                    else:
                        wge(e, s_s, ev[("c_ev", do - 1, tc)])
                    for dc in range(DC):
                        i = e.matmul(pb[tc][:, :],
                                     w016[:, (do % 2) * 2048 + dc * 128:(do % 2) * 2048 + (dc + 1) * 128],
                                     big[:, dc * 2048 + tc * TC: dc * 2048 + (tc + 1) * TC],
                                     start=(dc == 0), stop=(dc == DC - 1))
                    s_t.inc(i, 1)
                    ev[("c_m", do, tc)] = s_t.n

                @blk.scalar
                def _(e, do=do, tc=tc):
                    par = tc % 2
                    wge(e, s_t, ev[("c_m", do, tc)])
                    wge(e, s_v, prevC("c_v", do, tc))
                    s_s.inc(e.activation(a32[:, 1024 + par * TC: 1024 + (par + 1) * TC], pb[tc][:, :],
                                         AF.Relu, bias=wob[:, do:do + 1]), 1)
                    ev[("c_ev", do, tc)] = s_s.n

                @blk.sync
                def _(e, do=do, tc=tc):
                    par = tc % 2
                    wge(e, s_v, prevC("c_v", do, tc))
                    wge(e, st_f[0], ev[("e_fs", NTCQ - 1, NDO - 2)])
                    wge(e, st_f[1], ev[("e_fs", NTCQ - 1, NDO - 1)])
                    m_cld[par].inc(e.dma_start(out=a32[:, par * TC:(par + 1) * TC],
                                               in_=x_rows[do * 128:(do + 1) * 128, tc * TC:(tc + 1) * TC]), 16)
                    m_cld[par].inc(e.dma_start(out=a32[:, 3584 + 256 * par: 3584 + 256 * (par + 1)].bitcast(F16),
                                               in_=xn_sc.ap()[do * 128:(do + 1) * 128, tc * TC:(tc + 1) * TC]), 16)
                    m_cld[par].inc(e.dma_start(out=a32[:, 3072 + 256 * par: 3072 + 256 * (par + 1)].bitcast(F16),
                                               in_=f_sc.ap()[do * 128:(do + 1) * 128, tc * TC:(tc + 1) * TC]), 16)
                    ev[("c_ld", do, tc)] = m_cld[par].n

                @blk.vector
                def _(e, do=do, tc=tc):
                    par = tc % 2
                    wge(e, s_s, ev[("c_ev", do, tc)])
                    wge(e, m_cld[par], ev[("c_ld", do, tc)])
                    wge(e, st_out[par], prevC("c_st", do, tc))
                    if (do, tc) == (0, 0):
                        wge(e, s_v, ev[("snv", NTC - 1, N16 - 1)])
                    s_v.inc(e.scalar_tensor_tensor(a32[:, 2048 + par * TC: 2048 + (par + 1) * TC],
                                           a32[:, 3584 + 256 * par: 3584 + 256 * (par + 1)].bitcast(F16),
                                           dvec[:, do:do + 1],
                                           a32[:, 1024 + par * TC: 1024 + (par + 1) * TC],
                                           OP.mult, OP.add), 1)
                    wge(e, s_v, s_v.n)
                    s_v.inc(e.tensor_tensor(a32[:, 2048 + par * TC: 2048 + (par + 1) * TC],
                                    a32[:, 2048 + par * TC: 2048 + (par + 1) * TC],
                                    a32[:, par * TC:(par + 1) * TC], OP.add), 1)
                    wge(e, s_v, s_v.n)
                    s_v.inc(e.tensor_tensor(a32[:, 2048 + par * TC: 2048 + (par + 1) * TC],
                                            a32[:, 2048 + par * TC: 2048 + (par + 1) * TC],
                                            a32[:, 3072 + 256 * par: 3072 + 256 * (par + 1)].bitcast(F16), OP.add), 1)
                    ev[("c_v", do, tc)] = s_v.n

                @blk.scalar
                def _(e, do=do, tc=tc):
                    par = tc % 2
                    wge(e, s_v, ev[("c_v", do, tc)])
                    st_out[par].inc(e.dma_start(
                        out=out_d[do * 128:(do + 1) * 128, tc * TC:(tc + 1) * TC],
                        in_=a32[:, 2048 + par * TC: 2048 + (par + 1) * TC]), 16)
                    ev[("c_st", do, tc)] = st_out[par].n

        if DEBUG:
            @blk.sync
            def _(e):
                wge(e, st_out[0], st_out[0].n)
                wge(e, st_out[1], st_out[1].n)
                for dst, src in [(dbg_yin, y_in.ap()), (dbg_yout, y_out.ap()),
                                 (dbg_f, f_sc.ap()), (dbg_pout, p_out.ap()),
                                 (dbg_xn, xn_sc.ap())]:
                    st_dbg.inc(e.dma_start(out=dst, in_=src), 16)
                wge(e, st_dbg, st_dbg.n)

        @blk.sync
        def _(e):
            e.wait_ge(st_out[0].h, st_out[0].n)
            e.wait_ge(st_out[1].h, st_out[1].n)
            e.wait_ge(st_y16.h, st_y16.n)
            e.wait_ge(st_pin.h, st_pin.n)
            e.wait_ge(st_xn.h, st_xn.n)
            e.wait_ge(st_f[0].h, st_f[0].n)
            e.wait_ge(st_f[1].h, st_f[1].n)
            e.wait_ge(st_gst[0].h, st_gst[0].n)
            e.wait_ge(st_gst[1].h, st_gst[1].n)
            e.wait_ge(qg_w0.h, qg_w0.n)
            e.wait_ge(qg_w1.h, qg_w1.n)
            e.wait_ge(qg_wo.h, qg_wo.n)
            e.wait_ge(qg_sb.h, qg_sb.n)

    return nc


_NC_CACHE = {}


def make_maps_inner(x, xpw, dtpw, dt_bias, A_log, Dv, Wout, outb, W0, bias0, W1, bias1):
    C = np.ascontiguousarray
    wx_t = C(xpw.T)
    w0_lay = C(W0.T.reshape(16, 128, 64, 128).transpose(2, 0, 1, 3).reshape(64 * 16 * 128, 128))
    w1_lay = C(W1.T.reshape(64, 128, 16, 128).transpose(2, 0, 1, 3).reshape(16 * 64 * 128, 128))
    wout_lay = C(Wout.T.reshape(16, 128, 16, 128).transpose(2, 0, 1, 3).reshape(16 * 16 * 128, 128))
    b0_l = C(bias0.reshape(64, 128).T)
    b1_l = C(bias1.reshape(16, 128).T)
    d_l = C(Dv.reshape(16, 128).T)
    woutb_l = C(outb.reshape(16, 128).T)
    in_maps = []
    for c in range(8):
        bi, qi = c // 4, c % 4
        sl = slice(qi * DSL, (qi + 1) * DSL)
        in_maps.append({
            "x_rows": C(x[bi, qi * TQ:(qi + 1) * TQ, :].T),
            "x_scan": C(x[bi, :, sl].T),
            "wx_t": wx_t,
            "dtpw_t": C(dtpw[sl, :].T),
            "dt_b": C(dt_bias[sl].reshape(4, 128).T),
            "a_log": C(A_log[sl].reshape(4, 128, 16).transpose(1, 0, 2).reshape(128, 64)),
            "d_vec": d_l,
            "wout_lay": wout_lay,
            "woutb": woutb_l,
            "w0_lay": w0_lay,
            "b0_in": b0_l,
            "w1_lay": w1_lay,
            "b1_in": b1_l,
        })
    return in_maps


def run_full(in_maps):
    if "nc" not in _NC_CACHE:
        _NC_CACHE["nc"] = build_nc()
    nc = _NC_CACHE["nc"]
    return run_bass_kernel_spmd(nc, in_maps, core_ids=list(range(8)))


def kernel(L0_attn_hidden_states, L0_attn_x_proj_weight, L0_attn_dt_proj_weight,
           L0_attn_dt_proj_bias, L0_attn_A_log, L0_attn_D, L0_attn_out_proj_weight,
           L0_attn_out_proj_bias, L0_ffn_W0, L0_ffn_bias0, L0_ffn_W1, L0_ffn_bias1):
    x = np.asarray(L0_attn_hidden_states, dtype=np.float32)
    in_maps = make_maps_inner(
        x,
        np.asarray(L0_attn_x_proj_weight, dtype=np.float32),
        np.asarray(L0_attn_dt_proj_weight, dtype=np.float32),
        np.asarray(L0_attn_dt_proj_bias, dtype=np.float32),
        np.asarray(L0_attn_A_log, dtype=np.float32),
        np.asarray(L0_attn_D, dtype=np.float32),
        np.asarray(L0_attn_out_proj_weight, dtype=np.float32),
        np.asarray(L0_attn_out_proj_bias, dtype=np.float32),
        np.asarray(L0_ffn_W0, dtype=np.float32),
        np.asarray(L0_ffn_bias0, dtype=np.float32),
        np.asarray(L0_ffn_W1, dtype=np.float32),
        np.asarray(L0_ffn_bias1, dtype=np.float32),
    )
    res = run_full(in_maps)
    out = np.empty((2, T, DM), dtype=np.float32)
    for c in range(8):
        bi, qi = c // 4, c % 4
        out[bi, qi * TQ:(qi + 1) * TQ, :] = np.asarray(res.results[c]["out"]).T
    return out
